# version 01
# baseline (speedup 1.0000x reference)
import numpy as np
import ml_dtypes
import concourse.bass as bass
import concourse.mybir as mybir
from contextlib import ExitStack
from concourse.bass_utils import run_bass_kernel_spmd

F32 = mybir.dt.float32
BF16 = mybir.dt.bfloat16
AF = mybir.ActivationFunctionType
ALU = mybir.AluOpType
AX = mybir.AxisListType

D = 1024
E = 8
SEQ = 8192
BATCH = 4
NCORE = 8
QROWS = 4096
NJ = 8
NCHUNK = 8
DC = 8
NEG = -1.0e30
EPS = 1e-5
SCL = 1.0 / 32.0
DEBUG_C = False


class SemBook:
    def __init__(self, nc, stack):
        self.nc, self.stack = nc, stack
        self.h, self.c = {}, {}

    def sem(self, name):
        if name not in self.h:
            self.h[name] = self.stack.enter_context(self.nc.semaphore(name))
            self.c[name] = 0
        return self.h[name]

    def inc(self, instr, name, n=1):
        instr.then_inc(self.sem(name), n)
        self.c[name] += n
        return self.c[name]

    def wait(self, eng, name, val):
        eng.wait_ge(self.sem(name), val)


class Walk:

    def __init__(self):
        self.cnt = {}

    def pub(self, sem, n=1):
        v = self.cnt.get(sem, 0) + n
        self.cnt[sem] = v
        return (sem, v)


class RWait:

    def __init__(self, buf, k):
        self.buf, self.k = buf, k

    def resolve(self):
        b, k = self.buf, self.k
        if k < b.nbuf:
            return []
        u = b.uses[k - b.nbuf]
        out = []
        if u["prod"] is not None:
            out.append(u["prod"])
        out.extend(u["cons"])
        return out


class RBuf:

    def __init__(self, nbuf=1):
        self.nbuf = nbuf
        self.uses = []

    def acquire(self):
        k = len(self.uses)
        self.uses.append({"prod": None, "cons": []})
        return k, [RWait(self, k)]

    def set_prod(self, k, tok):
        self.uses[k]["prod"] = tok

    def add_cons(self, k, tok):
        self.uses[k]["cons"].append(tok)


def mkemit(S, own):
    def emitw(eng, waits):
        best = {}
        flat = []
        for t in waits:
            if t is None:
                continue
            if isinstance(t, RWait):
                flat.extend(t.resolve())
            else:
                flat.append(t)
        for sem, v in flat:
            if sem == own:
                continue
            if best.get(sem, 0) < v:
                best[sem] = v
        for sem, v in best.items():
            S.wait(eng, sem, v)

    def pub(instr, tok):
        got = S.inc(instr, tok[0], 1)
        assert got == tok[1], (tok, got)

    def pub16(instr, tok):
        got = S.inc(instr, tok[0], 16)
        assert got == tok[1], (tok, got)
    return emitw, pub, pub16


def _rope_tables_np(m, d):
    i = np.arange(d // 2, dtype=np.float32)
    phi = np.power(np.float32(10000.0),
                   np.float32(-2.0) * (i - np.float32(1.0)) / np.float32(d))
    pos = np.arange(m, dtype=np.float32)
    ang = pos[:, None] * phi[None, :]
    return np.cos(ang, dtype=np.float32), np.sin(ang, dtype=np.float32)


def host_layouts(L0_attn_x, L0_attn_wq, L0_attn_wk, L0_attn_wv,
                 L0_ffn_Wg, L0_ffn_W1, L0_ffn_W2, L0_ffn_W3):
    x = np.ascontiguousarray(np.asarray(L0_attn_x, dtype=np.float32))
    wq = np.asarray(L0_attn_wq, dtype=np.float32)
    wk = np.asarray(L0_attn_wk, dtype=np.float32)
    wv = np.asarray(L0_attn_wv, dtype=np.float32)
    Wg = np.asarray(L0_ffn_Wg, dtype=np.float32)
    W1 = np.asarray(L0_ffn_W1, dtype=np.float32)
    W2 = np.asarray(L0_ffn_W2, dtype=np.float32)
    W3 = np.asarray(L0_ffn_W3, dtype=np.float32)

    perm = np.concatenate([np.arange(0, D, 2), np.arange(1, D, 2)])

    def wT_lay(w, permute):
        wt = (w[perm] if permute else w).T
        return np.ascontiguousarray(
            wt.reshape(DC, 128, D).transpose(1, 0, 2).reshape(128, DC * D))

    wqT, wkT, wvT = wT_lay(wq, True), wT_lay(wk, True), wT_lay(wv, False)

    W1h = np.ascontiguousarray(
        W1.reshape(E, DC, 128, 2, 1024).transpose(0, 3, 2, 1, 4).reshape(E, 2, 128, 8192))
    W2h = np.ascontiguousarray(
        W2.reshape(E, DC, 128, 2, 1024).transpose(0, 3, 2, 1, 4).reshape(E, 2, 128, 8192))
    W3h = np.ascontiguousarray(W3.reshape(E, 16, 128, D))
    Wgh = np.ascontiguousarray(
        Wg.reshape(DC, 128, E).transpose(1, 0, 2).reshape(128, DC * E))

    def rows_lay(a):
        n = a.shape[0] // 512
        return np.ascontiguousarray(
            a.reshape(n, 4, 128, D).transpose(0, 2, 1, 3).reshape(n, 128, 4 * D))

    cosT, sinT = _rope_tables_np(SEQ, D)
    cosK = np.ascontiguousarray(cosT.T.reshape(4, 128, SEQ))
    sinK = np.ascontiguousarray(sinT.T.reshape(4, 128, SEQ))
    ident = np.eye(128, dtype=ml_dtypes.bfloat16)

    per_core, row_idx = [], {}
    for c in range(NCORE):
        b, h = c // 2, c % 2
        rows = np.concatenate(
            [np.arange(J * 1024 + h * 512, J * 1024 + h * 512 + 512) for J in range(NJ)])
        row_idx[c] = rows
        xqc = np.ascontiguousarray(x[b][rows])
        cosQ = np.ascontiguousarray(cosK[:, :, rows] * np.float32(SCL))
        sinQ = np.ascontiguousarray(sinK[:, :, rows] * np.float32(SCL))
        qpos = (h * 512 + np.arange(512))[:, None]
        kpos = np.arange(1024)[None, :]
        mask = np.where(kpos <= qpos, np.float32(0.0), np.float32(NEG))
        mask = np.ascontiguousarray(
            mask.reshape(4, 128, 1024).transpose(1, 0, 2).reshape(128, 4096))
        per_core.append(dict(
            xb=rows_lay(x[b]), xq=rows_lay(xqc), cosQ=cosQ, sinQ=sinQ, mask=mask,
            wqT=wqT, wkT=wkT, wvT=wvT, cosK=cosK, sinK=sinK,
            W1h=W1h, W2h=W2h, W3h=W3h, Wgh=Wgh, ident=ident))
    return per_core, row_idx


def build_program(phases=("A", "B", "C"), taps=()):
    nc = bass.Bass("TRN2", target_bir_lowering=False)
    final_waits = []

    def scratch(name, shape, dtype, produced_by):
        if produced_by not in phases:
            kind = "ExternalInput"
        elif name in taps:
            kind = "ExternalOutput"
        else:
            kind = "Internal"
        return nc.dram_tensor(name, shape, dtype, kind=kind).ap()

    xb = nc.dram_tensor("xb", [16, 128, 4096], F32, kind="ExternalInput").ap()
    xq = nc.dram_tensor("xq", [8, 128, 4096], F32, kind="ExternalInput").ap()
    wqT = nc.dram_tensor("wqT", [128, 8192], F32, kind="ExternalInput").ap()
    wkT = nc.dram_tensor("wkT", [128, 8192], F32, kind="ExternalInput").ap()
    wvT = nc.dram_tensor("wvT", [128, 8192], F32, kind="ExternalInput").ap()
    cosK = nc.dram_tensor("cosK", [4, 128, SEQ], F32, kind="ExternalInput").ap()
    sinK = nc.dram_tensor("sinK", [4, 128, SEQ], F32, kind="ExternalInput").ap()
    cosQ = nc.dram_tensor("cosQ", [4, 128, QROWS], F32, kind="ExternalInput").ap()
    sinQ = nc.dram_tensor("sinQ", [4, 128, QROWS], F32, kind="ExternalInput").ap()
    maskd = nc.dram_tensor("mask", [128, 4096], F32, kind="ExternalInput").ap()
    W1f = nc.dram_tensor("W1h", [E, 2, 128, 8192], F32, kind="ExternalInput").ap()
    W2f = nc.dram_tensor("W2h", [E, 2, 128, 8192], F32, kind="ExternalInput").ap()
    W3f = nc.dram_tensor("W3h", [E, 16, 128, D], F32, kind="ExternalInput").ap()
    Wgf = nc.dram_tensor("Wgh", [128, DC * E], F32, kind="ExternalInput").ap()
    ident_d = nc.dram_tensor("ident", [128, 128], BF16, kind="ExternalInput").ap()

    KTs = scratch("KTs", [DC, 128, SEQ], BF16, "A")
    Vs = scratch("Vs", [SEQ, D], BF16, "A")
    QTs = scratch("QTs", [DC, 128, QROWS], BF16, "A")
    XNTs = scratch("XNTs", [DC, 128, QROWS], BF16, "A")
    ATTs = scratch("ATTs", [QROWS, D], F32, "B")
    W1b = nc.dram_tensor("W1b", [E, 2, 128, 8192], BF16, kind="Internal").ap()
    W2b = nc.dram_tensor("W2b", [E, 2, 128, 8192], BF16, kind="Internal").ap()
    W3b = nc.dram_tensor("W3b", [E, 16, 128, D], BF16, kind="Internal").ap()
    OUT = nc.dram_tensor("out", [NCHUNK, 128, 4096], F32, kind="ExternalOutput").ap()
    dbgH = dbgA = dbgW = None
    dbgG = None
    if DEBUG_C:
        dbgH = nc.dram_tensor("dbgH", [NCHUNK * E, 128, 8192], BF16,
                              kind="ExternalOutput").ap()
        dbgA = nc.dram_tensor("dbgA", [NCHUNK * E, 128, 4096], F32,
                              kind="ExternalOutput").ap()
        dbgW = nc.dram_tensor("dbgW", [128, 256], F32,
                              kind="ExternalOutput").ap()
        dbgG = {nm: nc.dram_tensor(nm, [128, wdt], F32, kind="ExternalOutput").ap()
                for nm, wdt in (("dbgm1", 4), ("dbgm2", 4), ("dbgd1", 4),
                                ("dbgd2", 4), ("dbge", 8), ("dbgp", 8),
                                ("dbgq1", 32), ("dbgmk", 32))}

    with ExitStack() as top:
        S = SemBook(nc, top)
        W = Walk()
        for s in ("pe", "dv", "ac", "po"):
            S.sem(s)
        block = top.enter_context(nc.Block())
        pf = [top.enter_context(nc.psum_tensor(f"pf{i}", [128, 512], F32))
              for i in range(6)]
        pbt = [top.enter_context(nc.psum_tensor(f"pbt{i}", [128, 1024], BF16))
               for i in range(2)]
        RB = [RBuf(1) for _ in range(8)]
        ident_s = top.enter_context(nc.sbuf_tensor("ident_s", [128, 128], BF16))

        t_ident = W.pub("d_id", 16)
        t_wc = None
        if "C" in phases:
            W.pub("d_wc", 16)
            W.pub("d_wc", 16)
            t_wc = W.pub("d_wc", 16)

        @block.gpsimd
        def _(g):
            S.inc(g.dma_start(ident_s[:], ident_d), "d_id", 16)
            if "C" in phases:
                S.inc(g.dma_start(W1b, W1f), "d_wc", 16)
                S.inc(g.dma_start(W2b, W2f), "d_wc", 16)
                S.inc(g.dma_start(W3b, W3f), "d_wc", 16)

        if "A" in phases:
            with ExitStack() as ctx:
                _phase_A(nc, block, S, W, ctx, RB, pf, pbt, ident_s, t_ident,
                         xb, xq, wqT, wkT, wvT, cosK, sinK, cosQ, sinQ,
                         KTs, Vs, QTs, XNTs, final_waits)
        if "B" in phases:
            with ExitStack() as ctx:
                _phase_B(nc, block, S, W, ctx, RB, pf, pbt, ident_s, t_ident,
                         KTs, Vs, QTs, maskd, ATTs, final_waits,
                         a_done=("A" in phases))
        if "C" in phases:
            with ExitStack() as ctx:
                _phase_C(nc, block, S, W, ctx, RB, pf, Wgf, t_wc,
                         W1b, W2b, W3b, XNTs, ATTs, xq, OUT, final_waits,
                         prev=("B" in phases), dbgH=dbgH, dbgA=dbgA, dbgW=dbgW, dbgG=dbgG)

        @block.sync
        def _(sync):
            for nm, val in final_waits:
                S.wait(sync, nm, val)
    return nc


def _phase_A(nc, block, S, W, ctx, RB, pf, pbt, ident_s, t_ident,
             xb, xq, wqT, wkT, wvT, cosK, sinK, cosQ, sinQ,
             KTs, Vs, QTs, XNTs, final_waits):
    sbt = lambda n, shp, dt: ctx.enter_context(nc.sbuf_tensor(n, shp, dt))
    xbuf = [sbt(f"a_x{i}", [128, 4096], F32) for i in range(2)]
    cbuf = [sbt(f"a_c{i}", [128, 2048], F32) for i in range(2)]
    sbin = [sbt(f"a_s{i}", [128, 2048], F32) for i in range(2)]
    xnb = [sbt(f"a_xnb{i}", [128, 1024], BF16) for i in range(2)]
    xnt = [sbt(f"a_xnt{i}", [128, 4096], BF16) for i in range(2)]
    ko = [sbt(f"a_ko{i}", [128, 4096], BF16) for i in range(2)]
    vo = [sbt(f"a_vo{i}", [128, 1024], BF16) for i in range(2)]
    rt = [sbt(f"a_rt{i}", [128, 512], F32) for i in range(4)]
    bnst = [sbt(f"a_bn{i}", [128, 12], F32) for i in range(2)]
    mv = [sbt(f"a_mv{i}", [128, 2], F32) for i in range(2)]
    rstd = [sbt(f"a_rs{i}", [128, 1], F32) for i in range(2)]
    wqs = sbt("a_wq", [128, 8192], BF16)
    wks = sbt("a_wk", [128, 8192], BF16)
    wvs = sbt("a_wv", [128, 8192], BF16)
    eps_t = sbt("a_eps", [128, 1], F32)

    t_eps = W.pub("dv")

    @block.vector
    def _(vec):
        got = S.inc(vec.memset(eps_t[:], EPS), "dv", 1)
        assert got == t_eps[1]

    W.pub("d_aw", 16)
    W.pub("d_aw", 16)
    t_aw = W.pub("d_aw", 16)

    @block.gpsimd
    def _(g):
        for wsb, wdr in ((wks, wkT), (wvs, wvT), (wqs, wqT)):
            S.inc(g.dma_start(wsb[:], wdr), "d_aw", 16)

    def one_pass(tag, nst, src_x, cos_d, sin_d, wproj, do_v, do_q,
                 kt_dst, v_dst, xnt_dst):
        RX = RBuf(2)
        RBN = RBuf(2)
        RMV = RBuf(2)
        RRS = RBuf(2)
        RXNB = RBuf(2)
        RXNT = RBuf(2)
        RKO = RBuf(2)
        RVO = RBuf(2)
        dx, dko, dvo, dxnt = (f"d_{tag}x", f"d_{tag}ko", f"d_{tag}vo",
                              f"d_{tag}xnt")

        EV = []
        for st in range(nst):
            par = st % 2
            ev = dict(st=st, par=par)
            k, w = RX.acquire()
            ev["x"] = (k, w)
            W.pub(dx, 16)
            W.pub(dx, 16)
            RX.set_prod(k, W.pub(dx, 16))
            ev["tiles"] = []
            for t in range(4):
                tp = t % 2
                tl = dict(t=t, tp=tp)
                kb, wb = RBN.acquire()
                tl["bn"] = (kb, wb)
                tl["t_bn0"] = W.pub("dv")
                tl["t_bn"] = W.pub("dv")
                km, wm = RMV.acquire()
                tl["mvw"] = wm
                t_ag = W.pub("dv")
                RBN.add_cons(kb, t_ag)
                RMV.set_prod(km, t_ag)
                tl["t_ag"] = t_ag
                kr, wr = RRS.acquire()
                tl["rsw"] = wr
                t_ln = W.pub("ac")
                RMV.add_cons(km, t_ln)
                tl["t_ln"] = t_ln
                t_rs = W.pub("ac")
                RRS.set_prod(kr, t_rs)
                tl["t_rs"] = t_rs
                kn, wn = RXNB.acquire()
                tl["xnbw"] = wn
                t_xn = W.pub("dv")
                RXNB.set_prod(kn, t_xn)
                RMV.add_cons(km, t_xn)
                RRS.add_cons(kr, t_xn)
                tl["t_xn"] = t_xn
                kt, wt = RB[6 + tp].acquire()
                tl["tbw"] = wt
                t_tp = W.pub("pe")
                RB[6 + tp].set_prod(kt, t_tp)
                RXNB.add_cons(kn, t_tp)
                tl["t_tp"] = t_tp
                if t == 0:
                    kxt, wxt = RXNT.acquire()
                    ev["xnt"] = kxt
                    ev["xntw"] = wxt
                t_ev = W.pub("ac")
                RB[6 + tp].add_cons(kt, t_ev)
                tl["t_ev"] = t_ev
                ev["tiles"].append(tl)
            RXNT.set_prod(ev["xnt"], ev["tiles"][3]["t_ev"])
            ev["prs"] = []
            for pr in range(4):
                ba, bb = (0, 1) if pr % 2 == 0 else (2, 3)
                q = dict(pr=pr, ba=ba, bb=bb)
                ka, wa = RB[ba].acquire()
                kb2, wb2 = RB[bb].acquire()
                q["baw"], q["bbw"] = wa, wb2
                t_pa = W.pub("pe")
                t_pb = W.pub("pe")
                RB[ba].set_prod(ka, t_pa)
                RB[bb].set_prod(kb2, t_pb)
                q["t_pa"], q["t_pb"] = t_pa, t_pb
                q["ka"], q["kb"] = ka, kb2
                if pr == 0:
                    kk, wk = RKO.acquire()
                    ev["ko"] = kk
                    ev["kow"] = wk
                t_ra = W.pub("dv")
                t_rb = W.pub("dv")
                t_ro = W.pub("dv")
                RB[ba].add_cons(ka, t_ra)
                RB[bb].add_cons(kb2, t_rb)
                q["t_ra"], q["t_rb"], q["t_ro"] = t_ra, t_rb, t_ro
                if pr == 3:
                    RX.add_cons(ev["x"][0], t_ro)
                ev["prs"].append(q)
            RKO.set_prod(ev["ko"], ev["prs"][3]["t_ro"])
            for _ in range(7):
                W.pub(dko, 16)
            tsk = W.pub(dko, 16)
            RKO.add_cons(ev["ko"], tsk)
            ev["t_stko"] = tsk
            if do_q:
                tsx = W.pub(dxnt, 16)
                RXNT.add_cons(ev["xnt"], tsx)
                ev["t_stxnt"] = tsx
                RXNT.add_cons(ev["xnt"], ev["prs"][3]["t_pb"])
            if do_v:
                ev["vs"] = []
                for sbk in range(4):
                    ba, bb = (4, 5) if sbk % 2 == 0 else (0, 1)
                    v = dict(sbk=sbk, ba=ba, bb=bb)
                    ka, wa = RB[ba].acquire()
                    kb2, wb2 = RB[bb].acquire()
                    v["baw"], v["bbw"] = wa, wb2
                    t_va = W.pub("pe")
                    t_vb = W.pub("pe")
                    RB[ba].set_prod(ka, t_va)
                    RB[bb].set_prod(kb2, t_vb)
                    v["t_va"], v["t_vb"] = t_va, t_vb
                    if sbk == 3:
                        RXNT.add_cons(ev["xnt"], t_vb)
                    kv, wv = RVO.acquire()
                    v["vow"] = wv
                    t_c1 = W.pub("ac")
                    t_c2 = W.pub("ac")
                    RB[ba].add_cons(ka, t_c1)
                    RB[bb].add_cons(kb2, t_c2)
                    RVO.set_prod(kv, t_c2)
                    v["t_c1"], v["t_c2"] = t_c1, t_c2
                    tsv = W.pub(dvo, 16)
                    RVO.add_cons(kv, tsv)
                    v["t_stv"] = tsv
                    ev["vs"].append(v)
            EV.append(ev)

        ew, pub, pub16 = mkemit(S, None)

        @block.sync
        def _(sync):
            def loads(ev):
                par, st = ev["par"], ev["st"]
                ew(sync, ev["x"][1])
                S.inc(sync.dma_start(xbuf[par][:], src_x[st]), dx, 16)
                S.inc(sync.dma_start(
                    cbuf[par][:].rearrange("p (oc s) -> p oc s", s=512),
                    cos_d[:, :, st * 512:(st + 1) * 512]
                    .rearrange("oc p s -> p oc s")), dx, 16)
                S.inc(sync.dma_start(
                    sbin[par][:].rearrange("p (oc s) -> p oc s", s=512),
                    sin_d[:, :, st * 512:(st + 1) * 512]
                    .rearrange("oc p s -> p oc s")), dx, 16)
                assert S.c[dx] == ev["x"][0] * 48 + 48

            def stores(ev):
                st, par = ev["st"], ev["par"]
                ew(sync, [ev["prs"][3]["t_ro"]])
                for oc in range(8):
                    ii = sync.dma_start(kt_dst[oc, :, st * 512:(st + 1) * 512],
                                        ko[par][:, oc * 512:(oc + 1) * 512])
                    S.inc(ii, dko, 16)
                assert S.c[dko] == ev["t_stko"][1]
                if do_q:
                    ew(sync, [ev["tiles"][3]["t_ev"]])
                    ii = sync.dma_start(
                        xnt_dst[:, :, st * 512:(st + 1) * 512]
                        .rearrange("dc p s -> p dc s"),
                        xnt[par][:].rearrange("p (dc s) -> p dc s", s=512))
                    pub16(ii, ev["t_stxnt"])
                if do_v:
                    for v in ev["vs"]:
                        vp = v["sbk"] % 2
                        ew(sync, [v["t_c2"]])
                        ii = sync.dma_start(
                            v_dst[st * 512 + v["sbk"] * 128:
                                  st * 512 + (v["sbk"] + 1) * 128, :],
                            vo[vp][:])
                        pub16(ii, v["t_stv"])

            for i, ev in enumerate(EV):
                loads(ev)
                if i > 0:
                    stores(EV[i - 1])
            stores(EV[-1])

        ew, pub, _ = mkemit(S, "dv")

        @block.vector
        def _(vec):
            for ev in EV:
                par = ev["par"]
                for tl in ev["tiles"]:
                    t, tp = tl["t"], tl["tp"]
                    xt = xbuf[par][:, t * 1024:(t + 1) * 1024]
                    if t == 0:
                        ew(vec, [(dx, ev["x"][0] * 48 + 48)])
                    ew(vec, tl["bn"][1])
                    pub(vec.bn_stats(bnst[tp][:, 0:6], xt[:, 0:512]), tl["t_bn0"])
                    pub(vec.bn_stats(bnst[tp][:, 6:12], xt[:, 512:1024]), tl["t_bn"])
                    S.wait(vec, "dv", tl["t_bn"][1])
                    ew(vec, tl["mvw"])
                    pub(vec.bn_aggr(mv[tp][:], bnst[tp][:]), tl["t_ag"])
                    ew(vec, [tl["t_rs"]] + tl["xnbw"])
                    pub(vec.tensor_scalar(xnb[tp][:], xt, mv[tp][:, 0:1],
                                          rstd[tp][:], ALU.subtract, ALU.mult),
                        tl["t_xn"])
                for q in ev["prs"]:
                    pr, ba, bb = q["pr"], q["ba"], q["bb"]
                    ww = [q["t_pa"], q["t_pb"]]
                    if pr == 0:
                        ww = ww + ev["kow"]
                    ew(vec, ww)
                    ca = cbuf[par][:, pr * 512:(pr + 1) * 512]
                    sa = sbin[par][:, pr * 512:(pr + 1) * 512]
                    vec.tensor_mul(rt[0][:], pf[ba][:, 0:512], ca)
                    vec.tensor_mul(rt[1][:], pf[bb][:, 0:512], sa)
                    pub(vec.tensor_mul(rt[2][:], pf[ba][:, 0:512], sa), q["t_ra"])
                    pub(vec.tensor_mul(rt[3][:], pf[bb][:, 0:512], ca), q["t_rb"])
                    vec.tensor_add(ko[par][:, pr * 512:(pr + 1) * 512],
                                   rt[0][:], rt[1][:])
                    pub(vec.tensor_sub(ko[par][:, (pr + 4) * 512:(pr + 5) * 512],
                                       rt[3][:], rt[2][:]), q["t_ro"])

        ew, pub, _ = mkemit(S, "ac")

        @block.scalar
        def _(sc):
            first = True
            for ev in EV:
                par = ev["par"]
                for tl in ev["tiles"]:
                    t, tp = tl["t"], tl["tp"]
                    if first:
                        ew(sc, [t_eps])
                        first = False
                    ew(sc, [tl["t_ag"]] + tl["rsw"])
                    pub(sc.activation(rstd[tp][:], mv[tp][:, 1:2], AF.Ln,
                                      bias=eps_t[:], scale=1.0), tl["t_ln"])
                    S.wait(sc, "ac", tl["t_ln"][1])
                    pub(sc.activation(rstd[tp][:], rstd[tp][:], AF.Exp,
                                      bias=0.0, scale=-0.5), tl["t_rs"])
                    ww = [tl["t_tp"]]
                    if t == 0:
                        ww = ww + ev["xntw"]
                    ew(sc, ww)
                    tb = 6 + tp
                    xv = xnt[par][:].rearrange("p (dc s) -> p dc s", s=512)
                    pub(sc.copy(xv[:, :, t * 128:(t + 1) * 128],
                                pbt[tb - 6][:].rearrange("p (dc s) -> p dc s", s=128)),
                        tl["t_ev"])
                if do_v:
                    for v in ev["vs"]:
                        vp = v["sbk"] % 2
                        ew(sc, [v["t_va"], v["t_vb"]] + v["vow"])
                        pub(sc.copy(vo[vp][:, 0:512], pf[v["ba"]][:, 0:512]),
                            v["t_c1"])
                        pub(sc.copy(vo[vp][:, 512:1024], pf[v["bb"]][:, 0:512]),
                            v["t_c2"])

        ew, pub, _ = mkemit(S, "pe")

        @block.tensor
        def _(te):
            first = True
            for ev in EV:
                par = ev["par"]
                for tl in ev["tiles"]:
                    t, tp = tl["t"], tl["tp"]
                    ww = [tl["t_xn"]] + tl["tbw"]
                    if first:
                        ww = ww + [t_ident, t_aw]
                        first = False
                    ew(te, ww)
                    tb = 6 + tp
                    for dcc in range(8):
                        ii = te.transpose(pbt[tb - 6][:, dcc * 128:(dcc + 1) * 128],
                                          xnb[tp][:, dcc * 128:(dcc + 1) * 128],
                                          ident_s[:])
                    pub(ii, tl["t_tp"])
                for q in ev["prs"]:
                    pr = q["pr"]
                    ww = q["baw"] + q["bbw"]
                    if pr == 0:
                        ww = ww + [ev["tiles"][3]["t_ev"]]
                    ew(te, ww)
                    for half, bk, tok in ((0, q["ba"], q["t_pa"]),
                                          (1, q["bb"], q["t_pb"])):
                        oc = pr + 4 * half
                        for dcc in range(8):
                            ii = te.matmul(
                                pf[bk][:, 0:512],
                                wproj[:, dcc * 1024 + oc * 128: dcc * 1024 + (oc + 1) * 128],
                                xnt[par][:, dcc * 512:(dcc + 1) * 512],
                                start=(dcc == 0), stop=(dcc == 7))
                        pub(ii, tok)
                if do_v:
                    for v in ev["vs"]:
                        ew(te, v["baw"] + v["bbw"])
                        for dcc in range(8):
                            lhs = xnt[par][:, dcc * 512 + v["sbk"] * 128:
                                           dcc * 512 + (v["sbk"] + 1) * 128]
                            i1 = te.matmul(pf[v["ba"]][:, 0:512], lhs,
                                           wvs[:, dcc * 1024: dcc * 1024 + 512],
                                           start=(dcc == 0), stop=(dcc == 7))
                            i2 = te.matmul(pf[v["bb"]][:, 0:512], lhs,
                                           wvs[:, dcc * 1024 + 512: (dcc + 1) * 1024],
                                           start=(dcc == 0), stop=(dcc == 7))
                        pub(i1, v["t_va"])
                        pub(i2, v["t_vb"])

        final_waits.append((dko, S.c[dko]))
        if do_v:
            final_waits.append((dvo, S.c[dvo]))
        if do_q:
            final_waits.append((dxnt, S.c[dxnt]))
        return EV

    one_pass("A", 16, xb, cosK, sinK, wks, True, False, KTs, Vs, None)
    one_pass("q", 8, xq, cosQ, sinQ, wqs, False, True, QTs, None, XNTs)


def _phase_B(nc, block, S, W, ctx, RB, pf, pbt, ident_s, t_ident,
             KTs, Vs, QTs, maskd, ATTs, final_waits, a_done):
    sbt = lambda n, shp, dt: ctx.enter_context(nc.sbuf_tensor(n, shp, dt))
    masks = sbt("b_mask", [128, 4096], F32)
    qb = [sbt(f"b_q{i}", [128, 4096], BF16) for i in range(2)]
    kb = [sbt(f"b_k{i}", [128, 4096], BF16) for i in range(2)]
    vb = [sbt(f"b_v{i}", [128, 4096], BF16) for i in range(2)]
    pbuf = [sbt(f"b_p{i}", [128, 512], BF16) for i in range(2)]
    pts = [sbt(f"b_pt{i}", [128, 512], BF16) for i in range(2)]
    accS = sbt("b_acc", [128, 4096], F32)
    mst = [sbt(f"b_m{i}", [128, 4], F32) for i in range(2)]
    mtmp = sbt("b_mt", [128, 4], F32)
    lbuf = sbt("b_l", [128, 4], F32)
    ltmp = sbt("b_lt", [128, 4], F32)
    negm = sbt("b_nm", [128, 4], F32)
    dmb = sbt("b_dm", [128, 4], F32)
    asc = sbt("b_as", [128, 4], F32)
    lp = sbt("b_lp", [128, 4], F32)
    linv = sbt("b_li", [128, 4], F32)
    astg = [sbt(f"b_at{i}", [128, 1024], F32) for i in range(2)]

    pre = []
    if a_done:
        pre = [("d_Ako", S.c["d_Ako"]), ("d_Avo", S.c["d_Avo"]),
               ("d_qko", S.c["d_qko"]), ("d_qxnt", S.c["d_qxnt"])]

    RQ, RK, RV = RBuf(2), RBuf(2), RBuf(2)
    RPB = [RBuf(1) for _ in range(2)]
    RPT = [RBuf(1) for _ in range(2)]
    RAT = RBuf(2)
    lu_last = [None]
    px_last = [None]

    t_mld = W.pub("d_bm", 16)

    JEV = []
    for J in range(NJ):
        kq, wq_ = RQ.acquire()
        jev = dict(J=J, kq=kq, wq=wq_, ts=[])
        jev["t_q"] = W.pub("d_bq", 16)
        RQ.set_prod(kq, jev["t_q"])
        for t in range(2 * J + 2):
            kk, wk_ = RK.acquire()
            kv, wv_ = RV.acquire()
            tev = dict(t=t, kk=kk, wk=wk_, kv=kv, wv=wv_,
                       diag=(t >= 2 * J), subs=[])
            tev["t_k"] = W.pub("d_bk", 16)
            RK.set_prod(kk, tev["t_k"])
            tev["t_v"] = W.pub("d_bv", 16)
            RV.set_prod(kv, tev["t_v"])
            for sub in range(4):
                par = (t * 4 + sub) % 2
                sbk = sub
                aa, bb2 = 4, 5
                sv = dict(sub=sub, par=par, sbk=sbk, aa=aa, bb2=bb2)
                ks, ws = RB[sbk].acquire()
                sv["ks"], sv["ksw"] = ks, ws
                t_rd = W.pub("dv")
                RB[sbk].add_cons(ks, t_rd)
                sv["t_rd"] = t_rd
                tev["subs"].append(sv)
            tev["w_ng"] = px_last[0]
            tev["t_mx"] = W.pub("dv")
            tev["t_nm"] = W.pub("dv")
            tev["t_md"] = W.pub("dv")
            for sv in tev["subs"]:
                sub, sbk = sv["sub"], sv["sbk"]
                t_mm = W.pub("pe")
                RB[sbk].set_prod(sv["ks"], t_mm)
                sv["t_mm"] = t_mm
                if sub == 3:
                    RK.add_cons(kk, t_mm)
                    if t == 2 * J + 1:
                        RQ.add_cons(kq, t_mm)
            for sv in tev["subs"]:
                sub, par, sbk = sv["sub"], sv["par"], sv["sbk"]
                sv["w_ascp"] = lu_last[0]
                t_al = W.pub("ac")
                sv["t_al"] = t_al
                kpb, wpb = RPB[par].acquire()
                sv["wpb"] = wpb + [lu_last[0]]
                t_px = W.pub("ac")
                RPB[par].set_prod(kpb, t_px)
                RB[sbk].add_cons(sv["ks"], t_px)
                sv["t_px"] = t_px
                sv["kpb"] = kpb
                if sub == 3:
                    px_last[0] = t_px
                ktb, wtb = RB[6 + par].acquire()
                sv["wtb"] = wtb
                t_tr = W.pub("pe")
                RB[6 + par].set_prod(ktb, t_tr)
                RPB[par].add_cons(kpb, t_tr)
                sv["t_tr"] = t_tr
                kpt, wpt = RPT[par].acquire()
                sv["wpt"] = wpt
                t_pc = W.pub("ac")
                RPT[par].set_prod(kpt, t_pc)
                RB[6 + par].add_cons(ktb, t_pc)
                sv["t_pc"] = t_pc
                kaa, waa = RB[sv["aa"]].acquire()
                kab, wab = RB[sv["bb2"]].acquire()
                sv["waa"], sv["wab"] = waa, wab
                sv["kaa"], sv["kab"] = kaa, kab
                t_va = W.pub("pe")
                t_vb = W.pub("pe")
                RB[sv["aa"]].set_prod(kaa, t_va)
                RB[sv["bb2"]].set_prod(kab, t_vb)
                RPT[par].add_cons(kpt, t_vb)
                if sub == 3:
                    RV.add_cons(kv, t_vb)
                sv["t_va"], sv["t_vb"] = t_va, t_vb
            for sv in tev["subs"]:
                sv["t_rs"] = W.pub("dv")
                t_ca = W.pub("dv")
                t_cb = W.pub("dv")
                RB[sv["aa"]].add_cons(sv["kaa"], t_ca)
                RB[sv["bb2"]].add_cons(sv["kab"], t_cb)
                sv["t_ca"], sv["t_cb"] = t_ca, t_cb
            tev["t_lt"] = W.pub("dv")
            tev["t_lu"] = W.pub("dv")
            lu_last[0] = tev["t_lu"]
            jev["ts"].append(tev)
        jev["evac"] = []
        jev["t_li"] = W.pub("dv")
        for sub in range(4):
            ka, wa = RAT.acquire()
            t_at = W.pub("dv")
            RAT.set_prod(ka, t_at)
            t_st = W.pub("d_bat", 16)
            RAT.add_cons(ka, t_st)
            jev["evac"].append(dict(sub=sub, ka=ka, wa=wa,
                                    t_at=t_at, t_st=t_st))
        JEV.append(jev)

    ew, pub, pub16 = mkemit(S, None)

    @block.sync
    def _(sync):
        ew(sync, pre)
        S.inc(sync.dma_start(masks[:], maskd), "d_bm", 16)
        for jev in JEV:
            J = jev["J"]
            ew(sync, jev["wq"])
            pub16(sync.dma_start(
                qb[J % 2][:].rearrange("p (dc s) -> p dc s", s=512),
                QTs[:, :, J * 512:(J + 1) * 512].rearrange("dc p s -> p dc s")),
                jev["t_q"])
            for tev in jev["ts"]:
                t = tev["t"]
                ew(sync, tev["wk"])
                pub16(sync.dma_start(
                    kb[t % 2][:].rearrange("p (dc s) -> p dc s", s=512),
                    KTs[:, :, t * 512:(t + 1) * 512].rearrange("dc p s -> p dc s")),
                    tev["t_k"])
                ew(sync, tev["wv"])
                pub16(sync.dma_start(
                    vb[t % 2][:].rearrange("p (kc d) -> p kc d", d=1024),
                    Vs[t * 512:(t + 1) * 512, :]
                    .rearrange("(kc p) d -> p kc d", p=128)), tev["t_v"])

    ew, pub, pub16 = mkemit(S, "po")

    @block.gpsimd
    def _(g):
        for jev in JEV:
            J = jev["J"]
            for evc in jev["evac"]:
                sp = evc["sub"] % 2
                ew(g, [evc["t_at"]])
                pub16(g.dma_start(
                    ATTs[J * 512 + evc["sub"] * 128:
                         J * 512 + (evc["sub"] + 1) * 128, :],
                    astg[sp][:]), evc["t_st"])
    final_waits.append(("d_bat", S.c["d_bat"]))

    ew, pub, _ = mkemit(S, "pe")

    @block.tensor
    def _(te):
        for jev in JEV:
            J = jev["J"]
            for tev in jev["ts"]:
                t = tev["t"]
                for sv in tev["subs"]:
                    sub, sbk = sv["sub"], sv["sbk"]
                    ww = list(sv["ksw"])
                    if sub == 0:
                        ww += [tev["t_k"], tev["t_v"]]
                        if t == 0:
                            ww += [jev["t_q"]]
                    ew(te, ww)
                    for dcc in range(8):
                        ii = te.matmul(
                            pf[sbk][:, 0:512],
                            qb[J % 2][:, dcc * 512 + sub * 128: dcc * 512 + (sub + 1) * 128],
                            kb[t % 2][:, dcc * 512:(dcc + 1) * 512],
                            start=(dcc == 0), stop=(dcc == 7))
                    pub(ii, sv["t_mm"])
                for sv in tev["subs"]:
                    sub, par, sbk = sv["sub"], sv["par"], sv["sbk"]
                    ew(te, [sv["t_px"]] + sv["wtb"])
                    tb = 6 + par
                    for kc in range(4):
                        ii = te.transpose(pbt[tb - 6][:, kc * 128:(kc + 1) * 128],
                                          pbuf[par][:, kc * 128:(kc + 1) * 128],
                                          ident_s[:])
                    pub(ii, sv["t_tr"])
                    ew(te, [sv["t_pc"]] + sv["waa"] + sv["wab"])
                    for kc in range(4):
                        i1 = te.matmul(pf[sv["aa"]][:, 0:512],
                                       pts[par][:, kc * 128:(kc + 1) * 128],
                                       vb[t % 2][:, kc * 1024: kc * 1024 + 512],
                                       start=(kc == 0), stop=(kc == 3))
                        i2 = te.matmul(pf[sv["bb2"]][:, 0:512],
                                       pts[par][:, kc * 128:(kc + 1) * 128],
                                       vb[t % 2][:, kc * 1024 + 512: (kc + 1) * 1024],
                                       start=(kc == 0), stop=(kc == 3))
                    pub(i1, sv["t_va"])
                    pub(i2, sv["t_vb"])

    ew, pub, _ = mkemit(S, "dv")

    @block.vector
    def _(vec):
        ew(vec, [t_mld])
        for jev in JEV:
            J = jev["J"]
            vec.memset(accS[:], 0.0)
            vec.memset(mst[0][:], NEG)
            vec.memset(lbuf[:], 0.0)
            for tev in jev["ts"]:
                t, diag = tev["t"], tev["diag"]
                mprev, mcur = mst[t % 2], mst[(t + 1) % 2]
                for sv in tev["subs"]:
                    sub, sbk = sv["sub"], sv["sbk"]
                    ew(vec, [sv["t_mm"]])
                    if diag:
                        msl = masks[:].rearrange("p (q k) -> p q k", k=1024)[
                            :, sub, (t - 2 * J) * 512:(t - 2 * J + 1) * 512]
                        vec.tensor_add(pf[sbk][:, 0:512], pf[sbk][:, 0:512], msl)
                    pub(vec.reduce_max(mtmp[:, sub:sub + 1], pf[sbk][:, 0:512],
                                       axis=AX.X), sv["t_rd"])
                S.wait(vec, "dv", tev["subs"][3]["t_rd"][1])
                ew(vec, [tev["w_ng"]])
                pub(vec.tensor_tensor(mcur[:], mtmp[:], mprev[:], ALU.max),
                    tev["t_mx"])
                S.wait(vec, "dv", tev["t_mx"][1])
                pub(vec.tensor_scalar(negm[:], mcur[:], -1.0, None, ALU.mult),
                    tev["t_nm"])
                pub(vec.tensor_sub(dmb[:], mprev[:], mcur[:]), tev["t_md"])
                for sv in tev["subs"]:
                    sub = sv["sub"]
                    accv = accS[:, sub * 1024:(sub + 1) * 1024]
                    ew(vec, [sv["t_px"]])
                    pub(vec.tensor_scalar(accv, accv, asc[:, sub:sub + 1],
                                          None, ALU.mult), sv["t_rs"])
                    ew(vec, [sv["t_va"]])
                    pub(vec.tensor_add(accv[:, 0:512], accv[:, 0:512],
                                       pf[sv["aa"]][:, 0:512]), sv["t_ca"])
                    ew(vec, [sv["t_vb"]])
                    pub(vec.tensor_add(accv[:, 512:1024], accv[:, 512:1024],
                                       pf[sv["bb2"]][:, 0:512]), sv["t_cb"])
                pub(vec.tensor_mul(ltmp[:], lbuf[:], asc[:]), tev["t_lt"])
                S.wait(vec, "dv", tev["t_lt"][1])
                pub(vec.tensor_add(lbuf[:], ltmp[:], lp[:]), tev["t_lu"])
            S.wait(vec, "dv", jev["ts"][-1]["t_lu"][1])
            pub(vec.reciprocal(linv[:], lbuf[:]), jev["t_li"])
            S.wait(vec, "dv", jev["t_li"][1])
            for evc in jev["evac"]:
                sub = evc["sub"]
                sp = sub % 2
                ew(vec, evc["wa"])
                pub(vec.tensor_scalar(astg[sp][:],
                                      accS[:, sub * 1024:(sub + 1) * 1024],
                                      linv[:, sub:sub + 1], None, ALU.mult),
                    evc["t_at"])

    ew, pub, _ = mkemit(S, "ac")

    @block.scalar
    def _(sc):
        for jev in JEV:
            for tev in jev["ts"]:
                for sv in tev["subs"]:
                    sub, par, sbk = sv["sub"], sv["par"], sv["sbk"]
                    ew(sc, [tev["t_md"], sv["w_ascp"]])
                    pub(sc.activation(asc[:, sub:sub + 1], dmb[:, sub:sub + 1],
                                      AF.Exp), sv["t_al"])
                    ew(sc, [x for x in sv["wpb"] if x] + [sv["t_mm"]])
                    pub(sc.activation(pbuf[par][:], pf[sbk][:, 0:512], AF.Exp,
                                      bias=negm[:, sub:sub + 1], scale=1.0,
                                      accum_out=lp[:, sub:sub + 1]), sv["t_px"])
                    ew(sc, [sv["t_tr"]] + sv["wpt"])
                    tb = 6 + par
                    pub(sc.copy(pts[par][:], pbt[tb - 6][:, 0:512]), sv["t_pc"])


def _phase_C(nc, block, S, W, ctx, RB, pf, Wgf, t_wc,
             W1b, W2b, W3b, XNTs, ATTs, xq, OUT, final_waits, prev,
             dbgH=None, dbgA=None, dbgW=None, dbgG=None):
    sbt = lambda n, shp, dt: ctx.enter_context(nc.sbuf_tensor(n, shp, dt))
    NSLOT = 5
    slots = [sbt(f"c_w{i}", [128, 8192], BF16) for i in range(NSLOT)]
    xcb = [sbt(f"c_x{i}", [128, 4096], BF16) for i in range(2)]
    hT = sbt("c_h", [128, 8192], BF16)
    sg = [sbt(f"c_sg{i}", [128, 512], F32) for i in range(2)]
    accF = sbt("c_acc", [128, 4096], F32)
    wgs = sbt("c_wg", [128, 64], BF16)
    wbuf = sbt("c_wb", [128, 256], F32)
    gm1 = sbt("c_g1", [128, 4], F32)
    gm2 = sbt("c_g2", [128, 4], F32)
    gd1 = sbt("c_gd1", [128, 4], F32)
    gd2 = sbt("c_gd2", [128, 4], F32)
    ge = sbt("c_ge", [128, 8], F32)
    gp = sbt("c_gp", [128, 8], F32)
    geq1 = sbt("c_geq1", [128, 32], F32)
    gmask = sbt("c_gmask", [128, 32], F32)
    geq2 = sbt("c_geq2", [128, 32], F32)
    gta = sbt("c_gta", [128, 32], F32)
    cmbx = sbt("c_cx", [128, 4096], F32)
    cmba = sbt("c_ca", [128, 4096], F32)
    ostg = sbt("c_o", [128, 4096], F32)

    pre = []
    if prev:
        pre = [("d_qxnt", S.c["d_qxnt"]), ("d_bat", S.c["d_bat"])]

    RW = RBuf(NSLOT)
    RXC = RBuf(2)
    RSG = [RBuf(1) for _ in range(2)]
    RH = RBuf(1)
    RCMB = RBuf(1)
    ROS = RBuf(1)

    t_wg = W.pub("d_cwg", 16)

    GEV = []
    for chunk in range(NCHUNK):
        kx, wx = RXC.acquire()
        g = dict(chunk=chunk, kx=kx, wx=wx)
        g["t_x"] = W.pub("d_cx", 16)
        RXC.set_prod(kx, g["t_x"])
        kb0, wb0 = RB[0].acquire()
        g["wb0"] = wb0
        t_g = W.pub("pe")
        RB[0].set_prod(kb0, t_g)
        RXC.add_cons(kx, t_g)
        g["t_g"] = t_g
        g["t_ir"] = []
        for sub in range(4):
            t1 = W.pub("dv")
            te1 = W.pub("dv")
            tm = W.pub("dv")
            t2 = W.pub("dv")
            g["t_ir"].append((t1, te1, tm, t2))
        t_gd1 = W.pub("dv")
        t_gd = W.pub("dv")
        RB[0].add_cons(kb0, t_gd)
        g["t_gd1"] = t_gd1
        g["t_gd"] = t_gd
        g["t_ges"] = [W.pub("ac") for _ in range(8)]
        g["t_ge"] = g["t_ges"][-1]
        g["t_ga"] = [(W.pub("dv"), W.pub("dv"), W.pub("dv"), W.pub("dv"))
                     for _ in range(4)]
        g["t_gp"] = g["t_ga"][-1][-1]
        g["t_gq"] = [(W.pub("dv"), W.pub("dv")) for _ in range(4)]
        t_gw = W.pub("dv")
        g["t_gw"] = t_gw
        GEV.append(g)
    CEV = []
    for chunk in range(NCHUNK):
        kx, wx = RXC.acquire()
        cev = dict(chunk=chunk, kx=kx, wx=wx, es=[])
        cev["t_x"] = W.pub("d_cx", 16)
        RXC.set_prod(kx, cev["t_x"])
        kc, wc = RCMB.acquire()
        cev["wc"] = wc
        W.pub("d_cc", 16)
        cev["t_c"] = W.pub("d_cc", 16)
        RCMB.set_prod(kc, cev["t_c"])
        for e in range(E):
            eev = dict(e=e, wl=[], uv=[], ys=[])
            kh, wh = RH.acquire()
            eev["wh"] = wh
            items = [("W1", 0), ("W2", 0), ("W1", 1), ("W2", 1),
                     ("W3", 0), ("W3", 1)]
            slot_of = {}
            for wn, half in items:
                kw, wwl = RW.acquire()
                t_l = W.pub("d_cw", 16)
                RW.set_prod(kw, t_l)
                eev["wl"].append(dict(wn=wn, half=half, slot=kw % NSLOT,
                                      kw=kw, w=wwl, t=t_l))
                slot_of[(wn, half)] = (kw % NSLOT, kw, t_l)
            for half in range(2):
                for hc8 in range(8):
                    hc = half * 8 + hc8
                    uv = dict(half=half, hc8=hc8, hc=hc,
                              w1=slot_of[("W1", half)], w2=slot_of[("W2", half)])
                    kU, wU = RB[hc % 2].acquire()
                    kV, wV = RB[2 + hc % 2].acquire()
                    uv["wU"], uv["wV"] = wU, wV
                    t_u = W.pub("pe")
                    t_v = W.pub("pe")
                    RB[hc % 2].set_prod(kU, t_u)
                    RB[2 + hc % 2].set_prod(kV, t_v)
                    uv["t_u"], uv["t_v"] = t_u, t_v
                    if hc8 == 7:
                        RW.add_cons(slot_of[("W1", half)][1], t_u)
                        RW.add_cons(slot_of[("W2", half)][1], t_v)
                        if half == 1 and e == E - 1:
                            RXC.add_cons(kx, t_v)
                    ksg, wsg = RSG[hc % 2].acquire()
                    uv["wsg"] = wsg
                    t_s = W.pub("ac")
                    RSG[hc % 2].set_prod(ksg, t_s)
                    RB[hc % 2].add_cons(kU, t_s)
                    uv["t_s"] = t_s
                    t_h = W.pub("dv")
                    RSG[hc % 2].add_cons(ksg, t_h)
                    RB[2 + hc % 2].add_cons(kV, t_h)
                    uv["t_h"] = t_h
                    if hc == 15:
                        RH.set_prod(kh, t_h)
                    eev["uv"].append(uv)
            ybanks = [(4, 5), (0, 1), (2, 3), (4, 5)]
            for sbk in range(4):
                ba, bb2 = ybanks[sbk]
                y = dict(sbk=sbk, ba=ba, bb2=bb2,
                         w30=slot_of[("W3", 0)], w31=slot_of[("W3", 1)])
                ka, wa = RB[ba].acquire()
                kb2, wb2 = RB[bb2].acquire()
                y["wa"], y["wb"] = wa, wb2
                t_ya = W.pub("pe")
                t_yb = W.pub("pe")
                RB[ba].set_prod(ka, t_ya)
                RB[bb2].set_prod(kb2, t_yb)
                y["t_ya"], y["t_yb"] = t_ya, t_yb
                if sbk == 3:
                    RW.add_cons(slot_of[("W3", 0)][1], t_yb)
                    RW.add_cons(slot_of[("W3", 1)][1], t_yb)
                    RH.add_cons(kh, t_yb)
                t_ea = W.pub("dv")
                t_eb = W.pub("dv")
                RB[ba].add_cons(ka, t_ea)
                RB[bb2].add_cons(kb2, t_eb)
                y["t_ea"], y["t_eb"] = t_ea, t_eb
                eev["ys"].append(y)
            if dbgH is not None:
                eev["t_dbgh"] = W.pub("d_dbgh", 16)
                RH.add_cons(kh, eev["t_dbgh"])
                eev["t_dbga"] = W.pub("d_dbga", 16)
            cev["es"].append(eev)
        ko_, wo = ROS.acquire()
        cev["wo"] = wo
        t_o = W.pub("dv")
        ROS.set_prod(ko_, t_o)
        RCMB.add_cons(kc, t_o)
        cev["t_o"] = t_o
        t_os = W.pub("d_co", 16)
        ROS.add_cons(ko_, t_os)
        cev["t_os"] = t_os
        CEV.append(cev)

    ew, pub, pub16 = mkemit(S, "po")

    @block.gpsimd
    def _(g):
        pub16(g.dma_start(wgs[:], Wgf), t_wg)
        if dbgW is not None:
            ew(g, [GEV[0]["t_gw"]])
            for nm, tns in (("dbgm1", gm1), ("dbgm2", gm2), ("dbgd1", gd1),
                            ("dbgd2", gd2), ("dbge", ge), ("dbgp", gp),
                            ("dbgq1", geq1), ("dbgmk", gmask)):
                S.inc(g.dma_start(dbgG[nm], tns[:]), "d_dbgw", 16)
            ew(g, [GEV[-1]["t_gw"]])
            vv = S.inc(g.dma_start(dbgW, wbuf[:]), "d_dbgw", 16)
            final_waits.append(("d_dbgw", vv))
        if dbgH is not None:
            for cev in CEV:
                for eev in cev["es"]:
                    idx = cev["chunk"] * E + eev["e"]
                    ew(g, [eev["uv"][15]["t_h"]])
                    pub16(g.dma_start(dbgH[idx], hT[:]), eev["t_dbgh"])
                    ew(g, [eev["ys"][3]["t_eb"]])
                    pub16(g.dma_start(dbgA[idx], accF[:]), eev["t_dbga"])
        for cev in CEV:
            ew(g, [cev["t_o"]])
            pub16(g.dma_start(OUT[cev["chunk"]], ostg[:]), cev["t_os"])
    final_waits.append(("d_co", S.c["d_co"]))
    if dbgH is not None:
        final_waits.append(("d_dbgh", S.c["d_dbgh"]))
        final_waits.append(("d_dbga", S.c["d_dbga"]))

    ew, pub, pub16 = mkemit(S, None)

    @block.sync
    def _(sync):
        if t_wc is not None:
            ew(sync, [t_wc] + pre)

        def ld_xcb(chunk, kx, tok, waits):
            ew(sync, waits)
            pub16(sync.dma_start(
                xcb[kx % 2][:].rearrange("p (dc s) -> p dc s", s=512),
                XNTs[:, :, chunk * 512:(chunk + 1) * 512]
                .rearrange("dc p s -> p dc s")), tok)

        for g in GEV:
            ld_xcb(g["chunk"], g["kx"], g["t_x"], g["wx"])
        for cev in CEV:
            chunk = cev["chunk"]
            ld_xcb(chunk, cev["kx"], cev["t_x"], cev["wx"])
            ew(sync, cev["wc"])
            S.inc(sync.dma_start(cmbx[:], xq[chunk]), "d_cc", 16)
            ii = sync.dma_start(
                cmba[:].rearrange("p (sb d) -> p sb d", d=1024),
                ATTs[chunk * 512:(chunk + 1) * 512, :]
                .rearrange("(sb p) d -> p sb d", p=128))
            pub16(ii, cev["t_c"])
            for eev in cev["es"]:
                for wl in eev["wl"]:
                    ew(sync, wl["w"])
                    slot = wl["slot"]
                    if wl["wn"] == "W1":
                        ii = sync.dma_start(slots[slot][:], W1b[eev["e"], wl["half"]])
                    elif wl["wn"] == "W2":
                        ii = sync.dma_start(slots[slot][:], W2b[eev["e"], wl["half"]])
                    else:
                        ii = sync.dma_start(
                            slots[slot][:].rearrange("p (hc d) -> p hc d", d=1024),
                            W3b[eev["e"], wl["half"] * 8:(wl["half"] + 1) * 8]
                            .rearrange("hc p d -> p hc d"))
                    pub16(ii, wl["t"])

    ew, pub, _ = mkemit(S, "pe")

    @block.tensor
    def _(te):
        for g in GEV:
            chunk = g["chunk"]
            ww = [g["t_x"], t_wg] + g["wb0"]
            ew(te, ww)
            for sbk in range(4):
                for dcc in range(8):
                    ii = te.matmul(
                        pf[0][:, sbk * 8:(sbk + 1) * 8],
                        xcb[g["kx"] % 2][:, dcc * 512 + sbk * 128:
                                         dcc * 512 + (sbk + 1) * 128],
                        wgs[:, dcc * 8:(dcc + 1) * 8],
                        start=(dcc == 0), stop=(dcc == 7))
            pub(ii, g["t_g"])
        for cev in CEV:
            chunk = cev["chunk"]
            for eev in cev["es"]:
                for uv in eev["uv"]:
                    hc, hc8, half = uv["hc"], uv["hc8"], uv["half"]
                    s1 = uv["w1"][0]
                    s2 = uv["w2"][0]
                    ww = list(uv["wU"])
                    if hc8 == 0:
                        ww += [uv["w1"][2], uv["w2"][2]]
                        if hc == 0 and eev["e"] == 0:
                            ww += [cev["t_x"]]
                    ew(te, ww)
                    for dcc in range(8):
                        ii = te.matmul(
                            pf[hc % 2][:, 0:512],
                            slots[s1][:, dcc * 1024 + hc8 * 128:
                                      dcc * 1024 + (hc8 + 1) * 128],
                            xcb[cev["kx"] % 2][:, dcc * 512:(dcc + 1) * 512],
                            start=(dcc == 0), stop=(dcc == 7))
                    pub(ii, uv["t_u"])
                    ew(te, uv["wV"])
                    for dcc in range(8):
                        ii = te.matmul(
                            pf[2 + hc % 2][:, 0:512],
                            slots[s2][:, dcc * 1024 + hc8 * 128:
                                      dcc * 1024 + (hc8 + 1) * 128],
                            xcb[cev["kx"] % 2][:, dcc * 512:(dcc + 1) * 512],
                            start=(dcc == 0), stop=(dcc == 7))
                    pub(ii, uv["t_v"])
                s30 = eev["ys"][0]["w30"][0]
                s31 = eev["ys"][0]["w31"][0]
                for i, y in enumerate(eev["ys"]):
                    sbk = y["sbk"]
                    ww = list(y["wa"]) + list(y["wb"])
                    if i == 0:
                        ww += [y["w30"][2], y["w31"][2],
                               eev["uv"][15]["t_h"]]
                    ew(te, ww)
                    for hc in range(16):
                        half, hc8 = hc // 8, hc % 8
                        w3s = s30 if half == 0 else s31
                        lhs = hT[:, hc * 512 + sbk * 128: hc * 512 + (sbk + 1) * 128]
                        i1 = te.matmul(pf[y["ba"]][:, 0:512], lhs,
                                       slots[w3s][:, hc8 * 1024: hc8 * 1024 + 512],
                                       start=(hc == 0), stop=(hc == 15))
                        i2 = te.matmul(pf[y["bb2"]][:, 0:512], lhs,
                                       slots[w3s][:, hc8 * 1024 + 512: (hc8 + 1) * 1024],
                                       start=(hc == 0), stop=(hc == 15))
                    pub(i1, y["t_ya"])
                    pub(i2, y["t_yb"])

    ew, pub, _ = mkemit(S, "ac")

    @block.scalar
    def _(sc):
        for g in GEV:
            ew(sc, [g["t_gd"]])
            for sub in range(4):
                pub(sc.activation(ge[:, sub:sub + 1], gd1[:, sub:sub + 1],
                                  AF.Exp), g["t_ges"][2 * sub])
                pub(sc.activation(ge[:, 4 + sub:5 + sub], gd2[:, sub:sub + 1],
                                  AF.Exp), g["t_ges"][2 * sub + 1])
        for cev in CEV:
            for eev in cev["es"]:
                for uv in eev["uv"]:
                    hc = uv["hc"]
                    ew(sc, [uv["t_u"]] + uv["wsg"])
                    pub(sc.activation(sg[hc % 2][:], pf[hc % 2][:, 0:512],
                                      AF.Silu), uv["t_s"])

    ew, pub, _ = mkemit(S, "dv")

    @block.vector
    def _(vec):
        for g in GEV:
            chunk = g["chunk"]
            ew(vec, [g["t_g"]])
            for sub in range(4):
                lg = pf[0][:, sub * 8:(sub + 1) * 8]
                t1, te1, tm, t2 = g["t_ir"][sub]
                pub(vec.reduce_max(gm1[:, sub:sub + 1], lg, axis=AX.X), t1)
                S.wait(vec, "dv", t1[1])
                pub(vec.tensor_scalar(geq1[:, sub * 8:(sub + 1) * 8], lg,
                                      gm1[:, sub:sub + 1], None, ALU.is_equal), te1)
                S.wait(vec, "dv", te1[1])
                pub(vec.scalar_tensor_tensor(gmask[:, sub * 8:(sub + 1) * 8],
                                             geq1[:, sub * 8:(sub + 1) * 8], NEG, lg,
                                             ALU.mult, ALU.add), tm)
                S.wait(vec, "dv", tm[1])
                pub(vec.reduce_max(gm2[:, sub:sub + 1],
                                   gmask[:, sub * 8:(sub + 1) * 8], axis=AX.X), t2)
                S.wait(vec, "dv", t2[1])
            pub(vec.tensor_sub(gd1[:], gm2[:], gm1[:]), g["t_gd1"])
            pub(vec.tensor_sub(gd2[:], gm1[:], gm2[:]), g["t_gd"])
            ew(vec, [g["t_ge"]])
            for sub in range(4):
                ta1, ta2, tr1, tr2 = g["t_ga"][sub]
                pub(vec.tensor_scalar(gp[:, sub:sub + 1], ge[:, sub:sub + 1],
                                      1.0, None, ALU.add), ta1)
                pub(vec.tensor_scalar(gp[:, 4 + sub:5 + sub],
                                      ge[:, 4 + sub:5 + sub], 1.0, None, ALU.add), ta2)
                S.wait(vec, "dv", ta2[1])
                pub(vec.reciprocal(gp[:, sub:sub + 1], gp[:, sub:sub + 1]), tr1)
                pub(vec.reciprocal(gp[:, 4 + sub:5 + sub], gp[:, 4 + sub:5 + sub]),
                    tr2)
            S.wait(vec, "dv", g["t_gp"][1])
            for sub in range(4):
                sl = slice(sub * 8, (sub + 1) * 8)
                tta, tq = g["t_gq"][sub]
                pub(vec.tensor_scalar(gta[:, sl], geq1[:, sl], gp[:, sub:sub + 1],
                                      None, ALU.mult), tta)
                pub(vec.tensor_scalar(geq2[:, sl], gmask[:, sl], gm2[:, sub:sub + 1],
                                      None, ALU.is_equal), tq)
                S.wait(vec, "dv", tq[1])
                ii = vec.scalar_tensor_tensor(
                    wbuf[:, chunk * 32 + sub * 8: chunk * 32 + (sub + 1) * 8],
                    geq2[:, sl], gp[:, 4 + sub:5 + sub], gta[:, sl],
                    ALU.mult, ALU.add)
            pub(ii, g["t_gw"])
        for cev in CEV:
            chunk = cev["chunk"]
            vec.memset(accF[:], 0.0)
            for eev in cev["es"]:
                e = eev["e"]
                for uv in eev["uv"]:
                    hc = uv["hc"]
                    ww = [uv["t_s"], uv["t_v"]]
                    if hc == 0:
                        ww = ww + eev["wh"]
                    ew(vec, ww)
                    pub(vec.tensor_mul(hT[:, hc * 512:(hc + 1) * 512],
                                       sg[hc % 2][:], pf[2 + hc % 2][:, 0:512]),
                        uv["t_h"])
                if dbgA is not None and (e > 0 or chunk > 0):
                    prev_idx = chunk * E + e - 1 if e > 0 else (chunk - 1) * E + E - 1
                    S.wait(vec, "d_dbga", 16 * (prev_idx + 1))
                for y in eev["ys"]:
                    sbk = y["sbk"]
                    wcol = wbuf[:, chunk * 32 + sbk * 8 + e:
                                chunk * 32 + sbk * 8 + e + 1]
                    accv = accF[:, sbk * 1024:(sbk + 1) * 1024]
                    ew(vec, [y["t_ya"]])
                    pub(vec.scalar_tensor_tensor(accv[:, 0:512], pf[y["ba"]][:, 0:512],
                                                 wcol, accv[:, 0:512],
                                                 ALU.mult, ALU.add), y["t_ea"])
                    ew(vec, [y["t_yb"]])
                    pub(vec.scalar_tensor_tensor(accv[:, 512:1024],
                                                 pf[y["bb2"]][:, 0:512],
                                                 wcol, accv[:, 512:1024],
                                                 ALU.mult, ALU.add), y["t_eb"])
            ew(vec, [cev["t_c"]] + cev["wo"])
            vec.tensor_add(ostg[:], cmbx[:], cmba[:])
            pub(vec.tensor_add(ostg[:], ostg[:], accF[:]), cev["t_o"])


def kernel(L0_attn_x, L0_attn_wq, L0_attn_wk, L0_attn_wv,
           L0_ffn_Wg, L0_ffn_W1, L0_ffn_W2, L0_ffn_W3):
    per_core, row_idx = host_layouts(
        L0_attn_x, L0_attn_wq, L0_attn_wk, L0_attn_wv,
        L0_ffn_Wg, L0_ffn_W1, L0_ffn_W2, L0_ffn_W3)
    nc = build_program()
    in_maps = [dict(per_core[c]) for c in range(NCORE)]
    res = run_bass_kernel_spmd(nc, in_maps, core_ids=list(range(NCORE)))
    out = np.empty((BATCH, SEQ, D), dtype=np.float32)
    for c in range(NCORE):
        oc = np.asarray(res.results[c]["out"])
        oc = oc.reshape(NCHUNK, 128, 4, D).transpose(0, 2, 1, 3).reshape(QROWS, D)
        out[c // 2][row_idx[c]] = oc
    return out


# version 02
# speedup vs baseline: 1.0823x; 1.0823x over previous
import numpy as np
import ml_dtypes
import concourse.bass as bass
import concourse.mybir as mybir
from contextlib import ExitStack
from concourse.bass_utils import run_bass_kernel_spmd

F32 = mybir.dt.float32
BF16 = mybir.dt.bfloat16
AF = mybir.ActivationFunctionType
ALU = mybir.AluOpType
AX = mybir.AxisListType

D = 1024
E = 8
SEQ = 8192
BATCH = 4
NCORE = 8
QROWS = 4096
NJ = 8
NCHUNK = 8
DC = 8
NEG = -1.0e30
EPS = 1e-5
SCL = 1.0 / 32.0
DEBUG_C = False


class SemBook:
    def __init__(self, nc, stack):
        self.nc, self.stack = nc, stack
        self.h, self.c = {}, {}

    def sem(self, name):
        if name not in self.h:
            self.h[name] = self.stack.enter_context(self.nc.semaphore(name))
            self.c[name] = 0
        return self.h[name]

    def inc(self, instr, name, n=1):
        instr.then_inc(self.sem(name), n)
        self.c[name] += n
        return self.c[name]

    def wait(self, eng, name, val):
        eng.wait_ge(self.sem(name), val)


class Walk:

    def __init__(self):
        self.cnt = {}

    def pub(self, sem, n=1):
        v = self.cnt.get(sem, 0) + n
        self.cnt[sem] = v
        return (sem, v)


class RWait:

    def __init__(self, buf, k):
        self.buf, self.k = buf, k

    def resolve(self):
        b, k = self.buf, self.k
        if k < b.nbuf:
            return []
        u = b.uses[k - b.nbuf]
        out = []
        if u["prod"] is not None:
            out.append(u["prod"])
        out.extend(u["cons"])
        return out


class RBuf:

    def __init__(self, nbuf=1):
        self.nbuf = nbuf
        self.uses = []

    def acquire(self):
        k = len(self.uses)
        self.uses.append({"prod": None, "cons": []})
        return k, [RWait(self, k)]

    def set_prod(self, k, tok):
        self.uses[k]["prod"] = tok

    def add_cons(self, k, tok):
        self.uses[k]["cons"].append(tok)


def mkemit(S, own):
    def emitw(eng, waits):
        best = {}
        flat = []
        for t in waits:
            if t is None:
                continue
            if isinstance(t, RWait):
                flat.extend(t.resolve())
            else:
                flat.append(t)
        for sem, v in flat:
            if sem == own:
                continue
            if best.get(sem, 0) < v:
                best[sem] = v
        for sem, v in best.items():
            S.wait(eng, sem, v)

    def pub(instr, tok):
        got = S.inc(instr, tok[0], 1)
        assert got == tok[1], (tok, got)

    def pub16(instr, tok):
        got = S.inc(instr, tok[0], 16)
        assert got == tok[1], (tok, got)
    return emitw, pub, pub16


def _rope_tables_np(m, d):
    i = np.arange(d // 2, dtype=np.float32)
    phi = np.power(np.float32(10000.0),
                   np.float32(-2.0) * (i - np.float32(1.0)) / np.float32(d))
    pos = np.arange(m, dtype=np.float32)
    ang = pos[:, None] * phi[None, :]
    return np.cos(ang, dtype=np.float32), np.sin(ang, dtype=np.float32)


def host_layouts(L0_attn_x, L0_attn_wq, L0_attn_wk, L0_attn_wv,
                 L0_ffn_Wg, L0_ffn_W1, L0_ffn_W2, L0_ffn_W3):
    x = np.ascontiguousarray(np.asarray(L0_attn_x, dtype=np.float32))
    wq = np.asarray(L0_attn_wq, dtype=np.float32)
    wk = np.asarray(L0_attn_wk, dtype=np.float32)
    wv = np.asarray(L0_attn_wv, dtype=np.float32)
    Wg = np.asarray(L0_ffn_Wg, dtype=np.float32)
    W1 = np.asarray(L0_ffn_W1, dtype=np.float32)
    W2 = np.asarray(L0_ffn_W2, dtype=np.float32)
    W3 = np.asarray(L0_ffn_W3, dtype=np.float32)

    perm = np.concatenate([np.arange(0, D, 2), np.arange(1, D, 2)])

    def wT_lay(w, permute):
        wt = (w[perm] if permute else w).T
        return np.ascontiguousarray(
            wt.reshape(DC, 128, D).transpose(1, 0, 2).reshape(128, DC * D))

    wqT, wkT, wvT = wT_lay(wq, True), wT_lay(wk, True), wT_lay(wv, False)

    W1h = np.ascontiguousarray(
        W1.reshape(E, DC, 128, 2, 1024).transpose(0, 3, 2, 1, 4).reshape(E, 2, 128, 8192))
    W2h = np.ascontiguousarray(
        W2.reshape(E, DC, 128, 2, 1024).transpose(0, 3, 2, 1, 4).reshape(E, 2, 128, 8192))
    W3h = np.ascontiguousarray(W3.reshape(E, 16, 128, D))
    Wgh = np.ascontiguousarray(
        Wg.reshape(DC, 128, E).transpose(1, 0, 2).reshape(128, DC * E))

    def rows_lay(a):
        n = a.shape[0] // 512
        return np.ascontiguousarray(
            a.reshape(n, 4, 128, D).transpose(0, 2, 1, 3).reshape(n, 128, 4 * D))

    cosT, sinT = _rope_tables_np(SEQ, D)
    cosK = np.ascontiguousarray(cosT.T.reshape(4, 128, SEQ))
    sinK = np.ascontiguousarray(sinT.T.reshape(4, 128, SEQ))
    ident = np.eye(128, dtype=ml_dtypes.bfloat16)

    per_core, row_idx = [], {}
    for c in range(NCORE):
        b, h = c // 2, c % 2
        rows = np.concatenate(
            [np.arange(J * 1024 + h * 512, J * 1024 + h * 512 + 512) for J in range(NJ)])
        row_idx[c] = rows
        xqc = np.ascontiguousarray(x[b][rows])
        cosQ = np.ascontiguousarray(cosK[:, :, rows] * np.float32(SCL))
        sinQ = np.ascontiguousarray(sinK[:, :, rows] * np.float32(SCL))
        qpos = (h * 512 + np.arange(512))[:, None]
        kpos = np.arange(1024)[None, :]
        mask = np.where(kpos <= qpos, np.float32(0.0), np.float32(NEG))
        mask = np.ascontiguousarray(
            mask.reshape(4, 128, 1024).transpose(1, 0, 2).reshape(128, 4096))
        per_core.append(dict(
            xb=rows_lay(x[b]), xq=rows_lay(xqc), cosQ=cosQ, sinQ=sinQ, mask=mask,
            wqT=wqT, wkT=wkT, wvT=wvT, cosK=cosK, sinK=sinK,
            W1h=W1h, W2h=W2h, W3h=W3h, Wgh=Wgh, ident=ident))
    return per_core, row_idx


def build_program(phases=("A", "B", "C"), taps=()):
    nc = bass.Bass("TRN2", target_bir_lowering=False)
    final_waits = []

    def scratch(name, shape, dtype, produced_by):
        if produced_by not in phases:
            kind = "ExternalInput"
        elif name in taps:
            kind = "ExternalOutput"
        else:
            kind = "Internal"
        return nc.dram_tensor(name, shape, dtype, kind=kind).ap()

    xb = nc.dram_tensor("xb", [16, 128, 4096], F32, kind="ExternalInput").ap()
    xq = nc.dram_tensor("xq", [8, 128, 4096], F32, kind="ExternalInput").ap()
    wqT = nc.dram_tensor("wqT", [128, 8192], F32, kind="ExternalInput").ap()
    wkT = nc.dram_tensor("wkT", [128, 8192], F32, kind="ExternalInput").ap()
    wvT = nc.dram_tensor("wvT", [128, 8192], F32, kind="ExternalInput").ap()
    cosK = nc.dram_tensor("cosK", [4, 128, SEQ], F32, kind="ExternalInput").ap()
    sinK = nc.dram_tensor("sinK", [4, 128, SEQ], F32, kind="ExternalInput").ap()
    cosQ = nc.dram_tensor("cosQ", [4, 128, QROWS], F32, kind="ExternalInput").ap()
    sinQ = nc.dram_tensor("sinQ", [4, 128, QROWS], F32, kind="ExternalInput").ap()
    maskd = nc.dram_tensor("mask", [128, 4096], F32, kind="ExternalInput").ap()
    W1f = nc.dram_tensor("W1h", [E, 2, 128, 8192], F32, kind="ExternalInput").ap()
    W2f = nc.dram_tensor("W2h", [E, 2, 128, 8192], F32, kind="ExternalInput").ap()
    W3f = nc.dram_tensor("W3h", [E, 16, 128, D], F32, kind="ExternalInput").ap()
    Wgf = nc.dram_tensor("Wgh", [128, DC * E], F32, kind="ExternalInput").ap()
    ident_d = nc.dram_tensor("ident", [128, 128], BF16, kind="ExternalInput").ap()

    KTs = scratch("KTs", [DC, 128, SEQ], BF16, "A")
    Vs = scratch("Vs", [SEQ, D], BF16, "A")
    QTs = scratch("QTs", [DC, 128, QROWS], BF16, "A")
    XNTs = scratch("XNTs", [DC, 128, QROWS], BF16, "A")
    ATTs = scratch("ATTs", [QROWS, D], F32, "B")
    W1b = nc.dram_tensor("W1b", [E, 2, 128, 8192], BF16, kind="Internal").ap()
    W2b = nc.dram_tensor("W2b", [E, 2, 128, 8192], BF16, kind="Internal").ap()
    W3b = nc.dram_tensor("W3b", [E, 16, 128, D], BF16, kind="Internal").ap()
    OUT = nc.dram_tensor("out", [NCHUNK, 128, 4096], F32, kind="ExternalOutput").ap()
    dbgH = dbgA = dbgW = None
    dbgG = None
    if DEBUG_C:
        dbgH = nc.dram_tensor("dbgH", [NCHUNK * E, 128, 8192], BF16,
                              kind="ExternalOutput").ap()
        dbgA = nc.dram_tensor("dbgA", [NCHUNK * E, 128, 4096], F32,
                              kind="ExternalOutput").ap()
        dbgW = nc.dram_tensor("dbgW", [128, 256], F32,
                              kind="ExternalOutput").ap()
        dbgG = {nm: nc.dram_tensor(nm, [128, wdt], F32, kind="ExternalOutput").ap()
                for nm, wdt in (("dbgm1", 4), ("dbgm2", 4), ("dbgd1", 4),
                                ("dbgd2", 4), ("dbge", 8), ("dbgp", 8),
                                ("dbgq1", 32), ("dbgmk", 32))}

    with ExitStack() as top:
        S = SemBook(nc, top)
        W = Walk()
        for s in ("pe", "dv", "ac", "po"):
            S.sem(s)
        block = top.enter_context(nc.Block())
        pf = [top.enter_context(nc.psum_tensor(f"pf{i}", [128, 512], F32))
              for i in range(6)]
        pbt = [top.enter_context(nc.psum_tensor(f"pbt{i}", [128, 1024], BF16))
               for i in range(2)]
        RB = [RBuf(1) for _ in range(8)]
        ident_s = top.enter_context(nc.sbuf_tensor("ident_s", [128, 128], BF16))

        t_ident = W.pub("d_id", 16)
        t_wc = None
        if "C" in phases:
            t_wc = [W.pub("d_wc0", 16), W.pub("d_wc1", 16), W.pub("d_wc2", 16)]

        @block.gpsimd
        def _(g):
            S.inc(g.dma_start(ident_s[:], ident_d), "d_id", 16)
            if "C" in phases:
                S.inc(g.dma_start(W1b, W1f), "d_wc0", 16)
                S.inc(g.dma_start(W2b, W2f), "d_wc1", 16)
                S.inc(g.dma_start(W3b, W3f), "d_wc2", 16)

        if "A" in phases:
            with ExitStack() as ctx:
                _phase_A(nc, block, S, W, ctx, RB, pf, pbt, ident_s, t_ident,
                         xb, xq, wqT, wkT, wvT, cosK, sinK, cosQ, sinQ,
                         KTs, Vs, QTs, XNTs, final_waits)
        if "B" in phases:
            with ExitStack() as ctx:
                _phase_B(nc, block, S, W, ctx, RB, pf, pbt, ident_s, t_ident,
                         KTs, Vs, QTs, maskd, ATTs, final_waits,
                         a_done=("A" in phases))
        if "C" in phases:
            with ExitStack() as ctx:
                _phase_C(nc, block, S, W, ctx, RB, pf, Wgf, t_wc,
                         W1b, W2b, W3b, XNTs, ATTs, xq, OUT, final_waits,
                         prev=("B" in phases), dbgH=dbgH, dbgA=dbgA, dbgW=dbgW, dbgG=dbgG)

        @block.sync
        def _(sync):
            for nm, val in final_waits:
                S.wait(sync, nm, val)
    return nc


def _phase_A(nc, block, S, W, ctx, RB, pf, pbt, ident_s, t_ident,
             xb, xq, wqT, wkT, wvT, cosK, sinK, cosQ, sinQ,
             KTs, Vs, QTs, XNTs, final_waits):
    sbt = lambda n, shp, dt: ctx.enter_context(nc.sbuf_tensor(n, shp, dt))
    xbuf = [sbt(f"a_x{i}", [128, 4096], F32) for i in range(2)]
    cbuf = [sbt(f"a_c{i}", [128, 2048], F32) for i in range(2)]
    sbin = [sbt(f"a_s{i}", [128, 2048], F32) for i in range(2)]
    xnb = [sbt(f"a_xnb{i}", [128, 1024], BF16) for i in range(2)]
    xnt = [sbt(f"a_xnt{i}", [128, 4096], BF16) for i in range(2)]
    ko = [sbt(f"a_ko{i}", [128, 4096], BF16) for i in range(2)]
    vo = [sbt(f"a_vo{i}", [128, 1024], BF16) for i in range(2)]
    rt = [sbt(f"a_rt{i}", [128, 512], F32) for i in range(4)]
    bnst = [sbt(f"a_bn{i}", [128, 12], F32) for i in range(2)]
    mv = [sbt(f"a_mv{i}", [128, 2], F32) for i in range(2)]
    rstd = [sbt(f"a_rs{i}", [128, 1], F32) for i in range(2)]
    wqs = sbt("a_wq", [128, 8192], BF16)
    wks = sbt("a_wk", [128, 8192], BF16)
    wvs = sbt("a_wv", [128, 8192], BF16)
    eps_t = sbt("a_eps", [128, 1], F32)

    t_eps = W.pub("dv")

    @block.vector
    def _(vec):
        got = S.inc(vec.memset(eps_t[:], EPS), "dv", 1)
        assert got == t_eps[1]

    t_aw = [W.pub(f"d_aw{i}", 16) for i in range(3)]

    @block.gpsimd
    def _(g):
        for i, (wsb, wdr) in enumerate(((wks, wkT), (wvs, wvT), (wqs, wqT))):
            S.inc(g.dma_start(wsb[:], wdr), f"d_aw{i}", 16)

    def one_pass(tag, nst, src_x, cos_d, sin_d, wproj, do_v, do_q,
                 kt_dst, v_dst, xnt_dst):
        RX = RBuf(2)
        RBN = RBuf(2)
        RMV = RBuf(2)
        RRS = RBuf(2)
        RXNB = RBuf(2)
        RXNT = RBuf(2)
        RKO = RBuf(2)
        RVO = RBuf(2)
    

        EV = []
        for st in range(nst):
            par = st % 2
            ev = dict(st=st, par=par)
            k, w = RX.acquire()
            ev["x"] = (k, w)
            dxp = f"d_{tag}x{k % 2}"
            W.pub(dxp, 16)
            W.pub(dxp, 16)
            tkx = W.pub(dxp, 16)
            RX.set_prod(k, tkx)
            ev["t_x"] = tkx
            ev["tiles"] = []
            for t in range(4):
                tp = t % 2
                tl = dict(t=t, tp=tp)
                kb, wb = RBN.acquire()
                tl["bn"] = (kb, wb)
                tl["t_bn0"] = W.pub("dv")
                tl["t_bn"] = W.pub("dv")
                km, wm = RMV.acquire()
                tl["mvw"] = wm
                t_ag = W.pub("dv")
                RBN.add_cons(kb, t_ag)
                RMV.set_prod(km, t_ag)
                tl["t_ag"] = t_ag
                kr, wr = RRS.acquire()
                tl["rsw"] = wr
                t_ln = W.pub("ac")
                RMV.add_cons(km, t_ln)
                tl["t_ln"] = t_ln
                t_rs = W.pub("ac")
                RRS.set_prod(kr, t_rs)
                tl["t_rs"] = t_rs
                kn, wn = RXNB.acquire()
                tl["xnbw"] = wn
                t_xn = W.pub("dv")
                RXNB.set_prod(kn, t_xn)
                RMV.add_cons(km, t_xn)
                RRS.add_cons(kr, t_xn)
                tl["t_xn"] = t_xn
                kt, wt = RB[6 + tp].acquire()
                tl["tbw"] = wt
                t_tp = W.pub("pe")
                RB[6 + tp].set_prod(kt, t_tp)
                RXNB.add_cons(kn, t_tp)
                tl["t_tp"] = t_tp
                if t == 0:
                    kxt, wxt = RXNT.acquire()
                    ev["xnt"] = kxt
                    ev["xntw"] = wxt
                t_ev = W.pub("ac")
                RB[6 + tp].add_cons(kt, t_ev)
                tl["t_ev"] = t_ev
                ev["tiles"].append(tl)
            RXNT.set_prod(ev["xnt"], ev["tiles"][3]["t_ev"])
            ev["prs"] = []
            for pr in range(4):
                ba, bb = (0, 1) if pr % 2 == 0 else (2, 3)
                q = dict(pr=pr, ba=ba, bb=bb)
                ka, wa = RB[ba].acquire()
                kb2, wb2 = RB[bb].acquire()
                q["baw"], q["bbw"] = wa, wb2
                t_pa = W.pub("pe")
                t_pb = W.pub("pe")
                RB[ba].set_prod(ka, t_pa)
                RB[bb].set_prod(kb2, t_pb)
                q["t_pa"], q["t_pb"] = t_pa, t_pb
                q["ka"], q["kb"] = ka, kb2
                if pr == 0:
                    kk, wk = RKO.acquire()
                    ev["ko"] = kk
                    ev["kow"] = wk
                t_ra = W.pub("dv")
                t_rb = W.pub("dv")
                t_ro = W.pub("dv")
                RB[ba].add_cons(ka, t_ra)
                RB[bb].add_cons(kb2, t_rb)
                q["t_ra"], q["t_rb"], q["t_ro"] = t_ra, t_rb, t_ro
                if pr == 3:
                    RX.add_cons(ev["x"][0], t_ro)
                ev["prs"].append(q)
            RKO.set_prod(ev["ko"], ev["prs"][3]["t_ro"])
            dkop = f"d_{tag}ko{ev['ko'] % 2}"
            for _ in range(7):
                W.pub(dkop, 16)
            tsk = W.pub(dkop, 16)
            RKO.add_cons(ev["ko"], tsk)
            ev["t_stko"] = tsk
            if do_q:
                tsx = W.pub(f"d_{tag}xnt{ev['xnt'] % 2}", 16)
                RXNT.add_cons(ev["xnt"], tsx)
                ev["t_stxnt"] = tsx
                RXNT.add_cons(ev["xnt"], ev["prs"][3]["t_pb"])
            if do_v:
                ev["vs"] = []
                for sbk in range(4):
                    ba, bb = (4, 5) if sbk % 2 == 0 else (0, 1)
                    v = dict(sbk=sbk, ba=ba, bb=bb)
                    ka, wa = RB[ba].acquire()
                    kb2, wb2 = RB[bb].acquire()
                    v["baw"], v["bbw"] = wa, wb2
                    t_va = W.pub("pe")
                    t_vb = W.pub("pe")
                    RB[ba].set_prod(ka, t_va)
                    RB[bb].set_prod(kb2, t_vb)
                    v["t_va"], v["t_vb"] = t_va, t_vb
                    if sbk == 3:
                        RXNT.add_cons(ev["xnt"], t_vb)
                    kv, wv = RVO.acquire()
                    v["vow"] = wv
                    t_c1 = W.pub("ac")
                    t_c2 = W.pub("ac")
                    RB[ba].add_cons(ka, t_c1)
                    RB[bb].add_cons(kb2, t_c2)
                    RVO.set_prod(kv, t_c2)
                    v["t_c1"], v["t_c2"] = t_c1, t_c2
                    tsv = W.pub(f"d_{tag}vo{kv % 2}", 16)
                    RVO.add_cons(kv, tsv)
                    v["t_stv"] = tsv
                    ev["vs"].append(v)
            EV.append(ev)

        ew, pub, pub16 = mkemit(S, None)

        @block.sync
        def _(sync):
            def loads(ev):
                par, st = ev["par"], ev["st"]
                dxp = ev["t_x"][0]
                ew(sync, ev["x"][1])
                S.inc(sync.dma_start(xbuf[par][:], src_x[st]), dxp, 16)
                S.inc(sync.dma_start(
                    cbuf[par][:].rearrange("p (oc s) -> p oc s", s=512),
                    cos_d[:, :, st * 512:(st + 1) * 512]
                    .rearrange("oc p s -> p oc s")), dxp, 16)
                S.inc(sync.dma_start(
                    sbin[par][:].rearrange("p (oc s) -> p oc s", s=512),
                    sin_d[:, :, st * 512:(st + 1) * 512]
                    .rearrange("oc p s -> p oc s")), dxp, 16)
                assert S.c[dxp] == ev["t_x"][1]

            def stores(ev):
                st, par = ev["st"], ev["par"]
                dkop = ev["t_stko"][0]
                ew(sync, [ev["prs"][3]["t_ro"]])
                for oc in range(8):
                    ii = sync.dma_start(kt_dst[oc, :, st * 512:(st + 1) * 512],
                                        ko[par][:, oc * 512:(oc + 1) * 512])
                    S.inc(ii, dkop, 16)
                assert S.c[dkop] == ev["t_stko"][1]
                if do_q:
                    ew(sync, [ev["tiles"][3]["t_ev"]])
                    ii = sync.dma_start(
                        xnt_dst[:, :, st * 512:(st + 1) * 512]
                        .rearrange("dc p s -> p dc s"),
                        xnt[par][:].rearrange("p (dc s) -> p dc s", s=512))
                    pub16(ii, ev["t_stxnt"])
                if do_v:
                    for v in ev["vs"]:
                        vp = v["sbk"] % 2
                        ew(sync, [v["t_c2"]])
                        ii = sync.dma_start(
                            v_dst[st * 512 + v["sbk"] * 128:
                                  st * 512 + (v["sbk"] + 1) * 128, :],
                            vo[vp][:])
                        pub16(ii, v["t_stv"])

            for i, ev in enumerate(EV):
                loads(ev)
                if i > 0:
                    stores(EV[i - 1])
            stores(EV[-1])

        ew, pub, _ = mkemit(S, "dv")

        @block.vector
        def _(vec):
            for ev in EV:
                par = ev["par"]
                for tl in ev["tiles"]:
                    t, tp = tl["t"], tl["tp"]
                    xt = xbuf[par][:, t * 1024:(t + 1) * 1024]
                    if t == 0:
                        ew(vec, [ev["t_x"]])
                    ew(vec, tl["bn"][1])
                    pub(vec.bn_stats(bnst[tp][:, 0:6], xt[:, 0:512]), tl["t_bn0"])
                    pub(vec.bn_stats(bnst[tp][:, 6:12], xt[:, 512:1024]), tl["t_bn"])
                    S.wait(vec, "dv", tl["t_bn"][1])
                    ew(vec, tl["mvw"])
                    pub(vec.bn_aggr(mv[tp][:], bnst[tp][:]), tl["t_ag"])
                    ew(vec, [tl["t_rs"]] + tl["xnbw"])
                    pub(vec.tensor_scalar(xnb[tp][:], xt, mv[tp][:, 0:1],
                                          rstd[tp][:], ALU.subtract, ALU.mult),
                        tl["t_xn"])
                for q in ev["prs"]:
                    pr, ba, bb = q["pr"], q["ba"], q["bb"]
                    ww = [q["t_pa"], q["t_pb"]]
                    if pr == 0:
                        ww = ww + ev["kow"]
                    ew(vec, ww)
                    ca = cbuf[par][:, pr * 512:(pr + 1) * 512]
                    sa = sbin[par][:, pr * 512:(pr + 1) * 512]
                    vec.tensor_mul(rt[0][:], pf[ba][:, 0:512], ca)
                    vec.tensor_mul(rt[1][:], pf[bb][:, 0:512], sa)
                    pub(vec.tensor_mul(rt[2][:], pf[ba][:, 0:512], sa), q["t_ra"])
                    pub(vec.tensor_mul(rt[3][:], pf[bb][:, 0:512], ca), q["t_rb"])
                    vec.tensor_add(ko[par][:, pr * 512:(pr + 1) * 512],
                                   rt[0][:], rt[1][:])
                    pub(vec.tensor_sub(ko[par][:, (pr + 4) * 512:(pr + 5) * 512],
                                       rt[3][:], rt[2][:]), q["t_ro"])

        ew, pub, _ = mkemit(S, "ac")

        @block.scalar
        def _(sc):
            first = True
            for ev in EV:
                par = ev["par"]
                for tl in ev["tiles"]:
                    t, tp = tl["t"], tl["tp"]
                    if first:
                        ew(sc, [t_eps])
                        first = False
                    ew(sc, [tl["t_ag"]] + tl["rsw"])
                    pub(sc.activation(rstd[tp][:], mv[tp][:, 1:2], AF.Ln,
                                      bias=eps_t[:], scale=1.0), tl["t_ln"])
                    S.wait(sc, "ac", tl["t_ln"][1])
                    pub(sc.activation(rstd[tp][:], rstd[tp][:], AF.Exp,
                                      bias=0.0, scale=-0.5), tl["t_rs"])
                    ww = [tl["t_tp"]]
                    if t == 0:
                        ww = ww + ev["xntw"]
                    ew(sc, ww)
                    tb = 6 + tp
                    xv = xnt[par][:].rearrange("p (dc s) -> p dc s", s=512)
                    pub(sc.copy(xv[:, :, t * 128:(t + 1) * 128],
                                pbt[tb - 6][:].rearrange("p (dc s) -> p dc s", s=128)),
                        tl["t_ev"])
                if do_v:
                    for v in ev["vs"]:
                        vp = v["sbk"] % 2
                        ew(sc, [v["t_va"], v["t_vb"]] + v["vow"])
                        pub(sc.copy(vo[vp][:, 0:512], pf[v["ba"]][:, 0:512]),
                            v["t_c1"])
                        pub(sc.copy(vo[vp][:, 512:1024], pf[v["bb"]][:, 0:512]),
                            v["t_c2"])

        ew, pub, _ = mkemit(S, "pe")

        @block.tensor
        def _(te):
            first = True
            for ev in EV:
                par = ev["par"]
                for tl in ev["tiles"]:
                    t, tp = tl["t"], tl["tp"]
                    ww = [tl["t_xn"]] + tl["tbw"]
                    if first:
                        ww = ww + [t_ident] + t_aw
                        first = False
                    ew(te, ww)
                    tb = 6 + tp
                    for dcc in range(8):
                        ii = te.transpose(pbt[tb - 6][:, dcc * 128:(dcc + 1) * 128],
                                          xnb[tp][:, dcc * 128:(dcc + 1) * 128],
                                          ident_s[:])
                    pub(ii, tl["t_tp"])
                for q in ev["prs"]:
                    pr = q["pr"]
                    ww = q["baw"] + q["bbw"]
                    if pr == 0:
                        ww = ww + [ev["tiles"][3]["t_ev"]]
                    ew(te, ww)
                    for half, bk, tok in ((0, q["ba"], q["t_pa"]),
                                          (1, q["bb"], q["t_pb"])):
                        oc = pr + 4 * half
                        for dcc in range(8):
                            ii = te.matmul(
                                pf[bk][:, 0:512],
                                wproj[:, dcc * 1024 + oc * 128: dcc * 1024 + (oc + 1) * 128],
                                xnt[par][:, dcc * 512:(dcc + 1) * 512],
                                start=(dcc == 0), stop=(dcc == 7))
                        pub(ii, tok)
                if do_v:
                    for v in ev["vs"]:
                        ew(te, v["baw"] + v["bbw"])
                        for dcc in range(8):
                            lhs = xnt[par][:, dcc * 512 + v["sbk"] * 128:
                                           dcc * 512 + (v["sbk"] + 1) * 128]
                            i1 = te.matmul(pf[v["ba"]][:, 0:512], lhs,
                                           wvs[:, dcc * 1024: dcc * 1024 + 512],
                                           start=(dcc == 0), stop=(dcc == 7))
                            i2 = te.matmul(pf[v["bb"]][:, 0:512], lhs,
                                           wvs[:, dcc * 1024 + 512: (dcc + 1) * 1024],
                                           start=(dcc == 0), stop=(dcc == 7))
                        pub(i1, v["t_va"])
                        pub(i2, v["t_vb"])

        for i in range(2):
            for bb2 in (f"d_{tag}ko{i}", f"d_{tag}vo{i}" if do_v else None,
                        f"d_{tag}xnt{i}" if do_q else None):
                if bb2 is not None and bb2 in S.c:
                    final_waits.append((bb2, S.c[bb2]))
        return EV

    one_pass("A", 16, xb, cosK, sinK, wks, True, False, KTs, Vs, None)
    one_pass("q", 8, xq, cosQ, sinQ, wqs, False, True, QTs, None, XNTs)


def _phase_B(nc, block, S, W, ctx, RB, pf, pbt, ident_s, t_ident,
             KTs, Vs, QTs, maskd, ATTs, final_waits, a_done):
    sbt = lambda n, shp, dt: ctx.enter_context(nc.sbuf_tensor(n, shp, dt))
    masks = sbt("b_mask", [128, 4096], F32)
    qb = [sbt(f"b_q{i}", [128, 4096], BF16) for i in range(2)]
    kb = [sbt(f"b_k{i}", [128, 4096], BF16) for i in range(2)]
    vb = [sbt(f"b_v{i}", [128, 4096], BF16) for i in range(2)]
    pbuf = [sbt(f"b_p{i}", [128, 512], BF16) for i in range(2)]
    pts = [sbt(f"b_pt{i}", [128, 512], BF16) for i in range(2)]
    accS = sbt("b_acc", [128, 4096], F32)
    mst = [sbt(f"b_m{i}", [128, 4], F32) for i in range(2)]
    mtmp = sbt("b_mt", [128, 4], F32)
    lbuf = sbt("b_l", [128, 4], F32)
    ltmp = sbt("b_lt", [128, 4], F32)
    negm = sbt("b_nm", [128, 4], F32)
    dmb = sbt("b_dm", [128, 4], F32)
    asc = sbt("b_as", [128, 4], F32)
    lp = sbt("b_lp", [128, 4], F32)
    linv = sbt("b_li", [128, 4], F32)
    astg = [sbt(f"b_at{i}", [128, 1024], F32) for i in range(2)]

    pre = []
    if a_done:
        pre = [(nm, S.c[nm]) for nm in
               ("d_Ako0", "d_Ako1", "d_Avo0", "d_Avo1",
                "d_qko0", "d_qko1", "d_qxnt0", "d_qxnt1") if nm in S.c]

    RQ, RK, RV = RBuf(2), RBuf(2), RBuf(2)
    RPB = [RBuf(1) for _ in range(2)]
    RPT = [RBuf(1) for _ in range(2)]
    RAT = RBuf(2)
    lu_last = [None]
    px_last = [None]

    t_mld = W.pub("d_bm", 16)

    JEV = []
    for J in range(NJ):
        kq, wq_ = RQ.acquire()
        jev = dict(J=J, kq=kq, wq=wq_, ts=[])
        jev["t_q"] = W.pub(f"d_bq{kq % 2}", 16)
        RQ.set_prod(kq, jev["t_q"])
        for t in range(2 * J + 2):
            kk, wk_ = RK.acquire()
            kv, wv_ = RV.acquire()
            tev = dict(t=t, kk=kk, wk=wk_, kv=kv, wv=wv_,
                       diag=(t >= 2 * J), subs=[])
            tev["t_k"] = W.pub(f"d_bk{kk % 2}", 16)
            RK.set_prod(kk, tev["t_k"])
            tev["t_v"] = W.pub(f"d_bv{kv % 2}", 16)
            RV.set_prod(kv, tev["t_v"])
            for sub in range(4):
                par = (t * 4 + sub) % 2
                sbk = sub
                aa, bb2 = 4, 5
                sv = dict(sub=sub, par=par, sbk=sbk, aa=aa, bb2=bb2)
                ks, ws = RB[sbk].acquire()
                sv["ks"], sv["ksw"] = ks, ws
                t_rd = W.pub("dv")
                RB[sbk].add_cons(ks, t_rd)
                sv["t_rd"] = t_rd
                tev["subs"].append(sv)
            tev["w_ng"] = px_last[0]
            tev["t_mx"] = W.pub("dv")
            tev["t_nm"] = W.pub("dv")
            tev["t_md"] = W.pub("dv")
            for sv in tev["subs"]:
                sub, sbk = sv["sub"], sv["sbk"]
                t_mm = W.pub("pe")
                RB[sbk].set_prod(sv["ks"], t_mm)
                sv["t_mm"] = t_mm
                if sub == 3:
                    RK.add_cons(kk, t_mm)
                    if t == 2 * J + 1:
                        RQ.add_cons(kq, t_mm)
            for sv in tev["subs"]:
                sub, par, sbk = sv["sub"], sv["par"], sv["sbk"]
                sv["w_ascp"] = lu_last[0]
                t_al = W.pub("ac")
                sv["t_al"] = t_al
                kpb, wpb = RPB[par].acquire()
                sv["wpb"] = wpb + [lu_last[0]]
                t_px = W.pub("ac")
                RPB[par].set_prod(kpb, t_px)
                RB[sbk].add_cons(sv["ks"], t_px)
                sv["t_px"] = t_px
                sv["kpb"] = kpb
                if sub == 3:
                    px_last[0] = t_px
                ktb, wtb = RB[6 + par].acquire()
                sv["wtb"] = wtb
                t_tr = W.pub("pe")
                RB[6 + par].set_prod(ktb, t_tr)
                RPB[par].add_cons(kpb, t_tr)
                sv["t_tr"] = t_tr
                kpt, wpt = RPT[par].acquire()
                sv["wpt"] = wpt
                t_pc = W.pub("ac")
                RPT[par].set_prod(kpt, t_pc)
                RB[6 + par].add_cons(ktb, t_pc)
                sv["t_pc"] = t_pc
                kaa, waa = RB[sv["aa"]].acquire()
                kab, wab = RB[sv["bb2"]].acquire()
                sv["waa"], sv["wab"] = waa, wab
                sv["kaa"], sv["kab"] = kaa, kab
                t_va = W.pub("pe")
                t_vb = W.pub("pe")
                RB[sv["aa"]].set_prod(kaa, t_va)
                RB[sv["bb2"]].set_prod(kab, t_vb)
                RPT[par].add_cons(kpt, t_vb)
                if sub == 3:
                    RV.add_cons(kv, t_vb)
                sv["t_va"], sv["t_vb"] = t_va, t_vb
            for sv in tev["subs"]:
                sv["t_rs"] = W.pub("dv")
                t_ca = W.pub("dv")
                t_cb = W.pub("dv")
                RB[sv["aa"]].add_cons(sv["kaa"], t_ca)
                RB[sv["bb2"]].add_cons(sv["kab"], t_cb)
                sv["t_ca"], sv["t_cb"] = t_ca, t_cb
            tev["t_lt"] = W.pub("dv")
            tev["t_lu"] = W.pub("dv")
            lu_last[0] = tev["t_lu"]
            jev["ts"].append(tev)
        jev["evac"] = []
        jev["t_li"] = W.pub("dv")
        for sub in range(4):
            ka, wa = RAT.acquire()
            t_at = W.pub("dv")
            RAT.set_prod(ka, t_at)
            t_st = W.pub(f"d_bat{ka % 2}", 16)
            RAT.add_cons(ka, t_st)
            jev["evac"].append(dict(sub=sub, ka=ka, wa=wa,
                                    t_at=t_at, t_st=t_st))
        JEV.append(jev)

    ew, pub, pub16 = mkemit(S, None)

    @block.sync
    def _(sync):
        ew(sync, pre)
        S.inc(sync.dma_start(masks[:], maskd), "d_bm", 16)
        for jev in JEV:
            J = jev["J"]
            ew(sync, jev["wq"])
            pub16(sync.dma_start(
                qb[J % 2][:].rearrange("p (dc s) -> p dc s", s=512),
                QTs[:, :, J * 512:(J + 1) * 512].rearrange("dc p s -> p dc s")),
                jev["t_q"])
            for tev in jev["ts"]:
                t = tev["t"]
                ew(sync, tev["wk"])
                pub16(sync.dma_start(
                    kb[t % 2][:].rearrange("p (dc s) -> p dc s", s=512),
                    KTs[:, :, t * 512:(t + 1) * 512].rearrange("dc p s -> p dc s")),
                    tev["t_k"])
                ew(sync, tev["wv"])
                pub16(sync.dma_start(
                    vb[t % 2][:].rearrange("p (kc d) -> p kc d", d=1024),
                    Vs[t * 512:(t + 1) * 512, :]
                    .rearrange("(kc p) d -> p kc d", p=128)), tev["t_v"])

    ew, pub, pub16 = mkemit(S, "po")

    @block.gpsimd
    def _(g):
        for jev in JEV:
            J = jev["J"]
            for evc in jev["evac"]:
                sp = evc["sub"] % 2
                ew(g, [evc["t_at"]])
                pub16(g.dma_start(
                    ATTs[J * 512 + evc["sub"] * 128:
                         J * 512 + (evc["sub"] + 1) * 128, :],
                    astg[sp][:]), evc["t_st"])
    for i in range(2):
        if f"d_bat{i}" in S.c:
            final_waits.append((f"d_bat{i}", S.c[f"d_bat{i}"]))

    ew, pub, _ = mkemit(S, "pe")

    @block.tensor
    def _(te):
        for jev in JEV:
            J = jev["J"]
            for tev in jev["ts"]:
                t = tev["t"]
                for sv in tev["subs"]:
                    sub, sbk = sv["sub"], sv["sbk"]
                    ww = list(sv["ksw"])
                    if sub == 0:
                        ww += [tev["t_k"], tev["t_v"]]
                        if t == 0:
                            ww += [jev["t_q"]]
                    ew(te, ww)
                    for dcc in range(8):
                        ii = te.matmul(
                            pf[sbk][:, 0:512],
                            qb[J % 2][:, dcc * 512 + sub * 128: dcc * 512 + (sub + 1) * 128],
                            kb[t % 2][:, dcc * 512:(dcc + 1) * 512],
                            start=(dcc == 0), stop=(dcc == 7))
                    pub(ii, sv["t_mm"])
                for sv in tev["subs"]:
                    sub, par, sbk = sv["sub"], sv["par"], sv["sbk"]
                    ew(te, [sv["t_px"]] + sv["wtb"])
                    tb = 6 + par
                    for kc in range(4):
                        ii = te.transpose(pbt[tb - 6][:, kc * 128:(kc + 1) * 128],
                                          pbuf[par][:, kc * 128:(kc + 1) * 128],
                                          ident_s[:])
                    pub(ii, sv["t_tr"])
                    ew(te, [sv["t_pc"]] + sv["waa"] + sv["wab"])
                    for kc in range(4):
                        i1 = te.matmul(pf[sv["aa"]][:, 0:512],
                                       pts[par][:, kc * 128:(kc + 1) * 128],
                                       vb[t % 2][:, kc * 1024: kc * 1024 + 512],
                                       start=(kc == 0), stop=(kc == 3))
                        i2 = te.matmul(pf[sv["bb2"]][:, 0:512],
                                       pts[par][:, kc * 128:(kc + 1) * 128],
                                       vb[t % 2][:, kc * 1024 + 512: (kc + 1) * 1024],
                                       start=(kc == 0), stop=(kc == 3))
                    pub(i1, sv["t_va"])
                    pub(i2, sv["t_vb"])

    ew, pub, _ = mkemit(S, "dv")

    @block.vector
    def _(vec):
        ew(vec, [t_mld])
        for jev in JEV:
            J = jev["J"]
            vec.memset(accS[:], 0.0)
            vec.memset(mst[0][:], NEG)
            vec.memset(lbuf[:], 0.0)
            for tev in jev["ts"]:
                t, diag = tev["t"], tev["diag"]
                mprev, mcur = mst[t % 2], mst[(t + 1) % 2]
                for sv in tev["subs"]:
                    sub, sbk = sv["sub"], sv["sbk"]
                    ew(vec, [sv["t_mm"]])
                    if diag:
                        msl = masks[:].rearrange("p (q k) -> p q k", k=1024)[
                            :, sub, (t - 2 * J) * 512:(t - 2 * J + 1) * 512]
                        vec.tensor_add(pf[sbk][:, 0:512], pf[sbk][:, 0:512], msl)
                    pub(vec.reduce_max(mtmp[:, sub:sub + 1], pf[sbk][:, 0:512],
                                       axis=AX.X), sv["t_rd"])
                S.wait(vec, "dv", tev["subs"][3]["t_rd"][1])
                ew(vec, [tev["w_ng"]])
                pub(vec.tensor_tensor(mcur[:], mtmp[:], mprev[:], ALU.max),
                    tev["t_mx"])
                S.wait(vec, "dv", tev["t_mx"][1])
                pub(vec.tensor_scalar(negm[:], mcur[:], -1.0, None, ALU.mult),
                    tev["t_nm"])
                pub(vec.tensor_sub(dmb[:], mprev[:], mcur[:]), tev["t_md"])
                for sv in tev["subs"]:
                    sub = sv["sub"]
                    accv = accS[:, sub * 1024:(sub + 1) * 1024]
                    ew(vec, [sv["t_px"]])
                    pub(vec.tensor_scalar(accv, accv, asc[:, sub:sub + 1],
                                          None, ALU.mult), sv["t_rs"])
                    ew(vec, [sv["t_va"]])
                    pub(vec.tensor_add(accv[:, 0:512], accv[:, 0:512],
                                       pf[sv["aa"]][:, 0:512]), sv["t_ca"])
                    ew(vec, [sv["t_vb"]])
                    pub(vec.tensor_add(accv[:, 512:1024], accv[:, 512:1024],
                                       pf[sv["bb2"]][:, 0:512]), sv["t_cb"])
                pub(vec.tensor_mul(ltmp[:], lbuf[:], asc[:]), tev["t_lt"])
                S.wait(vec, "dv", tev["t_lt"][1])
                pub(vec.tensor_add(lbuf[:], ltmp[:], lp[:]), tev["t_lu"])
            S.wait(vec, "dv", jev["ts"][-1]["t_lu"][1])
            pub(vec.reciprocal(linv[:], lbuf[:]), jev["t_li"])
            S.wait(vec, "dv", jev["t_li"][1])
            for evc in jev["evac"]:
                sub = evc["sub"]
                sp = sub % 2
                ew(vec, evc["wa"])
                pub(vec.tensor_scalar(astg[sp][:],
                                      accS[:, sub * 1024:(sub + 1) * 1024],
                                      linv[:, sub:sub + 1], None, ALU.mult),
                    evc["t_at"])

    ew, pub, _ = mkemit(S, "ac")

    @block.scalar
    def _(sc):
        for jev in JEV:
            for tev in jev["ts"]:
                for sv in tev["subs"]:
                    sub, par, sbk = sv["sub"], sv["par"], sv["sbk"]
                    ew(sc, [tev["t_md"], sv["w_ascp"]])
                    pub(sc.activation(asc[:, sub:sub + 1], dmb[:, sub:sub + 1],
                                      AF.Exp), sv["t_al"])
                    ew(sc, [x for x in sv["wpb"] if x] + [sv["t_mm"]])
                    pub(sc.activation(pbuf[par][:], pf[sbk][:, 0:512], AF.Exp,
                                      bias=negm[:, sub:sub + 1], scale=1.0,
                                      accum_out=lp[:, sub:sub + 1]), sv["t_px"])
                    ew(sc, [sv["t_tr"]] + sv["wpt"])
                    tb = 6 + par
                    pub(sc.copy(pts[par][:], pbt[tb - 6][:, 0:512]), sv["t_pc"])


def _phase_C(nc, block, S, W, ctx, RB, pf, Wgf, t_wc,
             W1b, W2b, W3b, XNTs, ATTs, xq, OUT, final_waits, prev,
             dbgH=None, dbgA=None, dbgW=None, dbgG=None):
    sbt = lambda n, shp, dt: ctx.enter_context(nc.sbuf_tensor(n, shp, dt))
    NSLOT = 5
    slots = [sbt(f"c_w{i}", [128, 8192], BF16) for i in range(NSLOT)]
    xcb = [sbt(f"c_x{i}", [128, 4096], BF16) for i in range(2)]
    hT = sbt("c_h", [128, 8192], BF16)
    sg = [sbt(f"c_sg{i}", [128, 512], F32) for i in range(2)]
    accF = sbt("c_acc", [128, 4096], F32)
    wgs = sbt("c_wg", [128, 64], BF16)
    wbuf = sbt("c_wb", [128, 256], F32)
    gm1 = sbt("c_g1", [128, 4], F32)
    gm2 = sbt("c_g2", [128, 4], F32)
    gd1 = sbt("c_gd1", [128, 4], F32)
    gd2 = sbt("c_gd2", [128, 4], F32)
    ge = sbt("c_ge", [128, 8], F32)
    gp = sbt("c_gp", [128, 8], F32)
    geq1 = sbt("c_geq1", [128, 32], F32)
    gmask = sbt("c_gmask", [128, 32], F32)
    geq2 = sbt("c_geq2", [128, 32], F32)
    gta = sbt("c_gta", [128, 32], F32)
    cmbx = sbt("c_cx", [128, 4096], F32)
    cmba = sbt("c_ca", [128, 4096], F32)
    ostg = sbt("c_o", [128, 4096], F32)

    pre = []
    if prev:
        pre = [(nm, S.c[nm]) for nm in
               ("d_qxnt0", "d_qxnt1", "d_bat0", "d_bat1") if nm in S.c]

    RW = RBuf(NSLOT)
    RXC = RBuf(2)
    RSG = [RBuf(1) for _ in range(2)]
    RH = RBuf(1)
    RCMB = RBuf(1)
    ROS = RBuf(1)

    t_wg = W.pub("d_cwg", 16)

    GEV = []
    for chunk in range(NCHUNK):
        kx, wx = RXC.acquire()
        g = dict(chunk=chunk, kx=kx, wx=wx)
        g["t_x"] = W.pub(f"d_cx{kx % 2}", 16)
        RXC.set_prod(kx, g["t_x"])
        kb0, wb0 = RB[0].acquire()
        g["wb0"] = wb0
        t_g = W.pub("pe")
        RB[0].set_prod(kb0, t_g)
        RXC.add_cons(kx, t_g)
        g["t_g"] = t_g
        g["t_ir"] = []
        for sub in range(4):
            t1 = W.pub("dv")
            te1 = W.pub("dv")
            tm = W.pub("dv")
            t2 = W.pub("dv")
            g["t_ir"].append((t1, te1, tm, t2))
        t_gd1 = W.pub("dv")
        t_gd = W.pub("dv")
        RB[0].add_cons(kb0, t_gd)
        g["t_gd1"] = t_gd1
        g["t_gd"] = t_gd
        g["t_ges"] = [W.pub("ac") for _ in range(8)]
        g["t_ge"] = g["t_ges"][-1]
        g["t_ga"] = [(W.pub("dv"), W.pub("dv"), W.pub("dv"), W.pub("dv"))
                     for _ in range(4)]
        g["t_gp"] = g["t_ga"][-1][-1]
        g["t_gq"] = [(W.pub("dv"), W.pub("dv")) for _ in range(4)]
        t_gw = W.pub("dv")
        g["t_gw"] = t_gw
        GEV.append(g)
    CEV = []
    for chunk in range(NCHUNK):
        kx, wx = RXC.acquire()
        cev = dict(chunk=chunk, kx=kx, wx=wx, es=[])
        cev["t_x"] = W.pub(f"d_cx{kx % 2}", 16)
        RXC.set_prod(kx, cev["t_x"])
        kc, wc = RCMB.acquire()
        cev["wc"] = wc
        W.pub("d_cc", 16)
        cev["t_c"] = W.pub("d_cc", 16)
        RCMB.set_prod(kc, cev["t_c"])
        for e in range(E):
            eev = dict(e=e, wl=[], uv=[], ys=[])
            kh, wh = RH.acquire()
            eev["wh"] = wh
            items = [("W1", 0), ("W2", 0), ("W1", 1), ("W2", 1),
                     ("W3", 0), ("W3", 1)]
            slot_of = {}
            for wn, half in items:
                kw, wwl = RW.acquire()
                t_l = W.pub(f"d_cw{kw % NSLOT}", 16)
                RW.set_prod(kw, t_l)
                eev["wl"].append(dict(wn=wn, half=half, slot=kw % NSLOT,
                                      kw=kw, w=wwl, t=t_l))
                slot_of[(wn, half)] = (kw % NSLOT, kw, t_l)
            for half in range(2):
                for hc8 in range(8):
                    hc = half * 8 + hc8
                    uv = dict(half=half, hc8=hc8, hc=hc,
                              w1=slot_of[("W1", half)], w2=slot_of[("W2", half)])
                    kU, wU = RB[hc % 2].acquire()
                    kV, wV = RB[2 + hc % 2].acquire()
                    uv["wU"], uv["wV"] = wU, wV
                    t_u = W.pub("pe")
                    t_v = W.pub("pe")
                    RB[hc % 2].set_prod(kU, t_u)
                    RB[2 + hc % 2].set_prod(kV, t_v)
                    uv["t_u"], uv["t_v"] = t_u, t_v
                    if hc8 == 7:
                        RW.add_cons(slot_of[("W1", half)][1], t_u)
                        RW.add_cons(slot_of[("W2", half)][1], t_v)
                        if half == 1 and e == E - 1:
                            RXC.add_cons(kx, t_v)
                    ksg, wsg = RSG[hc % 2].acquire()
                    uv["wsg"] = wsg
                    t_s = W.pub("ac")
                    RSG[hc % 2].set_prod(ksg, t_s)
                    RB[hc % 2].add_cons(kU, t_s)
                    uv["t_s"] = t_s
                    t_h = W.pub("dv")
                    RSG[hc % 2].add_cons(ksg, t_h)
                    RB[2 + hc % 2].add_cons(kV, t_h)
                    uv["t_h"] = t_h
                    if hc == 15:
                        RH.set_prod(kh, t_h)
                    eev["uv"].append(uv)
            ybanks = [(4, 5), (0, 1), (2, 3), (4, 5)]
            for sbk in range(4):
                ba, bb2 = ybanks[sbk]
                y = dict(sbk=sbk, ba=ba, bb2=bb2,
                         w30=slot_of[("W3", 0)], w31=slot_of[("W3", 1)])
                ka, wa = RB[ba].acquire()
                kb2, wb2 = RB[bb2].acquire()
                y["wa"], y["wb"] = wa, wb2
                t_ya = W.pub("pe")
                t_yb = W.pub("pe")
                RB[ba].set_prod(ka, t_ya)
                RB[bb2].set_prod(kb2, t_yb)
                y["t_ya"], y["t_yb"] = t_ya, t_yb
                if sbk == 3:
                    RW.add_cons(slot_of[("W3", 0)][1], t_yb)
                    RW.add_cons(slot_of[("W3", 1)][1], t_yb)
                    RH.add_cons(kh, t_yb)
                t_ea = W.pub("dv")
                t_eb = W.pub("dv")
                RB[ba].add_cons(ka, t_ea)
                RB[bb2].add_cons(kb2, t_eb)
                y["t_ea"], y["t_eb"] = t_ea, t_eb
                eev["ys"].append(y)
            if dbgH is not None:
                eev["t_dbgh"] = W.pub("d_dbgh", 16)
                RH.add_cons(kh, eev["t_dbgh"])
                eev["t_dbga"] = W.pub("d_dbga", 16)
            cev["es"].append(eev)
        ko_, wo = ROS.acquire()
        cev["wo"] = wo
        t_o = W.pub("dv")
        ROS.set_prod(ko_, t_o)
        RCMB.add_cons(kc, t_o)
        cev["t_o"] = t_o
        t_os = W.pub("d_co", 16)
        ROS.add_cons(ko_, t_os)
        cev["t_os"] = t_os
        CEV.append(cev)

    ew, pub, pub16 = mkemit(S, "po")

    @block.gpsimd
    def _(g):
        pub16(g.dma_start(wgs[:], Wgf), t_wg)
        if dbgW is not None:
            ew(g, [GEV[0]["t_gw"]])
            for nm, tns in (("dbgm1", gm1), ("dbgm2", gm2), ("dbgd1", gd1),
                            ("dbgd2", gd2), ("dbge", ge), ("dbgp", gp),
                            ("dbgq1", geq1), ("dbgmk", gmask)):
                S.inc(g.dma_start(dbgG[nm], tns[:]), "d_dbgw", 16)
            ew(g, [GEV[-1]["t_gw"]])
            vv = S.inc(g.dma_start(dbgW, wbuf[:]), "d_dbgw", 16)
            final_waits.append(("d_dbgw", vv))
        if dbgH is not None:
            for cev in CEV:
                for eev in cev["es"]:
                    idx = cev["chunk"] * E + eev["e"]
                    ew(g, [eev["uv"][15]["t_h"]])
                    pub16(g.dma_start(dbgH[idx], hT[:]), eev["t_dbgh"])
                    ew(g, [eev["ys"][3]["t_eb"]])
                    pub16(g.dma_start(dbgA[idx], accF[:]), eev["t_dbga"])
        for cev in CEV:
            ew(g, [cev["t_o"]])
            pub16(g.dma_start(OUT[cev["chunk"]], ostg[:]), cev["t_os"])
    final_waits.append(("d_co", S.c["d_co"]))
    if dbgH is not None:
        final_waits.append(("d_dbgh", S.c["d_dbgh"]))
        final_waits.append(("d_dbga", S.c["d_dbga"]))

    ew, pub, pub16 = mkemit(S, None)

    @block.sync
    def _(sync):
        if t_wc is not None:
            ew(sync, list(t_wc) + pre)

        def ld_xcb(chunk, kx, tok, waits):
            ew(sync, waits)
            pub16(sync.dma_start(
                xcb[kx % 2][:].rearrange("p (dc s) -> p dc s", s=512),
                XNTs[:, :, chunk * 512:(chunk + 1) * 512]
                .rearrange("dc p s -> p dc s")), tok)

        for g in GEV:
            ld_xcb(g["chunk"], g["kx"], g["t_x"], g["wx"])
        for cev in CEV:
            chunk = cev["chunk"]
            ld_xcb(chunk, cev["kx"], cev["t_x"], cev["wx"])
            ew(sync, cev["wc"])
            S.inc(sync.dma_start(cmbx[:], xq[chunk]), "d_cc", 16)
            ii = sync.dma_start(
                cmba[:].rearrange("p (sb d) -> p sb d", d=1024),
                ATTs[chunk * 512:(chunk + 1) * 512, :]
                .rearrange("(sb p) d -> p sb d", p=128))
            pub16(ii, cev["t_c"])
            for eev in cev["es"]:
                for wl in eev["wl"]:
                    ew(sync, wl["w"])
                    slot = wl["slot"]
                    if wl["wn"] == "W1":
                        ii = sync.dma_start(slots[slot][:], W1b[eev["e"], wl["half"]])
                    elif wl["wn"] == "W2":
                        ii = sync.dma_start(slots[slot][:], W2b[eev["e"], wl["half"]])
                    else:
                        ii = sync.dma_start(
                            slots[slot][:].rearrange("p (hc d) -> p hc d", d=1024),
                            W3b[eev["e"], wl["half"] * 8:(wl["half"] + 1) * 8]
                            .rearrange("hc p d -> p hc d"))
                    pub16(ii, wl["t"])

    ew, pub, _ = mkemit(S, "pe")

    @block.tensor
    def _(te):
        for g in GEV:
            chunk = g["chunk"]
            ww = [g["t_x"], t_wg] + g["wb0"]
            ew(te, ww)
            for sbk in range(4):
                for dcc in range(8):
                    ii = te.matmul(
                        pf[0][:, sbk * 8:(sbk + 1) * 8],
                        xcb[g["kx"] % 2][:, dcc * 512 + sbk * 128:
                                         dcc * 512 + (sbk + 1) * 128],
                        wgs[:, dcc * 8:(dcc + 1) * 8],
                        start=(dcc == 0), stop=(dcc == 7))
            pub(ii, g["t_g"])
        for cev in CEV:
            chunk = cev["chunk"]
            for eev in cev["es"]:
                for uv in eev["uv"]:
                    hc, hc8, half = uv["hc"], uv["hc8"], uv["half"]
                    s1 = uv["w1"][0]
                    s2 = uv["w2"][0]
                    ww = list(uv["wU"])
                    if hc8 == 0:
                        ww += [uv["w1"][2], uv["w2"][2]]
                        if hc == 0 and eev["e"] == 0:
                            ww += [cev["t_x"]]
                    ew(te, ww)
                    for dcc in range(8):
                        ii = te.matmul(
                            pf[hc % 2][:, 0:512],
                            slots[s1][:, dcc * 1024 + hc8 * 128:
                                      dcc * 1024 + (hc8 + 1) * 128],
                            xcb[cev["kx"] % 2][:, dcc * 512:(dcc + 1) * 512],
                            start=(dcc == 0), stop=(dcc == 7))
                    pub(ii, uv["t_u"])
                    ew(te, uv["wV"])
                    for dcc in range(8):
                        ii = te.matmul(
                            pf[2 + hc % 2][:, 0:512],
                            slots[s2][:, dcc * 1024 + hc8 * 128:
                                      dcc * 1024 + (hc8 + 1) * 128],
                            xcb[cev["kx"] % 2][:, dcc * 512:(dcc + 1) * 512],
                            start=(dcc == 0), stop=(dcc == 7))
                    pub(ii, uv["t_v"])
                s30 = eev["ys"][0]["w30"][0]
                s31 = eev["ys"][0]["w31"][0]
                for i, y in enumerate(eev["ys"]):
                    sbk = y["sbk"]
                    ww = list(y["wa"]) + list(y["wb"])
                    if i == 0:
                        ww += [y["w30"][2], y["w31"][2],
                               eev["uv"][15]["t_h"]]
                    ew(te, ww)
                    for hc in range(16):
                        half, hc8 = hc // 8, hc % 8
                        w3s = s30 if half == 0 else s31
                        lhs = hT[:, hc * 512 + sbk * 128: hc * 512 + (sbk + 1) * 128]
                        i1 = te.matmul(pf[y["ba"]][:, 0:512], lhs,
                                       slots[w3s][:, hc8 * 1024: hc8 * 1024 + 512],
                                       start=(hc == 0), stop=(hc == 15))
                        i2 = te.matmul(pf[y["bb2"]][:, 0:512], lhs,
                                       slots[w3s][:, hc8 * 1024 + 512: (hc8 + 1) * 1024],
                                       start=(hc == 0), stop=(hc == 15))
                    pub(i1, y["t_ya"])
                    pub(i2, y["t_yb"])

    ew, pub, _ = mkemit(S, "ac")

    @block.scalar
    def _(sc):
        for g in GEV:
            ew(sc, [g["t_gd"]])
            for sub in range(4):
                pub(sc.activation(ge[:, sub:sub + 1], gd1[:, sub:sub + 1],
                                  AF.Exp), g["t_ges"][2 * sub])
                pub(sc.activation(ge[:, 4 + sub:5 + sub], gd2[:, sub:sub + 1],
                                  AF.Exp), g["t_ges"][2 * sub + 1])
        for cev in CEV:
            for eev in cev["es"]:
                for uv in eev["uv"]:
                    hc = uv["hc"]
                    ew(sc, [uv["t_u"]] + uv["wsg"])
                    pub(sc.activation(sg[hc % 2][:], pf[hc % 2][:, 0:512],
                                      AF.Silu), uv["t_s"])

    ew, pub, _ = mkemit(S, "dv")

    @block.vector
    def _(vec):
        for g in GEV:
            chunk = g["chunk"]
            ew(vec, [g["t_g"]])
            for sub in range(4):
                lg = pf[0][:, sub * 8:(sub + 1) * 8]
                t1, te1, tm, t2 = g["t_ir"][sub]
                pub(vec.reduce_max(gm1[:, sub:sub + 1], lg, axis=AX.X), t1)
                S.wait(vec, "dv", t1[1])
                pub(vec.tensor_scalar(geq1[:, sub * 8:(sub + 1) * 8], lg,
                                      gm1[:, sub:sub + 1], None, ALU.is_equal), te1)
                S.wait(vec, "dv", te1[1])
                pub(vec.scalar_tensor_tensor(gmask[:, sub * 8:(sub + 1) * 8],
                                             geq1[:, sub * 8:(sub + 1) * 8], NEG, lg,
                                             ALU.mult, ALU.add), tm)
                S.wait(vec, "dv", tm[1])
                pub(vec.reduce_max(gm2[:, sub:sub + 1],
                                   gmask[:, sub * 8:(sub + 1) * 8], axis=AX.X), t2)
                S.wait(vec, "dv", t2[1])
            pub(vec.tensor_sub(gd1[:], gm2[:], gm1[:]), g["t_gd1"])
            pub(vec.tensor_sub(gd2[:], gm1[:], gm2[:]), g["t_gd"])
            ew(vec, [g["t_ge"]])
            for sub in range(4):
                ta1, ta2, tr1, tr2 = g["t_ga"][sub]
                pub(vec.tensor_scalar(gp[:, sub:sub + 1], ge[:, sub:sub + 1],
                                      1.0, None, ALU.add), ta1)
                pub(vec.tensor_scalar(gp[:, 4 + sub:5 + sub],
                                      ge[:, 4 + sub:5 + sub], 1.0, None, ALU.add), ta2)
                S.wait(vec, "dv", ta2[1])
                pub(vec.reciprocal(gp[:, sub:sub + 1], gp[:, sub:sub + 1]), tr1)
                pub(vec.reciprocal(gp[:, 4 + sub:5 + sub], gp[:, 4 + sub:5 + sub]),
                    tr2)
            S.wait(vec, "dv", g["t_gp"][1])
            for sub in range(4):
                sl = slice(sub * 8, (sub + 1) * 8)
                tta, tq = g["t_gq"][sub]
                pub(vec.tensor_scalar(gta[:, sl], geq1[:, sl], gp[:, sub:sub + 1],
                                      None, ALU.mult), tta)
                pub(vec.tensor_scalar(geq2[:, sl], gmask[:, sl], gm2[:, sub:sub + 1],
                                      None, ALU.is_equal), tq)
                S.wait(vec, "dv", tq[1])
                ii = vec.scalar_tensor_tensor(
                    wbuf[:, chunk * 32 + sub * 8: chunk * 32 + (sub + 1) * 8],
                    geq2[:, sl], gp[:, 4 + sub:5 + sub], gta[:, sl],
                    ALU.mult, ALU.add)
            pub(ii, g["t_gw"])
        for cev in CEV:
            chunk = cev["chunk"]
            vec.memset(accF[:], 0.0)
            for eev in cev["es"]:
                e = eev["e"]
                for uv in eev["uv"]:
                    hc = uv["hc"]
                    ww = [uv["t_s"], uv["t_v"]]
                    if hc == 0:
                        ww = ww + eev["wh"]
                    ew(vec, ww)
                    pub(vec.tensor_mul(hT[:, hc * 512:(hc + 1) * 512],
                                       sg[hc % 2][:], pf[2 + hc % 2][:, 0:512]),
                        uv["t_h"])
                if dbgA is not None and (e > 0 or chunk > 0):
                    prev_idx = chunk * E + e - 1 if e > 0 else (chunk - 1) * E + E - 1
                    S.wait(vec, "d_dbga", 16 * (prev_idx + 1))
                for y in eev["ys"]:
                    sbk = y["sbk"]
                    wcol = wbuf[:, chunk * 32 + sbk * 8 + e:
                                chunk * 32 + sbk * 8 + e + 1]
                    accv = accF[:, sbk * 1024:(sbk + 1) * 1024]
                    ew(vec, [y["t_ya"]])
                    pub(vec.scalar_tensor_tensor(accv[:, 0:512], pf[y["ba"]][:, 0:512],
                                                 wcol, accv[:, 0:512],
                                                 ALU.mult, ALU.add), y["t_ea"])
                    ew(vec, [y["t_yb"]])
                    pub(vec.scalar_tensor_tensor(accv[:, 512:1024],
                                                 pf[y["bb2"]][:, 0:512],
                                                 wcol, accv[:, 512:1024],
                                                 ALU.mult, ALU.add), y["t_eb"])
            ew(vec, [cev["t_c"]] + cev["wo"])
            vec.tensor_add(ostg[:], cmbx[:], cmba[:])
            pub(vec.tensor_add(ostg[:], ostg[:], accF[:]), cev["t_o"])


def kernel(L0_attn_x, L0_attn_wq, L0_attn_wk, L0_attn_wv,
           L0_ffn_Wg, L0_ffn_W1, L0_ffn_W2, L0_ffn_W3):
    per_core, row_idx = host_layouts(
        L0_attn_x, L0_attn_wq, L0_attn_wk, L0_attn_wv,
        L0_ffn_Wg, L0_ffn_W1, L0_ffn_W2, L0_ffn_W3)
    nc = build_program()
    in_maps = [dict(per_core[c]) for c in range(NCORE)]
    res = run_bass_kernel_spmd(nc, in_maps, core_ids=list(range(NCORE)))
    out = np.empty((BATCH, SEQ, D), dtype=np.float32)
    for c in range(NCORE):
        oc = np.asarray(res.results[c]["out"])
        oc = oc.reshape(NCHUNK, 128, 4, D).transpose(0, 2, 1, 3).reshape(QROWS, D)
        out[c // 2][row_idx[c]] = oc
    return out


# version 04
# speedup vs baseline: 1.6416x; 1.5168x over previous
import numpy as np
import ml_dtypes
import concourse.bass as bass
import concourse.mybir as mybir
from contextlib import ExitStack
from concourse.library_config import mlp as MLP_LIB
from concourse.bass_utils import run_bass_kernel_spmd

F32 = mybir.dt.float32
BF16 = mybir.dt.bfloat16
I32 = mybir.dt.int32
I16 = mybir.dt.int16
AF = mybir.ActivationFunctionType
ALU = mybir.AluOpType
AX = mybir.AxisListType

D = 1024
E = 8
SEQ = 8192
BATCH = 4
NCORE = 8
QROWS = 4096
NJ = 8
NCHUNK = 8
DC = 8
NEG = -1.0e30
EPS = 1e-5
SCL = 1.0 / 32.0
DEBUG_C = False
C2_STAGES = ("G", "F", "M")
ROUTED = True
CCAP = 1536
NB_T = 32
ITI_ROWS = 8 * 1536 + 4096
YTRASH = 8 * 1536


class SemBook:
    def __init__(self, nc, stack):
        self.nc, self.stack = nc, stack
        self.h, self.c = {}, {}

    def sem(self, name):
        if name not in self.h:
            self.h[name] = self.stack.enter_context(self.nc.semaphore(name))
            self.c[name] = 0
        return self.h[name]

    def inc(self, instr, name, n=1):
        instr.then_inc(self.sem(name), n)
        self.c[name] += n
        return self.c[name]

    def wait(self, eng, name, val):
        eng.wait_ge(self.sem(name), val)


class Walk:

    def __init__(self):
        self.cnt = {}

    def pub(self, sem, n=1):
        v = self.cnt.get(sem, 0) + n
        self.cnt[sem] = v
        return (sem, v)


class RWait:

    def __init__(self, buf, k):
        self.buf, self.k = buf, k

    def resolve(self):
        b, k = self.buf, self.k
        if k < b.nbuf:
            return []
        u = b.uses[k - b.nbuf]
        out = []
        if u["prod"] is not None:
            out.append(u["prod"])
        out.extend(u["cons"])
        return out


class RBuf:

    def __init__(self, nbuf=1):
        self.nbuf = nbuf
        self.uses = []

    def acquire(self):
        k = len(self.uses)
        self.uses.append({"prod": None, "cons": []})
        return k, [RWait(self, k)]

    def set_prod(self, k, tok):
        self.uses[k]["prod"] = tok

    def add_cons(self, k, tok):
        self.uses[k]["cons"].append(tok)


def mkemit(S, own):
    def emitw(eng, waits):
        best = {}
        flat = []
        for t in waits:
            if t is None:
                continue
            if isinstance(t, RWait):
                flat.extend(t.resolve())
            else:
                flat.append(t)
        for sem, v in flat:
            if sem == own:
                continue
            if best.get(sem, 0) < v:
                best[sem] = v
        for sem, v in best.items():
            S.wait(eng, sem, v)

    def pub(instr, tok):
        got = S.inc(instr, tok[0], 1)
        assert got == tok[1], (tok, got)

    def pub16(instr, tok):
        got = S.inc(instr, tok[0], 16)
        assert got == tok[1], (tok, got)
    return emitw, pub, pub16


def _rope_tables_np(m, d):
    i = np.arange(d // 2, dtype=np.float32)
    phi = np.power(np.float32(10000.0),
                   np.float32(-2.0) * (i - np.float32(1.0)) / np.float32(d))
    pos = np.arange(m, dtype=np.float32)
    ang = pos[:, None] * phi[None, :]
    return np.cos(ang, dtype=np.float32), np.sin(ang, dtype=np.float32)


def host_layouts(L0_attn_x, L0_attn_wq, L0_attn_wk, L0_attn_wv,
                 L0_ffn_Wg, L0_ffn_W1, L0_ffn_W2, L0_ffn_W3):
    x = np.ascontiguousarray(np.asarray(L0_attn_x, dtype=np.float32))
    wq = np.asarray(L0_attn_wq, dtype=np.float32)
    wk = np.asarray(L0_attn_wk, dtype=np.float32)
    wv = np.asarray(L0_attn_wv, dtype=np.float32)
    Wg = np.asarray(L0_ffn_Wg, dtype=np.float32)
    W1 = np.asarray(L0_ffn_W1, dtype=np.float32)
    W2 = np.asarray(L0_ffn_W2, dtype=np.float32)
    W3 = np.asarray(L0_ffn_W3, dtype=np.float32)

    perm = np.concatenate([np.arange(0, D, 2), np.arange(1, D, 2)])

    def wT_lay(w, permute):
        wt = (w[perm] if permute else w).T
        return np.ascontiguousarray(
            wt.reshape(DC, 128, D).transpose(1, 0, 2).reshape(128, DC * D))

    wqT, wkT, wvT = wT_lay(wq, True), wT_lay(wk, True), wT_lay(wv, False)

    W1h = np.ascontiguousarray(
        W1.reshape(E, DC, 128, 2, 1024).transpose(0, 3, 2, 1, 4).reshape(E, 2, 128, 8192))
    W2h = np.ascontiguousarray(
        W2.reshape(E, DC, 128, 2, 1024).transpose(0, 3, 2, 1, 4).reshape(E, 2, 128, 8192))
    W3h = np.ascontiguousarray(W3.reshape(E, 16, 128, D))
    Wgh = np.ascontiguousarray(
        Wg.reshape(DC, 128, E).transpose(1, 0, 2).reshape(128, DC * E))

    def rows_lay(a):
        n = a.shape[0] // 512
        return np.ascontiguousarray(
            a.reshape(n, 4, 128, D).transpose(0, 2, 1, 3).reshape(n, 128, 4 * D))

    cosT, sinT = _rope_tables_np(SEQ, D)
    cosK = np.ascontiguousarray(cosT.T.reshape(4, 128, SEQ))
    sinK = np.ascontiguousarray(sinT.T.reshape(4, 128, SEQ))
    ident = np.eye(128, dtype=ml_dtypes.bfloat16)
    ident32 = np.eye(128, dtype=np.float32)

    per_core, row_idx = [], {}
    for c in range(NCORE):
        b, h = c // 2, c % 2
        rows = np.concatenate(
            [np.arange(J * 1024 + h * 512, J * 1024 + h * 512 + 512) for J in range(NJ)])
        row_idx[c] = rows
        xqc = np.ascontiguousarray(x[b][rows])
        cosQ = np.ascontiguousarray(cosK[:, :, rows] * np.float32(SCL))
        sinQ = np.ascontiguousarray(sinK[:, :, rows] * np.float32(SCL))
        qpos = (h * 512 + np.arange(512))[:, None]
        kpos = np.arange(1024)[None, :]
        mask = np.where(kpos <= qpos, np.float32(0.0), np.float32(NEG))
        mask = np.ascontiguousarray(
            mask.reshape(4, 128, 1024).transpose(1, 0, 2).reshape(128, 4096))
        per_core.append(dict(
            xb=rows_lay(x[b]), xq=rows_lay(xqc), cosQ=cosQ, sinQ=sinQ, mask=mask,
            wqT=wqT, wkT=wkT, wvT=wvT, cosK=cosK, sinK=sinK,
            W1h=W1h, W2h=W2h, W3h=W3h, Wgh=Wgh, ident=ident,
            ident32=ident32))
    return per_core, row_idx


def build_program(phases=("A", "B", "C"), taps=()):
    nc = bass.Bass("TRN2", target_bir_lowering=False)
    final_waits = []

    def scratch(name, shape, dtype, produced_by):
        if produced_by not in phases:
            kind = "ExternalInput"
        elif name in taps:
            kind = "ExternalOutput"
        else:
            kind = "Internal"
        return nc.dram_tensor(name, shape, dtype, kind=kind).ap()

    xb = nc.dram_tensor("xb", [16, 128, 4096], F32, kind="ExternalInput").ap()
    xq = nc.dram_tensor("xq", [8, 128, 4096], F32, kind="ExternalInput").ap()
    wqT = nc.dram_tensor("wqT", [128, 8192], F32, kind="ExternalInput").ap()
    wkT = nc.dram_tensor("wkT", [128, 8192], F32, kind="ExternalInput").ap()
    wvT = nc.dram_tensor("wvT", [128, 8192], F32, kind="ExternalInput").ap()
    cosK = nc.dram_tensor("cosK", [4, 128, SEQ], F32, kind="ExternalInput").ap()
    sinK = nc.dram_tensor("sinK", [4, 128, SEQ], F32, kind="ExternalInput").ap()
    cosQ = nc.dram_tensor("cosQ", [4, 128, QROWS], F32, kind="ExternalInput").ap()
    sinQ = nc.dram_tensor("sinQ", [4, 128, QROWS], F32, kind="ExternalInput").ap()
    maskd = nc.dram_tensor("mask", [128, 4096], F32, kind="ExternalInput").ap()
    W1f = nc.dram_tensor("W1h", [E, 2, 128, 8192], F32, kind="ExternalInput").ap()
    W2f = nc.dram_tensor("W2h", [E, 2, 128, 8192], F32, kind="ExternalInput").ap()
    W3f = nc.dram_tensor("W3h", [E, 16, 128, D], F32, kind="ExternalInput").ap()
    Wgf = nc.dram_tensor("Wgh", [128, DC * E], F32, kind="ExternalInput").ap()
    ident_d = nc.dram_tensor("ident", [128, 128], BF16, kind="ExternalInput").ap()

    KTs = scratch("KTs", [DC, 128, SEQ], BF16, "A")
    Vs = scratch("Vs", [SEQ, D], BF16, "A")
    QTs = scratch("QTs", [DC, 128, QROWS], BF16, "A")
    XNTs = scratch("XNTs", [DC, 128, QROWS], BF16, "A")
    ATTs = scratch("ATTs", [QROWS, D], F32, "B")
    W1b = nc.dram_tensor("W1b", [E, 2, 128, 8192], BF16, kind="Internal").ap()
    W2b = nc.dram_tensor("W2b", [E, 2, 128, 8192], BF16, kind="Internal").ap()
    W3b = nc.dram_tensor("W3b", [E, 16, 128, D], BF16, kind="Internal").ap()
    if ROUTED:
        XNS = scratch("XNS", [32, 128, D], BF16, "A")
        ITI = nc.dram_tensor("ITI", [ITI_ROWS, 64], F32, kind="Internal").ap()
        YALL = nc.dram_tensor("YALL", [E * CCAP + 128, D], F32, kind="Internal").ap()
    ident32_d = nc.dram_tensor("ident32", [128, 128], F32, kind="ExternalInput").ap()
    OUT = nc.dram_tensor("out", [NCHUNK, 128, 4096], F32, kind="ExternalOutput").ap()
    dbgH = dbgA = dbgW = None
    dbgG = None
    if DEBUG_C:
        dbgH = nc.dram_tensor("dbgH", [NCHUNK * E, 128, 8192], BF16,
                              kind="ExternalOutput").ap()
        dbgA = nc.dram_tensor("dbgA", [NCHUNK * E, 128, 4096], F32,
                              kind="ExternalOutput").ap()
        dbgW = nc.dram_tensor("dbgW", [128, 256], F32,
                              kind="ExternalOutput").ap()
        dbgG = {nm: nc.dram_tensor(nm, [128, wdt], F32, kind="ExternalOutput").ap()
                for nm, wdt in (("dbgm1", 4), ("dbgm2", 4), ("dbgd1", 4),
                                ("dbgd2", 4), ("dbge", 8), ("dbgp", 8),
                                ("dbgq1", 32), ("dbgmk", 32))}

    with ExitStack() as top:
        S = SemBook(nc, top)
        W = Walk()
        for s in ("pe", "dv", "ac", "po"):
            S.sem(s)
        block = top.enter_context(nc.Block())
        pf = [top.enter_context(nc.psum_tensor(f"pf{i}", [128, 512], F32))
              for i in range(6)]
        pbt = [top.enter_context(nc.psum_tensor(f"pbt{i}", [128, 1024], BF16))
               for i in range(2)]
        RB = [RBuf(1) for _ in range(8)]
        ident_s = top.enter_context(nc.sbuf_tensor("ident_s", [128, 128], BF16))

        ident32_s = top.enter_context(nc.sbuf_tensor("ident32_s", [128, 128], F32))

        t_ident = W.pub("d_id", 16)
        t_id32 = W.pub("d_id", 16)
        t_wc = None
        if "C" in phases:
            t_wc = [W.pub("d_wc0", 16), W.pub("d_wc1", 16), W.pub("d_wc2", 16)]

        @block.gpsimd
        def _(g):
            S.inc(g.dma_start(ident_s[:], ident_d), "d_id", 16)
            S.inc(g.dma_start(ident32_s[:], ident32_d), "d_id", 16)
            if "C" in phases:
                S.inc(g.dma_start(W1b, W1f), "d_wc0", 16)
                S.inc(g.dma_start(W2b, W2f), "d_wc1", 16)
                S.inc(g.dma_start(W3b, W3f), "d_wc2", 16)

        if "A" in phases:
            with ExitStack() as ctx:
                _phase_A(nc, block, S, W, ctx, RB, pf, pbt, ident_s, t_ident,
                         xb, xq, wqT, wkT, wvT, cosK, sinK, cosQ, sinQ,
                         KTs, Vs, QTs, XNTs, final_waits,
                         XNS=(XNS if ROUTED else None))
        if "B" in phases:
            with ExitStack() as ctx:
                _phase_B(nc, block, S, W, ctx, RB, pf, pbt, ident_s, t_ident,
                         KTs, Vs, QTs, maskd, ATTs, final_waits,
                         a_done=("A" in phases))
        if "C" in phases:
            if ROUTED:
                with ExitStack() as ctx:
                    _phase_C2(nc, block, S, W, ctx, RB, pf, pbt, ident_s,
                              ident32_s, t_ident, t_id32, Wgf, W1b, W2b, W3b,
                              t_wc, XNTs, XNS, ITI, YALL, ATTs, xq, OUT,
                              final_waits, prev=("B" in phases))
            else:
                with ExitStack() as ctx:
                    _phase_C(nc, block, S, W, ctx, RB, pf, Wgf, t_wc,
                             W1b, W2b, W3b, XNTs, ATTs, xq, OUT, final_waits,
                             prev=("B" in phases), dbgH=dbgH, dbgA=dbgA,
                             dbgW=dbgW, dbgG=dbgG)

        @block.sync
        def _(sync):
            for nm, val in final_waits:
                S.wait(sync, nm, val)
    mybir.codegen_inst_isa_subclasses(nc)
    return nc


def _phase_A(nc, block, S, W, ctx, RB, pf, pbt, ident_s, t_ident,
             xb, xq, wqT, wkT, wvT, cosK, sinK, cosQ, sinQ,
             KTs, Vs, QTs, XNTs, final_waits, XNS=None):
    sbt = lambda n, shp, dt: ctx.enter_context(nc.sbuf_tensor(n, shp, dt))
    xbuf = [sbt(f"a_x{i}", [128, 4096], F32) for i in range(2)]
    cbuf = [sbt(f"a_c{i}", [128, 2048], F32) for i in range(2)]
    sbin = [sbt(f"a_s{i}", [128, 2048], F32) for i in range(2)]
    xnb = [sbt(f"a_xnb{i}", [128, 1024], BF16) for i in range(4)]
    xnt = [sbt(f"a_xnt{i}", [128, 4096], BF16) for i in range(2)]
    ko = [sbt(f"a_ko{i}", [128, 4096], BF16) for i in range(2)]
    vo = [sbt(f"a_vo{i}", [128, 1024], BF16) for i in range(2)]
    rt = [sbt(f"a_rt{i}", [128, 512], F32) for i in range(4)]
    bnst = [sbt(f"a_bn{i}", [128, 12], F32) for i in range(2)]
    mv = [sbt(f"a_mv{i}", [128, 2], F32) for i in range(2)]
    rstd = [sbt(f"a_rs{i}", [128, 1], F32) for i in range(2)]
    wqs = sbt("a_wq", [128, 8192], BF16)
    wks = sbt("a_wk", [128, 8192], BF16)
    wvs = sbt("a_wv", [128, 8192], BF16)
    eps_t = sbt("a_eps", [128, 1], F32)

    t_eps = W.pub("dv")

    @block.vector
    def _(vec):
        got = S.inc(vec.memset(eps_t[:], EPS), "dv", 1)
        assert got == t_eps[1]

    t_aw = [W.pub(f"d_aw{i}", 16) for i in range(3)]

    @block.gpsimd
    def _(g):
        for i, (wsb, wdr) in enumerate(((wks, wkT), (wvs, wvT), (wqs, wqT))):
            S.inc(g.dma_start(wsb[:], wdr), f"d_aw{i}", 16)

    def one_pass(tag, nst, src_x, cos_d, sin_d, wproj, do_v, do_q,
                 kt_dst, v_dst, xnt_dst):
        RX = RBuf(2)
        RBN = RBuf(2)
        RMV = RBuf(2)
        RRS = RBuf(2)
        RXNB = RBuf(4)
        RXNT = RBuf(2)
        RKO = RBuf(2)
        RVO = RBuf(2)
    

        EV = []
        for st in range(nst):
            par = st % 2
            ev = dict(st=st, par=par)
            k, w = RX.acquire()
            ev["x"] = (k, w)
            dxp = f"d_{tag}x{k % 2}"
            W.pub(dxp, 16)
            W.pub(dxp, 16)
            tkx = W.pub(dxp, 16)
            RX.set_prod(k, tkx)
            ev["t_x"] = tkx
            ev["tiles"] = []
            for t in range(4):
                tp = t % 2
                tl = dict(t=t, tp=tp)
                kb, wb = RBN.acquire()
                tl["bn"] = (kb, wb)
                tl["t_bn0"] = W.pub("dv")
                tl["t_bn"] = W.pub("dv")
                km, wm = RMV.acquire()
                tl["mvw"] = wm
                t_ag = W.pub("dv")
                RBN.add_cons(kb, t_ag)
                RMV.set_prod(km, t_ag)
                tl["t_ag"] = t_ag
                kr, wr = RRS.acquire()
                tl["rsw"] = wr
                t_ln = W.pub("ac")
                RMV.add_cons(km, t_ln)
                tl["t_ln"] = t_ln
                t_rs = W.pub("ac")
                RRS.set_prod(kr, t_rs)
                tl["t_rs"] = t_rs
                kn, wn = RXNB.acquire()
                tl["xnbw"] = wn
                tl["kxnb"] = kn
                t_xn = W.pub("dv")
                RXNB.set_prod(kn, t_xn)
                RMV.add_cons(km, t_xn)
                RRS.add_cons(kr, t_xn)
                tl["t_xn"] = t_xn
                if do_q and XNS is not None:
                    tl["t_xns"] = W.pub(f"d_qxns{kn % 4}", 16)
                    RXNB.add_cons(kn, tl["t_xns"])
                kt, wt = RB[6 + tp].acquire()
                tl["tbw"] = wt
                t_tp = W.pub("pe")
                RB[6 + tp].set_prod(kt, t_tp)
                RXNB.add_cons(kn, t_tp)
                tl["t_tp"] = t_tp
                if t == 0:
                    kxt, wxt = RXNT.acquire()
                    ev["xnt"] = kxt
                    ev["xntw"] = wxt
                t_ev = W.pub("ac")
                RB[6 + tp].add_cons(kt, t_ev)
                tl["t_ev"] = t_ev
                ev["tiles"].append(tl)
            RXNT.set_prod(ev["xnt"], ev["tiles"][3]["t_ev"])
            ev["prs"] = []
            for pr in range(4):
                ba, bb = (0, 1) if pr % 2 == 0 else (2, 3)
                q = dict(pr=pr, ba=ba, bb=bb)
                ka, wa = RB[ba].acquire()
                kb2, wb2 = RB[bb].acquire()
                q["baw"], q["bbw"] = wa, wb2
                t_pa = W.pub("pe")
                t_pb = W.pub("pe")
                RB[ba].set_prod(ka, t_pa)
                RB[bb].set_prod(kb2, t_pb)
                q["t_pa"], q["t_pb"] = t_pa, t_pb
                q["ka"], q["kb"] = ka, kb2
                if pr == 0:
                    kk, wk = RKO.acquire()
                    ev["ko"] = kk
                    ev["kow"] = wk
                t_ra = W.pub("dv")
                t_rb = W.pub("dv")
                t_ro = W.pub("dv")
                RB[ba].add_cons(ka, t_ra)
                RB[bb].add_cons(kb2, t_rb)
                q["t_ra"], q["t_rb"], q["t_ro"] = t_ra, t_rb, t_ro
                if pr == 3:
                    RX.add_cons(ev["x"][0], t_ro)
                ev["prs"].append(q)
            RKO.set_prod(ev["ko"], ev["prs"][3]["t_ro"])
            dkop = f"d_{tag}ko{ev['ko'] % 2}"
            for _ in range(7):
                W.pub(dkop, 16)
            tsk = W.pub(dkop, 16)
            RKO.add_cons(ev["ko"], tsk)
            ev["t_stko"] = tsk
            if do_q:
                tsx = W.pub(f"d_{tag}xnt{ev['xnt'] % 2}", 16)
                RXNT.add_cons(ev["xnt"], tsx)
                ev["t_stxnt"] = tsx
                RXNT.add_cons(ev["xnt"], ev["prs"][3]["t_pb"])
            if do_v:
                ev["vs"] = []
                for sbk in range(4):
                    ba, bb = (4, 5) if sbk % 2 == 0 else (0, 1)
                    v = dict(sbk=sbk, ba=ba, bb=bb)
                    ka, wa = RB[ba].acquire()
                    kb2, wb2 = RB[bb].acquire()
                    v["baw"], v["bbw"] = wa, wb2
                    t_va = W.pub("pe")
                    t_vb = W.pub("pe")
                    RB[ba].set_prod(ka, t_va)
                    RB[bb].set_prod(kb2, t_vb)
                    v["t_va"], v["t_vb"] = t_va, t_vb
                    if sbk == 3:
                        RXNT.add_cons(ev["xnt"], t_vb)
                    kv, wv = RVO.acquire()
                    v["vow"] = wv
                    t_c1 = W.pub("ac")
                    t_c2 = W.pub("ac")
                    RB[ba].add_cons(ka, t_c1)
                    RB[bb].add_cons(kb2, t_c2)
                    RVO.set_prod(kv, t_c2)
                    v["t_c1"], v["t_c2"] = t_c1, t_c2
                    tsv = W.pub(f"d_{tag}vo{kv % 2}", 16)
                    RVO.add_cons(kv, tsv)
                    v["t_stv"] = tsv
                    ev["vs"].append(v)
            EV.append(ev)

        ew, pub, pub16 = mkemit(S, None)

        @block.sync
        def _(sync):
            def loads(ev):
                par, st = ev["par"], ev["st"]
                dxp = ev["t_x"][0]
                ew(sync, ev["x"][1])
                S.inc(sync.dma_start(xbuf[par][:], src_x[st]), dxp, 16)
                S.inc(sync.dma_start(
                    cbuf[par][:].rearrange("p (oc s) -> p oc s", s=512),
                    cos_d[:, :, st * 512:(st + 1) * 512]
                    .rearrange("oc p s -> p oc s")), dxp, 16)
                S.inc(sync.dma_start(
                    sbin[par][:].rearrange("p (oc s) -> p oc s", s=512),
                    sin_d[:, :, st * 512:(st + 1) * 512]
                    .rearrange("oc p s -> p oc s")), dxp, 16)
                assert S.c[dxp] == ev["t_x"][1]

            def stores(ev):
                st, par = ev["st"], ev["par"]
                if do_q and XNS is not None:
                    for tl in ev["tiles"]:
                        ew(sync, [tl["t_xn"]])
                        pub16(sync.dma_start(XNS[st * 4 + tl["t"]],
                                             xnb[tl["kxnb"] % 4][:]), tl["t_xns"])
                dkop = ev["t_stko"][0]
                ew(sync, [ev["prs"][3]["t_ro"]])
                for oc in range(8):
                    ii = sync.dma_start(kt_dst[oc, :, st * 512:(st + 1) * 512],
                                        ko[par][:, oc * 512:(oc + 1) * 512])
                    S.inc(ii, dkop, 16)
                assert S.c[dkop] == ev["t_stko"][1]
                if do_q:
                    ew(sync, [ev["tiles"][3]["t_ev"]])
                    ii = sync.dma_start(
                        xnt_dst[:, :, st * 512:(st + 1) * 512]
                        .rearrange("dc p s -> p dc s"),
                        xnt[par][:].rearrange("p (dc s) -> p dc s", s=512))
                    pub16(ii, ev["t_stxnt"])
                if do_v:
                    for v in ev["vs"]:
                        vp = v["sbk"] % 2
                        ew(sync, [v["t_c2"]])
                        ii = sync.dma_start(
                            v_dst[st * 512 + v["sbk"] * 128:
                                  st * 512 + (v["sbk"] + 1) * 128, :],
                            vo[vp][:])
                        pub16(ii, v["t_stv"])

            for i, ev in enumerate(EV):
                loads(ev)
                if i > 0:
                    stores(EV[i - 1])
            stores(EV[-1])

        ew, pub, _ = mkemit(S, "dv")

        @block.vector
        def _(vec):
            for ev in EV:
                par = ev["par"]
                for tl in ev["tiles"]:
                    t, tp = tl["t"], tl["tp"]
                    xt = xbuf[par][:, t * 1024:(t + 1) * 1024]
                    if t == 0:
                        ew(vec, [ev["t_x"]])
                    ew(vec, tl["bn"][1])
                    pub(vec.bn_stats(bnst[tp][:, 0:6], xt[:, 0:512]), tl["t_bn0"])
                    pub(vec.bn_stats(bnst[tp][:, 6:12], xt[:, 512:1024]), tl["t_bn"])
                    S.wait(vec, "dv", tl["t_bn"][1])
                    ew(vec, tl["mvw"])
                    pub(vec.bn_aggr(mv[tp][:], bnst[tp][:]), tl["t_ag"])
                    ew(vec, [tl["t_rs"]] + tl["xnbw"])
                    pub(vec.tensor_scalar(xnb[tl["kxnb"] % 4][:], xt, mv[tp][:, 0:1],
                                          rstd[tp][:], ALU.subtract, ALU.mult),
                        tl["t_xn"])
                for q in ev["prs"]:
                    pr, ba, bb = q["pr"], q["ba"], q["bb"]
                    ww = [q["t_pa"], q["t_pb"]]
                    if pr == 0:
                        ww = ww + ev["kow"]
                    ew(vec, ww)
                    ca = cbuf[par][:, pr * 512:(pr + 1) * 512]
                    sa = sbin[par][:, pr * 512:(pr + 1) * 512]
                    vec.tensor_mul(rt[0][:], pf[ba][:, 0:512], ca)
                    vec.tensor_mul(rt[1][:], pf[bb][:, 0:512], sa)
                    pub(vec.tensor_mul(rt[2][:], pf[ba][:, 0:512], sa), q["t_ra"])
                    pub(vec.tensor_mul(rt[3][:], pf[bb][:, 0:512], ca), q["t_rb"])
                    vec.tensor_add(ko[par][:, pr * 512:(pr + 1) * 512],
                                   rt[0][:], rt[1][:])
                    pub(vec.tensor_sub(ko[par][:, (pr + 4) * 512:(pr + 5) * 512],
                                       rt[3][:], rt[2][:]), q["t_ro"])

        ew, pub, _ = mkemit(S, "ac")

        @block.scalar
        def _(sc):
            first = True
            for ev in EV:
                par = ev["par"]
                for tl in ev["tiles"]:
                    t, tp = tl["t"], tl["tp"]
                    if first:
                        ew(sc, [t_eps])
                        first = False
                    ew(sc, [tl["t_ag"]] + tl["rsw"])
                    pub(sc.activation(rstd[tp][:], mv[tp][:, 1:2], AF.Ln,
                                      bias=eps_t[:], scale=1.0), tl["t_ln"])
                    S.wait(sc, "ac", tl["t_ln"][1])
                    pub(sc.activation(rstd[tp][:], rstd[tp][:], AF.Exp,
                                      bias=0.0, scale=-0.5), tl["t_rs"])
                    ww = [tl["t_tp"]]
                    if t == 0:
                        ww = ww + ev["xntw"]
                    ew(sc, ww)
                    tb = 6 + tp
                    xv = xnt[par][:].rearrange("p (dc s) -> p dc s", s=512)
                    pub(sc.copy(xv[:, :, t * 128:(t + 1) * 128],
                                pbt[tb - 6][:].rearrange("p (dc s) -> p dc s", s=128)),
                        tl["t_ev"])
                if do_v:
                    for v in ev["vs"]:
                        vp = v["sbk"] % 2
                        ew(sc, [v["t_va"], v["t_vb"]] + v["vow"])
                        pub(sc.copy(vo[vp][:, 0:512], pf[v["ba"]][:, 0:512]),
                            v["t_c1"])
                        pub(sc.copy(vo[vp][:, 512:1024], pf[v["bb"]][:, 0:512]),
                            v["t_c2"])

        ew, pub, _ = mkemit(S, "pe")

        @block.tensor
        def _(te):
            first = True
            for ev in EV:
                par = ev["par"]
                for tl in ev["tiles"]:
                    t, tp = tl["t"], tl["tp"]
                    ww = [tl["t_xn"]] + tl["tbw"]
                    if first:
                        ww = ww + [t_ident] + t_aw
                        first = False
                    ew(te, ww)
                    tb = 6 + tp
                    for dcc in range(8):
                        ii = te.transpose(pbt[tb - 6][:, dcc * 128:(dcc + 1) * 128],
                                          xnb[tl["kxnb"] % 4][:, dcc * 128:(dcc + 1) * 128],
                                          ident_s[:])
                    pub(ii, tl["t_tp"])
                for q in ev["prs"]:
                    pr = q["pr"]
                    ww = q["baw"] + q["bbw"]
                    if pr == 0:
                        ww = ww + [ev["tiles"][3]["t_ev"]]
                    ew(te, ww)
                    for half, bk, tok in ((0, q["ba"], q["t_pa"]),
                                          (1, q["bb"], q["t_pb"])):
                        oc = pr + 4 * half
                        for dcc in range(8):
                            ii = te.matmul(
                                pf[bk][:, 0:512],
                                wproj[:, dcc * 1024 + oc * 128: dcc * 1024 + (oc + 1) * 128],
                                xnt[par][:, dcc * 512:(dcc + 1) * 512],
                                start=(dcc == 0), stop=(dcc == 7))
                        pub(ii, tok)
                if do_v:
                    for v in ev["vs"]:
                        ew(te, v["baw"] + v["bbw"])
                        for dcc in range(8):
                            lhs = xnt[par][:, dcc * 512 + v["sbk"] * 128:
                                           dcc * 512 + (v["sbk"] + 1) * 128]
                            i1 = te.matmul(pf[v["ba"]][:, 0:512], lhs,
                                           wvs[:, dcc * 1024: dcc * 1024 + 512],
                                           start=(dcc == 0), stop=(dcc == 7))
                            i2 = te.matmul(pf[v["bb"]][:, 0:512], lhs,
                                           wvs[:, dcc * 1024 + 512: (dcc + 1) * 1024],
                                           start=(dcc == 0), stop=(dcc == 7))
                        pub(i1, v["t_va"])
                        pub(i2, v["t_vb"])

        for i in range(2):
            for bb2 in (f"d_{tag}ko{i}", f"d_{tag}vo{i}" if do_v else None,
                        f"d_{tag}xnt{i}" if do_q else None,
                        f"d_{tag}xns{i}" if (do_q and XNS is not None) else None):
                if bb2 is not None and bb2 in S.c:
                    final_waits.append((bb2, S.c[bb2]))
        return EV

    one_pass("A", 16, xb, cosK, sinK, wks, True, False, KTs, Vs, None)
    one_pass("q", 8, xq, cosQ, sinQ, wqs, False, True, QTs, None, XNTs)


def _phase_B(nc, block, S, W, ctx, RB, pf, pbt, ident_s, t_ident,
             KTs, Vs, QTs, maskd, ATTs, final_waits, a_done):
    sbt = lambda n, shp, dt: ctx.enter_context(nc.sbuf_tensor(n, shp, dt))
    masks = sbt("b_mask", [128, 4096], F32)
    qb = [sbt(f"b_q{i}", [128, 4096], BF16) for i in range(2)]
    kb = [sbt(f"b_k{i}", [128, 4096], BF16) for i in range(2)]
    vb = [sbt(f"b_v{i}", [128, 4096], BF16) for i in range(2)]
    pbuf = [sbt(f"b_p{i}", [128, 512], BF16) for i in range(2)]
    pts = [sbt(f"b_pt{i}", [128, 512], BF16) for i in range(2)]
    accS = sbt("b_acc", [128, 4096], F32)
    mst = [sbt(f"b_m{i}", [128, 4], F32) for i in range(2)]
    mtmp = sbt("b_mt", [128, 4], F32)
    lbuf = sbt("b_l", [128, 4], F32)
    ltmp = sbt("b_lt", [128, 4], F32)
    negm = sbt("b_nm", [128, 4], F32)
    dmb = sbt("b_dm", [128, 4], F32)
    asc = sbt("b_as", [128, 4], F32)
    lp = sbt("b_lp", [128, 4], F32)
    linv = sbt("b_li", [128, 4], F32)
    astg = [sbt(f"b_at{i}", [128, 1024], F32) for i in range(2)]

    pre = []
    if a_done:
        pre = [(nm, S.c[nm]) for nm in
               ("d_Ako0", "d_Ako1", "d_Avo0", "d_Avo1",
                "d_qko0", "d_qko1", "d_qxnt0", "d_qxnt1") if nm in S.c]

    RQ, RK, RV = RBuf(2), RBuf(2), RBuf(2)
    RPB = [RBuf(1) for _ in range(2)]
    RPT = [RBuf(1) for _ in range(2)]
    RAT = RBuf(2)
    lu_last = [None]
    px_last = [None]

    t_mld = W.pub("d_bm", 16)

    JEV = []
    for J in range(NJ):
        kq, wq_ = RQ.acquire()
        jev = dict(J=J, kq=kq, wq=wq_, ts=[])
        jev["t_q"] = W.pub(f"d_bq{kq % 2}", 16)
        RQ.set_prod(kq, jev["t_q"])
        for t in range(2 * J + 2):
            kk, wk_ = RK.acquire()
            kv, wv_ = RV.acquire()
            tev = dict(t=t, kk=kk, wk=wk_, kv=kv, wv=wv_,
                       diag=(t >= 2 * J), subs=[])
            tev["t_k"] = W.pub(f"d_bk{kk % 2}", 16)
            RK.set_prod(kk, tev["t_k"])
            tev["t_v"] = W.pub(f"d_bv{kv % 2}", 16)
            RV.set_prod(kv, tev["t_v"])
            for sub in range(4):
                par = (t * 4 + sub) % 2
                sbk = sub
                aa, bb2 = 4, 5
                sv = dict(sub=sub, par=par, sbk=sbk, aa=aa, bb2=bb2)
                ks, ws = RB[sbk].acquire()
                sv["ks"], sv["ksw"] = ks, ws
                t_rd = W.pub("dv")
                RB[sbk].add_cons(ks, t_rd)
                sv["t_rd"] = t_rd
                tev["subs"].append(sv)
            tev["w_ng"] = px_last[0]
            tev["t_mx"] = W.pub("dv")
            tev["t_nm"] = W.pub("dv")
            tev["t_md"] = W.pub("dv")
            for sv in tev["subs"]:
                sub, sbk = sv["sub"], sv["sbk"]
                t_mm = W.pub("pe")
                RB[sbk].set_prod(sv["ks"], t_mm)
                sv["t_mm"] = t_mm
                if sub == 3:
                    RK.add_cons(kk, t_mm)
                    if t == 2 * J + 1:
                        RQ.add_cons(kq, t_mm)
            for sv in tev["subs"]:
                sub, par, sbk = sv["sub"], sv["par"], sv["sbk"]
                sv["w_ascp"] = lu_last[0]
                t_al = W.pub("ac")
                sv["t_al"] = t_al
                kpb, wpb = RPB[par].acquire()
                sv["wpb"] = wpb + [lu_last[0]]
                t_px = W.pub("ac")
                RPB[par].set_prod(kpb, t_px)
                RB[sbk].add_cons(sv["ks"], t_px)
                sv["t_px"] = t_px
                sv["kpb"] = kpb
                if sub == 3:
                    px_last[0] = t_px
                ktb, wtb = RB[6 + par].acquire()
                sv["wtb"] = wtb
                t_tr = W.pub("pe")
                RB[6 + par].set_prod(ktb, t_tr)
                RPB[par].add_cons(kpb, t_tr)
                sv["t_tr"] = t_tr
                kpt, wpt = RPT[par].acquire()
                sv["wpt"] = wpt
                t_pc = W.pub("ac")
                RPT[par].set_prod(kpt, t_pc)
                RB[6 + par].add_cons(ktb, t_pc)
                sv["t_pc"] = t_pc
                kaa, waa = RB[sv["aa"]].acquire()
                kab, wab = RB[sv["bb2"]].acquire()
                sv["waa"], sv["wab"] = waa, wab
                sv["kaa"], sv["kab"] = kaa, kab
                t_va = W.pub("pe")
                t_vb = W.pub("pe")
                RB[sv["aa"]].set_prod(kaa, t_va)
                RB[sv["bb2"]].set_prod(kab, t_vb)
                RPT[par].add_cons(kpt, t_vb)
                if sub == 3:
                    RV.add_cons(kv, t_vb)
                sv["t_va"], sv["t_vb"] = t_va, t_vb
            for sv in tev["subs"]:
                sv["t_rs"] = W.pub("dv")
                t_ca = W.pub("dv")
                t_cb = W.pub("dv")
                RB[sv["aa"]].add_cons(sv["kaa"], t_ca)
                RB[sv["bb2"]].add_cons(sv["kab"], t_cb)
                sv["t_ca"], sv["t_cb"] = t_ca, t_cb
            tev["t_lt"] = W.pub("dv")
            tev["t_lu"] = W.pub("dv")
            lu_last[0] = tev["t_lu"]
            jev["ts"].append(tev)
        jev["evac"] = []
        jev["t_li"] = W.pub("dv")
        for sub in range(4):
            ka, wa = RAT.acquire()
            t_at = W.pub("dv")
            RAT.set_prod(ka, t_at)
            t_st = W.pub(f"d_bat{ka % 2}", 16)
            RAT.add_cons(ka, t_st)
            jev["evac"].append(dict(sub=sub, ka=ka, wa=wa,
                                    t_at=t_at, t_st=t_st))
        JEV.append(jev)

    ew, pub, pub16 = mkemit(S, None)

    @block.sync
    def _(sync):
        ew(sync, pre)
        S.inc(sync.dma_start(masks[:], maskd), "d_bm", 16)
        for jev in JEV:
            J = jev["J"]
            ew(sync, jev["wq"])
            pub16(sync.dma_start(
                qb[J % 2][:].rearrange("p (dc s) -> p dc s", s=512),
                QTs[:, :, J * 512:(J + 1) * 512].rearrange("dc p s -> p dc s")),
                jev["t_q"])
            for tev in jev["ts"]:
                t = tev["t"]
                ew(sync, tev["wk"])
                pub16(sync.dma_start(
                    kb[t % 2][:].rearrange("p (dc s) -> p dc s", s=512),
                    KTs[:, :, t * 512:(t + 1) * 512].rearrange("dc p s -> p dc s")),
                    tev["t_k"])
                ew(sync, tev["wv"])
                pub16(sync.dma_start(
                    vb[t % 2][:].rearrange("p (kc d) -> p kc d", d=1024),
                    Vs[t * 512:(t + 1) * 512, :]
                    .rearrange("(kc p) d -> p kc d", p=128)), tev["t_v"])

    ew, pub, pub16 = mkemit(S, "po")

    @block.gpsimd
    def _(g):
        for jev in JEV:
            J = jev["J"]
            for evc in jev["evac"]:
                sp = evc["sub"] % 2
                ew(g, [evc["t_at"]])
                pub16(g.dma_start(
                    ATTs[J * 512 + evc["sub"] * 128:
                         J * 512 + (evc["sub"] + 1) * 128, :],
                    astg[sp][:]), evc["t_st"])
    for i in range(2):
        if f"d_bat{i}" in S.c:
            final_waits.append((f"d_bat{i}", S.c[f"d_bat{i}"]))

    ew, pub, _ = mkemit(S, "pe")

    @block.tensor
    def _(te):
        for jev in JEV:
            J = jev["J"]
            for tev in jev["ts"]:
                t = tev["t"]
                for sv in tev["subs"]:
                    sub, sbk = sv["sub"], sv["sbk"]
                    ww = list(sv["ksw"])
                    if sub == 0:
                        ww += [tev["t_k"], tev["t_v"]]
                        if t == 0:
                            ww += [jev["t_q"]]
                    ew(te, ww)
                    for dcc in range(8):
                        ii = te.matmul(
                            pf[sbk][:, 0:512],
                            qb[J % 2][:, dcc * 512 + sub * 128: dcc * 512 + (sub + 1) * 128],
                            kb[t % 2][:, dcc * 512:(dcc + 1) * 512],
                            start=(dcc == 0), stop=(dcc == 7))
                    pub(ii, sv["t_mm"])
                for sv in tev["subs"]:
                    sub, par, sbk = sv["sub"], sv["par"], sv["sbk"]
                    ew(te, [sv["t_px"]] + sv["wtb"])
                    tb = 6 + par
                    for kc in range(4):
                        ii = te.transpose(pbt[tb - 6][:, kc * 128:(kc + 1) * 128],
                                          pbuf[par][:, kc * 128:(kc + 1) * 128],
                                          ident_s[:])
                    pub(ii, sv["t_tr"])
                    ew(te, [sv["t_pc"]] + sv["waa"] + sv["wab"])
                    for kc in range(4):
                        i1 = te.matmul(pf[sv["aa"]][:, 0:512],
                                       pts[par][:, kc * 128:(kc + 1) * 128],
                                       vb[t % 2][:, kc * 1024: kc * 1024 + 512],
                                       start=(kc == 0), stop=(kc == 3))
                        i2 = te.matmul(pf[sv["bb2"]][:, 0:512],
                                       pts[par][:, kc * 128:(kc + 1) * 128],
                                       vb[t % 2][:, kc * 1024 + 512: (kc + 1) * 1024],
                                       start=(kc == 0), stop=(kc == 3))
                    pub(i1, sv["t_va"])
                    pub(i2, sv["t_vb"])

    ew, pub, _ = mkemit(S, "dv")

    @block.vector
    def _(vec):
        ew(vec, [t_mld])
        for jev in JEV:
            J = jev["J"]
            vec.memset(accS[:], 0.0)
            vec.memset(mst[0][:], NEG)
            vec.memset(lbuf[:], 0.0)
            for tev in jev["ts"]:
                t, diag = tev["t"], tev["diag"]
                mprev, mcur = mst[t % 2], mst[(t + 1) % 2]
                for sv in tev["subs"]:
                    sub, sbk = sv["sub"], sv["sbk"]
                    ew(vec, [sv["t_mm"]])
                    if diag:
                        msl = masks[:].rearrange("p (q k) -> p q k", k=1024)[
                            :, sub, (t - 2 * J) * 512:(t - 2 * J + 1) * 512]
                        vec.tensor_add(pf[sbk][:, 0:512], pf[sbk][:, 0:512], msl)
                    pub(vec.reduce_max(mtmp[:, sub:sub + 1], pf[sbk][:, 0:512],
                                       axis=AX.X), sv["t_rd"])
                S.wait(vec, "dv", tev["subs"][3]["t_rd"][1])
                ew(vec, [tev["w_ng"]])
                pub(vec.tensor_tensor(mcur[:], mtmp[:], mprev[:], ALU.max),
                    tev["t_mx"])
                S.wait(vec, "dv", tev["t_mx"][1])
                pub(vec.tensor_scalar(negm[:], mcur[:], -1.0, None, ALU.mult),
                    tev["t_nm"])
                pub(vec.tensor_sub(dmb[:], mprev[:], mcur[:]), tev["t_md"])
                for sv in tev["subs"]:
                    sub = sv["sub"]
                    accv = accS[:, sub * 1024:(sub + 1) * 1024]
                    ew(vec, [sv["t_px"]])
                    pub(vec.tensor_scalar(accv, accv, asc[:, sub:sub + 1],
                                          None, ALU.mult), sv["t_rs"])
                    ew(vec, [sv["t_va"]])
                    pub(vec.tensor_add(accv[:, 0:512], accv[:, 0:512],
                                       pf[sv["aa"]][:, 0:512]), sv["t_ca"])
                    ew(vec, [sv["t_vb"]])
                    pub(vec.tensor_add(accv[:, 512:1024], accv[:, 512:1024],
                                       pf[sv["bb2"]][:, 0:512]), sv["t_cb"])
                pub(vec.tensor_mul(ltmp[:], lbuf[:], asc[:]), tev["t_lt"])
                S.wait(vec, "dv", tev["t_lt"][1])
                pub(vec.tensor_add(lbuf[:], ltmp[:], lp[:]), tev["t_lu"])
            S.wait(vec, "dv", jev["ts"][-1]["t_lu"][1])
            pub(vec.reciprocal(linv[:], lbuf[:]), jev["t_li"])
            S.wait(vec, "dv", jev["t_li"][1])
            for evc in jev["evac"]:
                sub = evc["sub"]
                sp = sub % 2
                ew(vec, evc["wa"])
                pub(vec.tensor_scalar(astg[sp][:],
                                      accS[:, sub * 1024:(sub + 1) * 1024],
                                      linv[:, sub:sub + 1], None, ALU.mult),
                    evc["t_at"])

    ew, pub, _ = mkemit(S, "ac")

    @block.scalar
    def _(sc):
        for jev in JEV:
            for tev in jev["ts"]:
                for sv in tev["subs"]:
                    sub, par, sbk = sv["sub"], sv["par"], sv["sbk"]
                    ew(sc, [tev["t_md"], sv["w_ascp"]])
                    pub(sc.activation(asc[:, sub:sub + 1], dmb[:, sub:sub + 1],
                                      AF.Exp), sv["t_al"])
                    ew(sc, [x for x in sv["wpb"] if x] + [sv["t_mm"]])
                    pub(sc.activation(pbuf[par][:], pf[sbk][:, 0:512], AF.Exp,
                                      bias=negm[:, sub:sub + 1], scale=1.0,
                                      accum_out=lp[:, sub:sub + 1]), sv["t_px"])
                    ew(sc, [sv["t_tr"]] + sv["wpt"])
                    tb = 6 + par
                    pub(sc.copy(pts[par][:], pbt[tb - 6][:, 0:512]), sv["t_pc"])


def _phase_C(nc, block, S, W, ctx, RB, pf, Wgf, t_wc,
             W1b, W2b, W3b, XNTs, ATTs, xq, OUT, final_waits, prev,
             dbgH=None, dbgA=None, dbgW=None, dbgG=None):
    sbt = lambda n, shp, dt: ctx.enter_context(nc.sbuf_tensor(n, shp, dt))
    NSLOT = 5
    slots = [sbt(f"c_w{i}", [128, 8192], BF16) for i in range(NSLOT)]
    xcb = [sbt(f"c_x{i}", [128, 4096], BF16) for i in range(2)]
    hT = sbt("c_h", [128, 8192], BF16)
    sg = [sbt(f"c_sg{i}", [128, 512], F32) for i in range(2)]
    accF = sbt("c_acc", [128, 4096], F32)
    wgs = sbt("c_wg", [128, 64], BF16)
    wbuf = sbt("c_wb", [128, 256], F32)
    gm1 = sbt("c_g1", [128, 4], F32)
    gm2 = sbt("c_g2", [128, 4], F32)
    gd1 = sbt("c_gd1", [128, 4], F32)
    gd2 = sbt("c_gd2", [128, 4], F32)
    ge = sbt("c_ge", [128, 8], F32)
    gp = sbt("c_gp", [128, 8], F32)
    geq1 = sbt("c_geq1", [128, 32], F32)
    gmask = sbt("c_gmask", [128, 32], F32)
    geq2 = sbt("c_geq2", [128, 32], F32)
    gta = sbt("c_gta", [128, 32], F32)
    cmbx = sbt("c_cx", [128, 4096], F32)
    cmba = sbt("c_ca", [128, 4096], F32)
    ostg = sbt("c_o", [128, 4096], F32)

    pre = []
    if prev:
        pre = [(nm, S.c[nm]) for nm in
               ("d_qxnt0", "d_qxnt1", "d_bat0", "d_bat1") if nm in S.c]

    RW = RBuf(NSLOT)
    RXC = RBuf(2)
    RSG = [RBuf(1) for _ in range(2)]
    RH = RBuf(1)
    RCMB = RBuf(1)
    ROS = RBuf(1)

    t_wg = W.pub("d_cwg", 16)

    GEV = []
    for chunk in range(NCHUNK):
        kx, wx = RXC.acquire()
        g = dict(chunk=chunk, kx=kx, wx=wx)
        g["t_x"] = W.pub(f"d_cx{kx % 2}", 16)
        RXC.set_prod(kx, g["t_x"])
        kb0, wb0 = RB[0].acquire()
        g["wb0"] = wb0
        t_g = W.pub("pe")
        RB[0].set_prod(kb0, t_g)
        RXC.add_cons(kx, t_g)
        g["t_g"] = t_g
        g["t_ir"] = []
        for sub in range(4):
            t1 = W.pub("dv")
            te1 = W.pub("dv")
            tm = W.pub("dv")
            t2 = W.pub("dv")
            g["t_ir"].append((t1, te1, tm, t2))
        t_gd1 = W.pub("dv")
        t_gd = W.pub("dv")
        RB[0].add_cons(kb0, t_gd)
        g["t_gd1"] = t_gd1
        g["t_gd"] = t_gd
        g["t_ges"] = [W.pub("ac") for _ in range(8)]
        g["t_ge"] = g["t_ges"][-1]
        g["t_ga"] = [(W.pub("dv"), W.pub("dv"), W.pub("dv"), W.pub("dv"))
                     for _ in range(4)]
        g["t_gp"] = g["t_ga"][-1][-1]
        g["t_gq"] = [(W.pub("dv"), W.pub("dv")) for _ in range(4)]
        t_gw = W.pub("dv")
        g["t_gw"] = t_gw
        GEV.append(g)
    CEV = []
    for chunk in range(NCHUNK):
        kx, wx = RXC.acquire()
        cev = dict(chunk=chunk, kx=kx, wx=wx, es=[])
        cev["t_x"] = W.pub(f"d_cx{kx % 2}", 16)
        RXC.set_prod(kx, cev["t_x"])
        kc, wc = RCMB.acquire()
        cev["wc"] = wc
        W.pub("d_cc", 16)
        cev["t_c"] = W.pub("d_cc", 16)
        RCMB.set_prod(kc, cev["t_c"])
        for e in range(E):
            eev = dict(e=e, wl=[], uv=[], ys=[])
            kh, wh = RH.acquire()
            eev["wh"] = wh
            items = [("W1", 0), ("W2", 0), ("W1", 1), ("W2", 1),
                     ("W3", 0), ("W3", 1)]
            slot_of = {}
            for wn, half in items:
                kw, wwl = RW.acquire()
                t_l = W.pub(f"d_cw{kw % NSLOT}", 16)
                RW.set_prod(kw, t_l)
                eev["wl"].append(dict(wn=wn, half=half, slot=kw % NSLOT,
                                      kw=kw, w=wwl, t=t_l))
                slot_of[(wn, half)] = (kw % NSLOT, kw, t_l)
            for half in range(2):
                for hc8 in range(8):
                    hc = half * 8 + hc8
                    uv = dict(half=half, hc8=hc8, hc=hc,
                              w1=slot_of[("W1", half)], w2=slot_of[("W2", half)])
                    kU, wU = RB[hc % 2].acquire()
                    kV, wV = RB[2 + hc % 2].acquire()
                    uv["wU"], uv["wV"] = wU, wV
                    t_u = W.pub("pe")
                    t_v = W.pub("pe")
                    RB[hc % 2].set_prod(kU, t_u)
                    RB[2 + hc % 2].set_prod(kV, t_v)
                    uv["t_u"], uv["t_v"] = t_u, t_v
                    if hc8 == 7:
                        RW.add_cons(slot_of[("W1", half)][1], t_u)
                        RW.add_cons(slot_of[("W2", half)][1], t_v)
                        if half == 1 and e == E - 1:
                            RXC.add_cons(kx, t_v)
                    ksg, wsg = RSG[hc % 2].acquire()
                    uv["wsg"] = wsg
                    t_s = W.pub("ac")
                    RSG[hc % 2].set_prod(ksg, t_s)
                    RB[hc % 2].add_cons(kU, t_s)
                    uv["t_s"] = t_s
                    t_h = W.pub("dv")
                    RSG[hc % 2].add_cons(ksg, t_h)
                    RB[2 + hc % 2].add_cons(kV, t_h)
                    uv["t_h"] = t_h
                    if hc == 15:
                        RH.set_prod(kh, t_h)
                    eev["uv"].append(uv)
            ybanks = [(4, 5), (0, 1), (2, 3), (4, 5)]
            for sbk in range(4):
                ba, bb2 = ybanks[sbk]
                y = dict(sbk=sbk, ba=ba, bb2=bb2,
                         w30=slot_of[("W3", 0)], w31=slot_of[("W3", 1)])
                ka, wa = RB[ba].acquire()
                kb2, wb2 = RB[bb2].acquire()
                y["wa"], y["wb"] = wa, wb2
                t_ya = W.pub("pe")
                t_yb = W.pub("pe")
                RB[ba].set_prod(ka, t_ya)
                RB[bb2].set_prod(kb2, t_yb)
                y["t_ya"], y["t_yb"] = t_ya, t_yb
                if sbk == 3:
                    RW.add_cons(slot_of[("W3", 0)][1], t_yb)
                    RW.add_cons(slot_of[("W3", 1)][1], t_yb)
                    RH.add_cons(kh, t_yb)
                t_ea = W.pub("dv")
                t_eb = W.pub("dv")
                RB[ba].add_cons(ka, t_ea)
                RB[bb2].add_cons(kb2, t_eb)
                y["t_ea"], y["t_eb"] = t_ea, t_eb
                eev["ys"].append(y)
            if dbgH is not None:
                eev["t_dbgh"] = W.pub("d_dbgh", 16)
                RH.add_cons(kh, eev["t_dbgh"])
                eev["t_dbga"] = W.pub("d_dbga", 16)
            cev["es"].append(eev)
        ko_, wo = ROS.acquire()
        cev["wo"] = wo
        t_o = W.pub("dv")
        ROS.set_prod(ko_, t_o)
        RCMB.add_cons(kc, t_o)
        cev["t_o"] = t_o
        t_os = W.pub("d_co", 16)
        ROS.add_cons(ko_, t_os)
        cev["t_os"] = t_os
        CEV.append(cev)

    ew, pub, pub16 = mkemit(S, "po")

    @block.gpsimd
    def _(g):
        pub16(g.dma_start(wgs[:], Wgf), t_wg)
        if dbgW is not None:
            ew(g, [GEV[0]["t_gw"]])
            for nm, tns in (("dbgm1", gm1), ("dbgm2", gm2), ("dbgd1", gd1),
                            ("dbgd2", gd2), ("dbge", ge), ("dbgp", gp),
                            ("dbgq1", geq1), ("dbgmk", gmask)):
                S.inc(g.dma_start(dbgG[nm], tns[:]), "d_dbgw", 16)
            ew(g, [GEV[-1]["t_gw"]])
            vv = S.inc(g.dma_start(dbgW, wbuf[:]), "d_dbgw", 16)
            final_waits.append(("d_dbgw", vv))
        if dbgH is not None:
            for cev in CEV:
                for eev in cev["es"]:
                    idx = cev["chunk"] * E + eev["e"]
                    ew(g, [eev["uv"][15]["t_h"]])
                    pub16(g.dma_start(dbgH[idx], hT[:]), eev["t_dbgh"])
                    ew(g, [eev["ys"][3]["t_eb"]])
                    pub16(g.dma_start(dbgA[idx], accF[:]), eev["t_dbga"])
        for cev in CEV:
            ew(g, [cev["t_o"]])
            pub16(g.dma_start(OUT[cev["chunk"]], ostg[:]), cev["t_os"])
    final_waits.append(("d_co0", S.c["d_co0"]))
    final_waits.append(("d_co1", S.c["d_co1"]))
    if dbgH is not None:
        final_waits.append(("d_dbgh", S.c["d_dbgh"]))
        final_waits.append(("d_dbga", S.c["d_dbga"]))

    ew, pub, pub16 = mkemit(S, None)

    @block.sync
    def _(sync):
        if t_wc is not None:
            ew(sync, list(t_wc) + pre)

        def ld_xcb(chunk, kx, tok, waits):
            ew(sync, waits)
            pub16(sync.dma_start(
                xcb[kx % 2][:].rearrange("p (dc s) -> p dc s", s=512),
                XNTs[:, :, chunk * 512:(chunk + 1) * 512]
                .rearrange("dc p s -> p dc s")), tok)

        for g in GEV:
            ld_xcb(g["chunk"], g["kx"], g["t_x"], g["wx"])
        for cev in CEV:
            chunk = cev["chunk"]
            ld_xcb(chunk, cev["kx"], cev["t_x"], cev["wx"])
            ew(sync, cev["wc"])
            S.inc(sync.dma_start(cmbx[:], xq[chunk]), "d_cc", 16)
            ii = sync.dma_start(
                cmba[:].rearrange("p (sb d) -> p sb d", d=1024),
                ATTs[chunk * 512:(chunk + 1) * 512, :]
                .rearrange("(sb p) d -> p sb d", p=128))
            pub16(ii, cev["t_c"])
            for eev in cev["es"]:
                for wl in eev["wl"]:
                    ew(sync, wl["w"])
                    slot = wl["slot"]
                    if wl["wn"] == "W1":
                        ii = sync.dma_start(slots[slot][:], W1b[eev["e"], wl["half"]])
                    elif wl["wn"] == "W2":
                        ii = sync.dma_start(slots[slot][:], W2b[eev["e"], wl["half"]])
                    else:
                        ii = sync.dma_start(
                            slots[slot][:].rearrange("p (hc d) -> p hc d", d=1024),
                            W3b[eev["e"], wl["half"] * 8:(wl["half"] + 1) * 8]
                            .rearrange("hc p d -> p hc d"))
                    pub16(ii, wl["t"])

    ew, pub, _ = mkemit(S, "pe")

    @block.tensor
    def _(te):
        for g in GEV:
            chunk = g["chunk"]
            ww = [g["t_x"], t_wg] + g["wb0"]
            ew(te, ww)
            for sbk in range(4):
                for dcc in range(8):
                    ii = te.matmul(
                        pf[0][:, sbk * 8:(sbk + 1) * 8],
                        xcb[g["kx"] % 2][:, dcc * 512 + sbk * 128:
                                         dcc * 512 + (sbk + 1) * 128],
                        wgs[:, dcc * 8:(dcc + 1) * 8],
                        start=(dcc == 0), stop=(dcc == 7))
            pub(ii, g["t_g"])
        for cev in CEV:
            chunk = cev["chunk"]
            for eev in cev["es"]:
                for uv in eev["uv"]:
                    hc, hc8, half = uv["hc"], uv["hc8"], uv["half"]
                    s1 = uv["w1"][0]
                    s2 = uv["w2"][0]
                    ww = list(uv["wU"])
                    if hc8 == 0:
                        ww += [uv["w1"][2], uv["w2"][2]]
                        if hc == 0 and eev["e"] == 0:
                            ww += [cev["t_x"]]
                    ew(te, ww)
                    for dcc in range(8):
                        ii = te.matmul(
                            pf[hc % 2][:, 0:512],
                            slots[s1][:, dcc * 1024 + hc8 * 128:
                                      dcc * 1024 + (hc8 + 1) * 128],
                            xcb[cev["kx"] % 2][:, dcc * 512:(dcc + 1) * 512],
                            start=(dcc == 0), stop=(dcc == 7))
                    pub(ii, uv["t_u"])
                    ew(te, uv["wV"])
                    for dcc in range(8):
                        ii = te.matmul(
                            pf[2 + hc % 2][:, 0:512],
                            slots[s2][:, dcc * 1024 + hc8 * 128:
                                      dcc * 1024 + (hc8 + 1) * 128],
                            xcb[cev["kx"] % 2][:, dcc * 512:(dcc + 1) * 512],
                            start=(dcc == 0), stop=(dcc == 7))
                    pub(ii, uv["t_v"])
                s30 = eev["ys"][0]["w30"][0]
                s31 = eev["ys"][0]["w31"][0]
                for i, y in enumerate(eev["ys"]):
                    sbk = y["sbk"]
                    ww = list(y["wa"]) + list(y["wb"])
                    if i == 0:
                        ww += [y["w30"][2], y["w31"][2],
                               eev["uv"][15]["t_h"]]
                    ew(te, ww)
                    for hc in range(16):
                        half, hc8 = hc // 8, hc % 8
                        w3s = s30 if half == 0 else s31
                        lhs = hT[:, hc * 512 + sbk * 128: hc * 512 + (sbk + 1) * 128]
                        i1 = te.matmul(pf[y["ba"]][:, 0:512], lhs,
                                       slots[w3s][:, hc8 * 1024: hc8 * 1024 + 512],
                                       start=(hc == 0), stop=(hc == 15))
                        i2 = te.matmul(pf[y["bb2"]][:, 0:512], lhs,
                                       slots[w3s][:, hc8 * 1024 + 512: (hc8 + 1) * 1024],
                                       start=(hc == 0), stop=(hc == 15))
                    pub(i1, y["t_ya"])
                    pub(i2, y["t_yb"])

    ew, pub, _ = mkemit(S, "ac")

    @block.scalar
    def _(sc):
        for g in GEV:
            ew(sc, [g["t_gd"]])
            for sub in range(4):
                pub(sc.activation(ge[:, sub:sub + 1], gd1[:, sub:sub + 1],
                                  AF.Exp), g["t_ges"][2 * sub])
                pub(sc.activation(ge[:, 4 + sub:5 + sub], gd2[:, sub:sub + 1],
                                  AF.Exp), g["t_ges"][2 * sub + 1])
        for cev in CEV:
            for eev in cev["es"]:
                for uv in eev["uv"]:
                    hc = uv["hc"]
                    ew(sc, [uv["t_u"]] + uv["wsg"])
                    pub(sc.activation(sg[hc % 2][:], pf[hc % 2][:, 0:512],
                                      AF.Silu), uv["t_s"])

    ew, pub, _ = mkemit(S, "dv")

    @block.vector
    def _(vec):
        for g in GEV:
            chunk = g["chunk"]
            ew(vec, [g["t_g"]])
            for sub in range(4):
                lg = pf[0][:, sub * 8:(sub + 1) * 8]
                t1, te1, tm, t2 = g["t_ir"][sub]
                pub(vec.reduce_max(gm1[:, sub:sub + 1], lg, axis=AX.X), t1)
                S.wait(vec, "dv", t1[1])
                pub(vec.tensor_scalar(geq1[:, sub * 8:(sub + 1) * 8], lg,
                                      gm1[:, sub:sub + 1], None, ALU.is_equal), te1)
                S.wait(vec, "dv", te1[1])
                pub(vec.scalar_tensor_tensor(gmask[:, sub * 8:(sub + 1) * 8],
                                             geq1[:, sub * 8:(sub + 1) * 8], NEG, lg,
                                             ALU.mult, ALU.add), tm)
                S.wait(vec, "dv", tm[1])
                pub(vec.reduce_max(gm2[:, sub:sub + 1],
                                   gmask[:, sub * 8:(sub + 1) * 8], axis=AX.X), t2)
                S.wait(vec, "dv", t2[1])
            pub(vec.tensor_sub(gd1[:], gm2[:], gm1[:]), g["t_gd1"])
            pub(vec.tensor_sub(gd2[:], gm1[:], gm2[:]), g["t_gd"])
            ew(vec, [g["t_ge"]])
            for sub in range(4):
                ta1, ta2, tr1, tr2 = g["t_ga"][sub]
                pub(vec.tensor_scalar(gp[:, sub:sub + 1], ge[:, sub:sub + 1],
                                      1.0, None, ALU.add), ta1)
                pub(vec.tensor_scalar(gp[:, 4 + sub:5 + sub],
                                      ge[:, 4 + sub:5 + sub], 1.0, None, ALU.add), ta2)
                S.wait(vec, "dv", ta2[1])
                pub(vec.reciprocal(gp[:, sub:sub + 1], gp[:, sub:sub + 1]), tr1)
                pub(vec.reciprocal(gp[:, 4 + sub:5 + sub], gp[:, 4 + sub:5 + sub]),
                    tr2)
            S.wait(vec, "dv", g["t_gp"][1])
            for sub in range(4):
                sl = slice(sub * 8, (sub + 1) * 8)
                tta, tq = g["t_gq"][sub]
                pub(vec.tensor_scalar(gta[:, sl], geq1[:, sl], gp[:, sub:sub + 1],
                                      None, ALU.mult), tta)
                pub(vec.tensor_scalar(geq2[:, sl], gmask[:, sl], gm2[:, sub:sub + 1],
                                      None, ALU.is_equal), tq)
                S.wait(vec, "dv", tq[1])
                ii = vec.scalar_tensor_tensor(
                    wbuf[:, chunk * 32 + sub * 8: chunk * 32 + (sub + 1) * 8],
                    geq2[:, sl], gp[:, 4 + sub:5 + sub], gta[:, sl],
                    ALU.mult, ALU.add)
            pub(ii, g["t_gw"])
        for cev in CEV:
            chunk = cev["chunk"]
            vec.memset(accF[:], 0.0)
            for eev in cev["es"]:
                e = eev["e"]
                for uv in eev["uv"]:
                    hc = uv["hc"]
                    ww = [uv["t_s"], uv["t_v"]]
                    if hc == 0:
                        ww = ww + eev["wh"]
                    ew(vec, ww)
                    pub(vec.tensor_mul(hT[:, hc * 512:(hc + 1) * 512],
                                       sg[hc % 2][:], pf[2 + hc % 2][:, 0:512]),
                        uv["t_h"])
                if dbgA is not None and (e > 0 or chunk > 0):
                    prev_idx = chunk * E + e - 1 if e > 0 else (chunk - 1) * E + E - 1
                    S.wait(vec, "d_dbga", 16 * (prev_idx + 1))
                for y in eev["ys"]:
                    sbk = y["sbk"]
                    wcol = wbuf[:, chunk * 32 + sbk * 8 + e:
                                chunk * 32 + sbk * 8 + e + 1]
                    accv = accF[:, sbk * 1024:(sbk + 1) * 1024]
                    ew(vec, [y["t_ya"]])
                    pub(vec.scalar_tensor_tensor(accv[:, 0:512], pf[y["ba"]][:, 0:512],
                                                 wcol, accv[:, 0:512],
                                                 ALU.mult, ALU.add), y["t_ea"])
                    ew(vec, [y["t_yb"]])
                    pub(vec.scalar_tensor_tensor(accv[:, 512:1024],
                                                 pf[y["bb2"]][:, 0:512],
                                                 wcol, accv[:, 512:1024],
                                                 ALU.mult, ALU.add), y["t_eb"])
            ew(vec, [cev["t_c"]] + cev["wo"])
            vec.tensor_add(ostg[:], cmbx[:], cmba[:])
            pub(vec.tensor_add(ostg[:], ostg[:], accF[:]), cev["t_o"])


def _phase_C2(nc, block, S, W, ctx, RB, pf, pbt, ident_s, ident32_s,
              t_ident, t_id32, Wgf, W1b, W2b, W3b, t_wc,
              XNTs, XNS, ITI, YALL, ATTs, xq, OUT, final_waits, prev):
    sbt = lambda n, shp, dt: ctx.enter_context(nc.sbuf_tensor(n, shp, dt))
    wp1 = sbt("r_w1", [128, NB_T], F32)
    wp2 = sbt("r_w2", [128, NB_T], F32)
    si16 = sbt("r_si", [128, 4 * 256], I16)
    it16 = sbt("r_it", [128, E * (CCAP // 16)], I16)
    toks = {}

    if "G" in C2_STAGES:
        with ExitStack() as c1:
            _c2_gate_route(nc, block, S, W, c1, RB, pf, ident32_s, t_id32,
                           Wgf, XNTs, ITI, YALL, wp1, wp2, si16, it16,
                           toks, prev)
    if "F" in C2_STAGES:
        with ExitStack() as c2:
            _c2_ffn(nc, block, S, W, c2, RB, pf, pbt, ident_s, t_ident,
                    W1b, W2b, W3b, t_wc, XNS, YALL, it16, toks, prev)
    if "M" in C2_STAGES:
        with ExitStack() as c3:
            _c2_combine(nc, block, S, W, c3, ATTs, xq, OUT, si16, wp1, wp2,
                        YALL, toks, final_waits, prev)


def _c2_gate_route(nc, block, S, W, ctx, RB, pf, ident32_s, t_id32,
                   Wgf, XNTs, ITI, YALL, wp1, wp2, si16, it16,
                   toks, prev):
    sbt = lambda n, shp, dt: ctx.enter_context(nc.sbuf_tensor(n, shp, dt))
    ECC = E * CCAP
    xcb = [sbt(f"c_x{i}", [128, 4096], BF16) for i in range(2)]
    wgs = sbt("c_wg", [128, 64], BF16)
    gm1 = sbt("c_g1", [128, 4], F32)
    gm2 = sbt("c_g2", [128, 4], F32)
    gd1 = sbt("c_gd1", [128, 4], F32)
    gd2 = sbt("c_gd2", [128, 4], F32)
    ge = sbt("c_ge", [128, 8], F32)
    gp = sbt("c_gp", [128, 8], F32)
    geq1 = sbt("c_geq1", [128, 32], F32)
    gmask = sbt("c_gmask", [128, 32], F32)
    geq2 = sbt("c_geq2", [128, 32], F32)
    eqT1 = sbt("r_e1", [8, 4096], F32)[:]
    eqT2 = sbt("r_e2", [8, 4096], F32)[:]
    maskT = sbt("r_mk", [8, 4096], F32)[:]
    csum = sbt("r_cs", [8, 4096], F32)[:]
    rov = sbt("r_ov", [8, 4096], F32)[:]
    tixE = sbt("r_tx", [8, 4096], F32)[:]
    sel = sbt("r_sl", [8, 4096], F32)
    rs2 = sbt("r_s2", [8, 4096], F32)
    sel2 = sbt("r_sm", [8, 4096], F32)
    sifs = [sbt(f"r_sf{k}", [128, NB_T], F32) for k in range(4)]
    sifA1, sifA2, sifB1, sifB2 = sifs
    tt = [sbt(f"r_tt{i}", [32, 128], F32) for i in range(2)]
    wt = sbt("r_wt", [16, 4 * 256], F32)
    sfw = [sbt("r_fw0", [16, 96 * 64], F32)]
    ones8 = sbt("r_on", [8, 1], F32)
    ebase = sbt("r_eb", [8, 1], F32)
    ids64 = sbt("r_id", [128, NB_T * 64], F32)
    zt = sbt("r_zt", [128, 1024], F32)

    pre_xnt = []
    if prev:
        pre_xnt = [(nm, S.c[nm]) for nm in ("d_qxnt0", "d_qxnt1") if nm in S.c]

    RXC = RBuf(2)
    RTT = RBuf(2)
    RFW = RBuf(1)

    t_wg = W.pub("d_cwg", 16)
    t_eb = W.pub("po")
    t_tix = W.pub("po")
    t_mz = W.pub("dv")
    t_zt = W.pub("dv")
    t_on = W.pub("dv")
    t_msi = W.pub("dv")
    t_mit = W.pub("dv")
    t_ids = W.pub("po")
    GEV = []
    for chunk in range(NCHUNK):
        kx, wx = RXC.acquire()
        g = dict(chunk=chunk, kx=kx, wx=wx)
        g["t_x"] = W.pub(f"d_cx{kx % 2}", 16)
        RXC.set_prod(kx, g["t_x"])
        kb0, wb0 = RB[0].acquire()
        g["wb0"] = wb0
        t_g = W.pub("pe")
        RB[0].set_prod(kb0, t_g)
        RXC.add_cons(kx, t_g)
        g["t_g"] = t_g
        g["t_ir"] = [tuple(W.pub("dv") for _ in range(4)) for _ in range(4)]
        t_gd1 = W.pub("dv")
        t_gd = W.pub("dv")
        RB[0].add_cons(kb0, t_gd)
        g["t_gd1"], g["t_gd"] = t_gd1, t_gd
        g["t_ges"] = [W.pub("ac") for _ in range(8)]
        g["t_ge"] = g["t_ges"][-1]
        g["t_ga"] = [tuple(W.pub("dv") for _ in range(4)) for _ in range(4)]
        g["t_gp"] = g["t_ga"][-1][-1]
        g["t_wp"] = (W.pub("dv"), W.pub("dv"))
        g["t_gq"] = [W.pub("dv") for _ in range(4)]
        k4, w4 = RB[4].acquire()
        k5, w5 = RB[5].acquire()
        g["w4"], g["w5"] = w4, w5
        g["t_t1"] = W.pub("pe")
        g["t_t2"] = W.pub("pe")
        RB[4].set_prod(k4, g["t_t1"])
        RB[5].set_prod(k5, g["t_t2"])
        g["t_ev1"] = W.pub("dv")
        g["t_ev2"] = W.pub("dv")
        RB[4].add_cons(k4, g["t_ev1"])
        RB[5].add_cons(k5, g["t_ev2"])
        GEV.append(g)
    RT = {}
    kb0, wb0r = RB[0].acquire()
    kb1, wb1r = RB[1].acquire()
    RT["wb0"], RT["wb1"] = wb0r, wb1r
    RT["t_m1"] = W.pub("pe")
    RT["t_m2"] = W.pub("pe")
    RB[0].set_prod(kb0, RT["t_m1"])
    RB[1].set_prod(kb1, RT["t_m2"])
    WRP = []
    for k in range(4):
        kbB, wbB = RB[2 + k % 2].acquire()
        ktt, wtt = RTT.acquire()
        kbS, wbS = RB[k % 2].acquire()
        w = dict(k=k, kbB=kbB, ktt=ktt, kbS=kbS, wbB=wbB, wtt=wtt, wbS=wbS)
        w["t_bt"] = W.pub("pe")
        RB[2 + k % 2].set_prod(kbB, w["t_bt"])
        w["t_sm"] = W.pub("pe")
        RB[k % 2].set_prod(kbS, w["t_sm"])
        RTT.add_cons(ktt, w["t_sm"])
        WRP.append(w)
    for nm in ("mk", "cs", "s1", "ov", "s2", "a", "b", "s3", "sel1", "sel2",
               "eB1", "eB2", "A1", "A2", "B1c", "B2c"):
        RT[nm] = W.pub("dv")
    RB[0].add_cons(kb0, RT["eB1"])
    RB[1].add_cons(kb1, RT["eB2"])
    for k, w in enumerate(WRP):
        w["t_tt"] = W.pub("dv")
        RTT.set_prod(w["ktt"], w["t_tt"])
        RB[2 + k % 2].add_cons(w["kbB"], w["t_tt"])
        w["t_wt"] = [W.pub("dv") for _ in range(8)]
        RB[k % 2].add_cons(w["kbS"], w["t_wt"][7])
    t_c16 = W.pub("dv")
    toks["t_c16"] = t_c16
    t_ite = [W.pub("dv") for _ in range(E)]
    t_itz = [W.pub("d_itz", 16) for _ in range(7)]
    t_sca = [W.pub("d_sca", 16) for _ in range(2)]
    t_rep1 = W.pub("d_rep1", 16)
    t_itw = []
    for e in range(E):
        kf, wf = RFW.acquire()
        t_l = W.pub(f"d_itw{kf % 2}", 16)
        RFW.set_prod(kf, t_l)
        RFW.add_cons(kf, t_ite[e])
        t_itw.append((kf, wf, t_l))
    t_rep2 = W.pub("d_rep2", 16)
    toks["t_itz"] = t_itz[-1]
    toks["t_sca"] = t_sca[-1]
    toks["t_rep1"] = t_rep1
    toks["t_rep2"] = t_rep2

    ew, pub, pub16 = mkemit(S, None)

    @block.sync
    def _(sync):
        ew(sync, pre_xnt)
        for g in GEV:
            chunk = g["chunk"]
            ew(sync, g["wx"])
            pub16(sync.dma_start(
                xcb[g["kx"] % 2][:].rearrange("p (dc s) -> p dc s", s=512),
                XNTs[:, :, chunk * 512:(chunk + 1) * 512]
                .rearrange("dc p s -> p dc s")), g["t_x"])
        ew(sync, [t_c16])
        pub16(sync.dma_start(si16[16:32, :], si16[0:16, :]), t_rep1)
        for e in range(E):
            kf, wf, t_l = t_itw[e]
            ew(sync, [t_sca[1]] + wf)
            pub16(sync.dma_start(
                sfw[0][:].rearrange("q (c x) -> q c x", x=64),
                ITI[e * CCAP:(e + 1) * CCAP, :]
                .rearrange("(c q) x -> q c x", q=16)), t_l)
        ew(sync, [t_ite[E - 1]])
        pub16(sync.dma_start(it16[16:32, :], it16[0:16, :]), t_rep2)

    ew, pub, pub16 = mkemit(S, "po")

    @block.gpsimd
    def _(g):
        pub16(g.dma_start(wgs[:], Wgf), t_wg)
        pub(g.iota(ebase[:], pattern=[[0, 1]], base=0,
                   channel_multiplier=CCAP,
                   allow_small_or_imprecise_dtypes=True), t_eb)
        pub(g.iota(tixE, pattern=[[1, 4096]], base=ECC,
                   channel_multiplier=0,
                   allow_small_or_imprecise_dtypes=True), t_tix)
        ew(g, [t_mz])
        pub(g.iota(ids64[:].rearrange("p (b e) -> p b e", e=64)[:, :, 0:1],
                   pattern=[[128, NB_T], [0, 1]], base=0, channel_multiplier=1,
                   allow_small_or_imprecise_dtypes=True), t_ids)
        g.load_library(MLP_LIB)
        ew(g, [t_zt])
        for c in range(6):
            pub16(g.dma_start(
                ITI[c * 2048:(c + 1) * 2048].rearrange("(c p) e -> p c e",
                                                       p=128),
                zt[:].rearrange("p (c e) -> p c e", e=64)), t_itz[c])
        pub16(g.dma_start(YALL[YTRASH:YTRASH + 128, :], zt[:, 0:1024]),
              t_itz[6])
        ew(g, [t_rep1, t_itz[4]])
        pub16(g.dma_scatter_add(ITI, ids64[:].rearrange("p (b e) -> p b e",
                                                        e=64),
                                si16[:, 512:768], 4096, 4096, 64), t_sca[0])
        pub16(g.dma_scatter_add(ITI, ids64[:].rearrange("p (b e) -> p b e",
                                                        e=64),
                                si16[:, 768:1024], 4096, 4096, 64), t_sca[1])

    ew, pub, _ = mkemit(S, "pe")

    @block.tensor
    def _(te):
        for g in GEV:
            ew(te, [g["t_x"], t_wg] + g["wb0"])
            for sbk in range(4):
                for dcc in range(8):
                    ii = te.matmul(
                        pf[0][:, sbk * 8:(sbk + 1) * 8],
                        xcb[g["kx"] % 2][:, dcc * 512 + sbk * 128:
                                         dcc * 512 + (sbk + 1) * 128],
                        wgs[:, dcc * 8:(dcc + 1) * 8],
                        start=(dcc == 0), stop=(dcc == 7))
            pub(ii, g["t_g"])
            ew(te, [t_id32, g["t_gq"][3]] + g["w4"] + g["w5"])
            for sub in range(4):
                i1 = te.transpose(pf[4][0:8, sub * 128:(sub + 1) * 128],
                                  geq1[:, sub * 8:(sub + 1) * 8], ident32_s[:])
            pub(i1, g["t_t1"])
            for sub in range(4):
                i2 = te.transpose(pf[5][0:8, sub * 128:(sub + 1) * 128],
                                  geq2[:, sub * 8:(sub + 1) * 8], ident32_s[:])
            pub(i2, g["t_t2"])
        ew(te, [RT["sel1"], RT["sel2"], t_on] + RT["wb0"] + RT["wb1"])
        for blk in range(NB_T):
            ii = te.matmul(pf[0][:, blk:blk + 1],
                           sel[0:8, blk * 128:(blk + 1) * 128],
                           ones8[:], start=True, stop=True)
        pub(ii, RT["t_m1"])
        for blk in range(NB_T):
            ii = te.matmul(pf[1][:, blk:blk + 1],
                           sel2[0:8, blk * 128:(blk + 1) * 128],
                           ones8[:], start=True, stop=True)
        pub(ii, RT["t_m2"])
        for w in WRP:
            k = w["k"]
            ew(te, [RT["B2c"]] + w["wbB"])
            pub(te.transpose(pf[2 + k % 2][0:32, 0:128], sifs[k][:],
                             ident32_s[:]), w["t_bt"])
            ew(te, [w["t_tt"]] + w["wbS"] + w["wtt"])
            for gg in range(8):
                ii = te.transpose(pf[k % 2][0:16, gg * 32:(gg + 1) * 32],
                                  tt[k % 2][:, gg * 16:(gg + 1) * 16],
                                  ident32_s[0:32, 0:32])
            pub(ii, w["t_sm"])

    ew, pub, _ = mkemit(S, "ac")

    @block.scalar
    def _(sc):
        for g in GEV:
            ew(sc, [g["t_gd"]])
            for sub in range(4):
                pub(sc.activation(ge[:, sub:sub + 1], gd1[:, sub:sub + 1],
                                  AF.Exp), g["t_ges"][2 * sub])
                pub(sc.activation(ge[:, 4 + sub:5 + sub], gd2[:, sub:sub + 1],
                                  AF.Exp), g["t_ges"][2 * sub + 1])

    ew, pub, _ = mkemit(S, "dv")

    @block.vector
    def _(vec):
        pub(vec.memset(ids64[:], 0.0), t_mz)
        pub(vec.memset(zt[:], 0.0), t_zt)
        pub(vec.memset(ones8[:], 1.0), t_on)
        pub(vec.memset(si16[:], 0), t_msi)
        pub(vec.memset(it16[:], 0), t_mit)
        for g in GEV:
            chunk = g["chunk"]
            ew(vec, [g["t_g"]])
            for sub in range(4):
                lg = pf[0][:, sub * 8:(sub + 1) * 8]
                t1, te1, tm, t2 = g["t_ir"][sub]
                pub(vec.reduce_max(gm1[:, sub:sub + 1], lg, axis=AX.X), t1)
                S.wait(vec, "dv", t1[1])
                pub(vec.tensor_scalar(geq1[:, sub * 8:(sub + 1) * 8], lg,
                                      gm1[:, sub:sub + 1], None, ALU.is_equal),
                    te1)
                S.wait(vec, "dv", te1[1])
                pub(vec.scalar_tensor_tensor(gmask[:, sub * 8:(sub + 1) * 8],
                                             geq1[:, sub * 8:(sub + 1) * 8],
                                             NEG, lg, ALU.mult, ALU.add), tm)
                S.wait(vec, "dv", tm[1])
                pub(vec.reduce_max(gm2[:, sub:sub + 1],
                                   gmask[:, sub * 8:(sub + 1) * 8], axis=AX.X),
                    t2)
                S.wait(vec, "dv", t2[1])
            pub(vec.tensor_sub(gd1[:], gm2[:], gm1[:]), g["t_gd1"])
            pub(vec.tensor_sub(gd2[:], gm1[:], gm2[:]), g["t_gd"])
            ew(vec, [g["t_ge"]])
            for sub in range(4):
                ta1, ta2, tr1, tr2 = g["t_ga"][sub]
                pub(vec.tensor_scalar(gp[:, sub:sub + 1], ge[:, sub:sub + 1],
                                      1.0, None, ALU.add), ta1)
                pub(vec.tensor_scalar(gp[:, 4 + sub:5 + sub],
                                      ge[:, 4 + sub:5 + sub], 1.0, None,
                                      ALU.add), ta2)
                S.wait(vec, "dv", ta2[1])
                pub(vec.reciprocal(gp[:, sub:sub + 1], gp[:, sub:sub + 1]), tr1)
                pub(vec.reciprocal(gp[:, 4 + sub:5 + sub],
                                   gp[:, 4 + sub:5 + sub]), tr2)
            S.wait(vec, "dv", g["t_gp"][1])
            pub(vec.tensor_copy(wp1[:, chunk * 4:(chunk + 1) * 4], gp[:, 0:4]),
                g["t_wp"][0])
            pub(vec.tensor_copy(wp2[:, chunk * 4:(chunk + 1) * 4], gp[:, 4:8]),
                g["t_wp"][1])
            for sub in range(4):
                sl = slice(sub * 8, (sub + 1) * 8)
                pub(vec.tensor_scalar(geq2[:, sl], gmask[:, sl],
                                      gm2[:, sub:sub + 1], None, ALU.is_equal),
                    g["t_gq"][sub])
            ew(vec, [g["t_t1"]])
            pub(vec.tensor_copy(eqT1[:, chunk * 512:(chunk + 1) * 512],
                                pf[4][0:8, 0:512]), g["t_ev1"])
            ew(vec, [g["t_t2"]])
            pub(vec.tensor_copy(eqT2[:, chunk * 512:(chunk + 1) * 512],
                                pf[5][0:8, 0:512]), g["t_ev2"])
        ew(vec, [GEV[-1]["t_ev2"]])
        pub(vec.tensor_add(maskT, eqT1, eqT2), RT["mk"])
        pub(vec.tensor_tensor_scan(csum, maskT, maskT, 0.0,
                                   ALU.add, ALU.bypass), RT["cs"])
        pub(vec.tensor_scalar(rs2[:], csum, -1.0, None, ALU.add), RT["s1"])
        pub(vec.tensor_scalar(rov, csum, float(CCAP), None, ALU.is_gt),
            RT["ov"])
        ew(vec, [t_eb])
        pub(vec.tensor_scalar(rs2[:], rs2[:], ebase[:], None, ALU.add),
            RT["s2"])
        ew(vec, [t_tix])
        pub(vec.tensor_sub(sel[:], tixE, rs2[:]), RT["a"])
        pub(vec.tensor_mul(sel[:], rov, sel[:]), RT["b"])
        pub(vec.tensor_add(rs2[:], rs2[:], sel[:]), RT["s3"])
        pub(vec.tensor_mul(sel[:], eqT1, rs2[:]), RT["sel1"])
        pub(vec.tensor_mul(sel2[:], eqT2, rs2[:]), RT["sel2"])
        ew(vec, [RT["t_m1"]])
        pub(vec.tensor_copy(sifB1[:], pf[0][:, 0:NB_T]), RT["eB1"])
        ew(vec, [RT["t_m2"]])
        pub(vec.tensor_copy(sifB2[:], pf[1][:, 0:NB_T]), RT["eB2"])
        S.wait(vec, "dv", RT["eB2"][1])
        pub(vec.tensor_scalar(sifA1[:], sifB1[:], float(E * CCAP), None,
                              ALU.min), RT["A1"])
        pub(vec.tensor_scalar(sifA2[:], sifB2[:], float(E * CCAP), None,
                              ALU.min), RT["A2"])
        S.wait(vec, "dv", RT["A2"][1])
        pub(vec.tensor_scalar(sifB1[:], sifB1[:], float(ITI_ROWS - 1), None,
                              ALU.min), RT["B1c"])
        pub(vec.tensor_scalar(sifB2[:], sifB2[:], float(ITI_ROWS - 1), None,
                              ALU.min), RT["B2c"])
        for w in WRP:
            k = w["k"]
            ew(vec, [w["t_bt"]])
            pub(vec.tensor_copy(tt[k % 2][:], pf[2 + k % 2][0:32, 0:128]),
                w["t_tt"])
            ew(vec, [w["t_sm"]])
            for gg in range(8):
                pub(vec.tensor_copy(
                    wt[0:16, k * 256:(k + 1) * 256]
                    .rearrange("q (b g) -> q b g", g=8)[:, :, gg:gg + 1],
                    pf[k % 2][0:16, gg * 32:(gg + 1) * 32]
                    .rearrange("q (b x) -> q b x", x=1)), w["t_wt"][gg])
        pub(vec.tensor_copy(si16[0:16, :], wt[0:16, :]), t_c16)
        for e in range(E):
            kf, wf, t_l = t_itw[e]
            ew(vec, [t_l])
            pub(vec.tensor_copy(
                it16[0:16, e * 96:(e + 1) * 96]
                .rearrange("q (c x) -> q c x", x=1),
                sfw[0][:].rearrange("q (c x) -> q c x", x=64)[:, :, 0:1]),
                t_ite[e])


def _c2_ffn(nc, block, S, W, ctx, RB, pf, pbt, ident_s, t_ident,
            W1b, W2b, W3b, t_wc, XNS, YALL, it16, toks, prev):
    sbt = lambda n, shp, dt: ctx.enter_context(nc.sbuf_tensor(n, shp, dt))
    NSLOT = 7
    NSB = CCAP // 128
    slots = [sbt(f"f_w{i}", [128, 4096], BF16) for i in range(NSLOT)]
    xgb = [sbt(f"f_xg{i}", [128, NSB * 1024], BF16) for i in range(2)]
    xgT = sbt("f_xt", [128, DC * CCAP], BF16)
    hTr = sbt("f_h", [128, 16 * CCAP], BF16)
    sg = sbt("f_sg", [128, 512], F32)
    ystg = sbt("f_y", [128, 1024], F32)
    xns_flat = XNS.rearrange("a p d -> (a p) d")

    RW = RBuf(NSLOT)
    RXG = RBuf(2)
    RXT = RBuf(1)
    RH = RBuf(1)
    RSG = RBuf(1)
    RYS = RBuf(1)
    pre_xns = []
    if prev:
        pre_xns = [(nm, S.c[nm]) for nm in
                   ("d_qxns0", "d_qxns1", "d_qxns2", "d_qxns3") if nm in S.c]

    EEV = []
    for e in range(E):
        kxg, wxg = RXG.acquire()
        eev = dict(e=e, kxg=kxg, wxg=wxg)
        eev["t_xga"] = W.pub(f"d_xg{kxg % 2}", 16)
        eev["t_xg"] = W.pub(f"d_xg{kxg % 2}", 16)
        RXG.set_prod(kxg, eev["t_xg"])
        items = [("W1", 0), ("W2", 0), ("W1", 1), ("W2", 1),
                 ("W1", 2), ("W2", 2), ("W1", 3), ("W2", 3),
                 ("W3", 0), ("W3", 1), ("W3", 2), ("W3", 3)]
        slot_of = {}
        eev["wl"] = []
        for wn, q in items:
            kw, wwl = RW.acquire()
            t_l = W.pub(f"d_w{kw % NSLOT}", 16)
            RW.set_prod(kw, t_l)
            eev["wl"].append(dict(wn=wn, q=q, slot=kw % NSLOT, kw=kw, w=wwl,
                                  t=t_l))
            slot_of[(wn, q)] = (kw % NSLOT, kw, t_l)
        kxt, wxt = RXT.acquire()
        eev["wxt"] = wxt
        eev["tps"] = []
        for blk in range(NSB):
            ktb, wtb = RB[6 + blk % 2].acquire()
            t_tp = W.pub("pe")
            RB[6 + blk % 2].set_prod(ktb, t_tp)
            t_ev = W.pub("ac")
            RB[6 + blk % 2].add_cons(ktb, t_ev)
            eev["tps"].append(dict(blk=blk, wtb=wtb, t_tp=t_tp, t_ev=t_ev))
        RXT.set_prod(kxt, eev["tps"][-1]["t_ev"])
        RXG.add_cons(kxg, eev["tps"][-1]["t_tp"])
        eev["uv"] = []
        cnt = 0
        for hc in range(16):
            for stile in range(3):
                q, hq = hc // 4, hc % 4
                uv = dict(hc=hc, stile=stile, q=q, hq=hq,
                          w1=slot_of[("W1", q)], w2=slot_of[("W2", q)])
                kU, wU = RB[cnt % 2].acquire()
                kV, wV = RB[2 + cnt % 2].acquire()
                uv["bu"], uv["bv"] = cnt % 2, 2 + cnt % 2
                uv["wU"], uv["wV"] = wU, wV
                t_u = W.pub("pe")
                t_v = W.pub("pe")
                RB[cnt % 2].set_prod(kU, t_u)
                RB[2 + cnt % 2].set_prod(kV, t_v)
                uv["t_u"], uv["t_v"] = t_u, t_v
                if hc % 4 == 3 and stile == 2:
                    RW.add_cons(slot_of[("W1", q)][1], t_u)
                    RW.add_cons(slot_of[("W2", q)][1], t_v)
                ksg, wsg = RSG.acquire()
                uv["wsg"] = wsg
                t_s = W.pub("ac")
                RSG.set_prod(ksg, t_s)
                RB[cnt % 2].add_cons(kU, t_s)
                uv["t_s"] = t_s
                t_h = W.pub("dv")
                RSG.add_cons(ksg, t_h)
                RB[2 + cnt % 2].add_cons(kV, t_h)
                uv["t_h"] = t_h
                uv["cnt"] = cnt
                cnt += 1
                eev["uv"].append(uv)
        kh, wh = RH.acquire()
        eev["wh"] = wh
        RH.set_prod(kh, eev["uv"][-1]["t_h"])
        eev["ys"] = []
        for sb in range(NSB):
            ka, wa = RB[4].acquire()
            kb2, wb2 = RB[5].acquire()
            y = dict(sb=sb, wa=wa, wb=wb2,
                     w3s=[slot_of[("W3", q)] for q in range(4)])
            t_ya = W.pub("pe")
            t_yb = W.pub("pe")
            RB[4].set_prod(ka, t_ya)
            RB[5].set_prod(kb2, t_yb)
            y["t_ya"], y["t_yb"] = t_ya, t_yb
            if sb == NSB - 1:
                for q in range(4):
                    RW.add_cons(slot_of[("W3", q)][1], t_yb)
                RH.add_cons(kh, t_yb)
                RXT.add_cons(kxt, t_yb)
            kys, wys = RYS.acquire()
            y["wys"] = wys
            t_e1 = W.pub("dv")
            t_e2 = W.pub("dv")
            RB[4].add_cons(ka, t_e1)
            RB[5].add_cons(kb2, t_e2)
            RYS.set_prod(kys, t_e2)
            y["t_e1"], y["t_e2"] = t_e1, t_e2
            t_st = W.pub(f"d_ya{sb % 2}", 16)
            RYS.add_cons(kys, t_st)
            y["t_st"] = t_st
            eev["ys"].append(y)
        EEV.append(eev)
    toks["t_ya"] = [EEV[-1]["ys"][-2]["t_st"], EEV[-1]["ys"][-1]["t_st"]]

    ew, pub, pub16 = mkemit(S, "po")

    @block.gpsimd
    def _(g):
        for eev in EEV:
            e = eev["e"]
            ww = [toks["t_rep2"]] + eev["wxg"]
            if e == 0:
                ww += pre_xns
            ew(g, ww)
            pub16(g.dma_gather(
                xgb[eev["kxg"] % 2][:, 0:6 * D]
                .rearrange("p (b d) -> p b d", d=D),
                xns_flat, it16[:, e * 96:e * 96 + 48], 768, 768, D),
                eev["t_xga"])
            pub16(g.dma_gather(
                xgb[eev["kxg"] % 2][:, 6 * D:12 * D]
                .rearrange("p (b d) -> p b d", d=D),
                xns_flat, it16[:, e * 96 + 48:e * 96 + 96], 768, 768, D),
                eev["t_xg"])
            for wl in eev["wl"]:
                ww = list(wl["w"])
                if e == 0 and wl["kw"] < NSLOT:
                    ww += t_wc
                ew(g, ww)
                slot, q = wl["slot"], wl["q"]
                if wl["wn"] == "W1":
                    src = W1b[e, q // 2].rearrange("p (dc h) -> p dc h", h=1024)[
                        :, :, (q % 2) * 512:(q % 2 + 1) * 512]
                    dst = slots[slot][:].rearrange("p (dc h) -> p dc h", h=512)
                elif wl["wn"] == "W2":
                    src = W2b[e, q // 2].rearrange("p (dc h) -> p dc h", h=1024)[
                        :, :, (q % 2) * 512:(q % 2 + 1) * 512]
                    dst = slots[slot][:].rearrange("p (dc h) -> p dc h", h=512)
                else:
                    src = W3b[e, q * 4:(q + 1) * 4].rearrange("hc p d -> p hc d")
                    dst = slots[slot][:].rearrange("p (hc d) -> p hc d", d=D)
                pub16(g.dma_start(dst, src), wl["t"])

    ew, pub, pub16 = mkemit(S, None)

    @block.sync
    def _(sync):
        for eev in EEV:
            e = eev["e"]
            for y in eev["ys"]:
                ew(sync, [y["t_e2"]])
                pub16(sync.dma_start(
                    YALL[e * CCAP + y["sb"] * 128:
                         e * CCAP + (y["sb"] + 1) * 128, :],
                    ystg[:]), y["t_st"])

    ew, pub, _ = mkemit(S, "pe")

    @block.tensor
    def _(te):
        for eev in EEV:
            for tp in eev["tps"]:
                blk = tp["blk"]
                ww = list(tp["wtb"])
                if blk == 0:
                    ww += [eev["t_xg"], t_ident] + eev["wxt"]
                ew(te, ww)
                for dcc in range(8):
                    ii = te.transpose(
                        pbt[blk % 2][:, dcc * 128:(dcc + 1) * 128],
                        xgb[eev["kxg"] % 2][:, blk * 1024 + dcc * 128:
                                            blk * 1024 + (dcc + 1) * 128],
                        ident_s[:])
                pub(ii, tp["t_tp"])
            for uv in eev["uv"]:
                stile, hq = uv["stile"], uv["hq"]
                ww = list(uv["wU"])
                if uv["cnt"] == 0:
                    ww += [eev["tps"][-1]["t_ev"]]
                if stile == 0:
                    ww += [uv["w1"][2], uv["w2"][2]]
                ew(te, ww)
                s1 = uv["w1"][0]
                for dcc in range(8):
                    ii = te.matmul(
                        pf[uv["bu"]][:, 0:512],
                        slots[s1][:, dcc * 512 + hq * 128:
                                  dcc * 512 + (hq + 1) * 128],
                        xgT[:, dcc * CCAP + stile * 512:
                            dcc * CCAP + (stile + 1) * 512],
                        start=(dcc == 0), stop=(dcc == 7))
                pub(ii, uv["t_u"])
                ew(te, uv["wV"])
                s2 = uv["w2"][0]
                for dcc in range(8):
                    ii = te.matmul(
                        pf[uv["bv"]][:, 0:512],
                        slots[s2][:, dcc * 512 + hq * 128:
                                  dcc * 512 + (hq + 1) * 128],
                        xgT[:, dcc * CCAP + stile * 512:
                            dcc * CCAP + (stile + 1) * 512],
                        start=(dcc == 0), stop=(dcc == 7))
                pub(ii, uv["t_v"])
            for y in eev["ys"]:
                sb = y["sb"]
                ww = list(y["wa"]) + list(y["wb"])
                if sb == 0:
                    ww += [eev["uv"][-1]["t_h"]] + [w3[2] for w3 in y["w3s"]]
                ew(te, ww)
                for hc in range(16):
                    q, hq = hc // 4, hc % 4
                    w3s = y["w3s"][q][0]
                    lhs = hTr[:, hc * CCAP + sb * 128:
                              hc * CCAP + (sb + 1) * 128]
                    i1 = te.matmul(pf[4][:, 0:512], lhs,
                                   slots[w3s][:, hq * 1024:hq * 1024 + 512],
                                   start=(hc == 0), stop=(hc == 15))
                    i2 = te.matmul(pf[5][:, 0:512], lhs,
                                   slots[w3s][:, hq * 1024 + 512:
                                              (hq + 1) * 1024],
                                   start=(hc == 0), stop=(hc == 15))
                pub(i1, y["t_ya"])
                pub(i2, y["t_yb"])

    ew, pub, _ = mkemit(S, "ac")

    @block.scalar
    def _(sc):
        for eev in EEV:
            for tp in eev["tps"]:
                blk = tp["blk"]
                ew(sc, [tp["t_tp"]])
                xv = xgT[:].rearrange("p (dc s) -> p dc s", s=CCAP)
                pub(sc.copy(xv[:, :, blk * 128:(blk + 1) * 128],
                            pbt[blk % 2][:].rearrange("p (dc s) -> p dc s",
                                                      s=128)), tp["t_ev"])
            for uv in eev["uv"]:
                ew(sc, [uv["t_u"]] + uv["wsg"])
                pub(sc.activation(sg[:], pf[uv["bu"]][:, 0:512], AF.Silu),
                    uv["t_s"])

    ew, pub, _ = mkemit(S, "dv")

    @block.vector
    def _(vec):
        for eev in EEV:
            for uv in eev["uv"]:
                hc, stile = uv["hc"], uv["stile"]
                ww = [uv["t_s"], uv["t_v"]]
                if uv["cnt"] == 0:
                    ww += eev["wh"]
                ew(vec, ww)
                pub(vec.tensor_mul(
                    hTr[:, hc * CCAP + stile * 512:
                        hc * CCAP + stile * 512 + 512],
                    sg[:], pf[uv["bv"]][:, 0:512]), uv["t_h"])
            for y in eev["ys"]:
                ew(vec, [y["t_ya"]] + y["wys"])
                pub(vec.tensor_copy(ystg[:, 0:512], pf[4][:, 0:512]),
                    y["t_e1"])
                ew(vec, [y["t_yb"]])
                pub(vec.tensor_copy(ystg[:, 512:1024], pf[5][:, 0:512]),
                    y["t_e2"])


def _c2_combine(nc, block, S, W, ctx, ATTs, xq, OUT, si16, wp1, wp2,
                YALL, toks, final_waits, prev):
    sbt = lambda n, shp, dt: ctx.enter_context(nc.sbuf_tensor(n, shp, dt))
    cmbx = [sbt(f"m_x{i}", [128, 4096], F32) for i in range(2)]
    cmba = [sbt(f"m_a{i}", [128, 4096], F32) for i in range(2)]
    yg1 = [sbt(f"m_y1{i}", [128, 4096], F32) for i in range(2)]
    yg2 = [sbt(f"m_y2{i}", [128, 4096], F32) for i in range(2)]
    base = sbt("m_b", [128, 4096], F32)
    ostg = [sbt(f"m_o{i}", [128, 4096], F32) for i in range(2)]

    RCM = RBuf(2)
    RYG = RBuf(2)
    ROS = RBuf(2)
    bat_pre = [(nm, S.c[nm]) for nm in ("d_bat0", "d_bat1") if nm in S.c]

    MEV = []
    for ch in range(NCHUNK):
        kc, wc = RCM.acquire()
        m = dict(ch=ch, kc=kc, wc=wc)
        W.pub(f"d_cm{kc % 2}", 16)
        m["t_c"] = W.pub(f"d_cm{kc % 2}", 16)
        RCM.set_prod(kc, m["t_c"])
        kg, wg_ = RYG.acquire()
        m["kg"], m["wg"] = kg, wg_
        m["t_g1"] = W.pub(f"d_g1{kg % 2}", 16)
        m["t_g2"] = W.pub(f"d_g2{kg % 2}", 16)
        RYG.set_prod(kg, m["t_g2"])
        ko_, wo = ROS.acquire()
        m["ko"], m["wo"] = ko_, wo
        m["t_bs"] = W.pub("dv")
        m["t_os"] = [tuple(W.pub("dv") for _ in range(2)) for _ in range(4)]
        RCM.add_cons(kc, m["t_bs"])
        t_o = m["t_os"][3][1]
        RYG.add_cons(kg, t_o)
        ROS.set_prod(ko_, t_o)
        m["t_st"] = W.pub(f"d_co{ko_ % 2}", 16)
        ROS.add_cons(ko_, m["t_st"])
        MEV.append(m)

    ew, pub, pub16 = mkemit(S, None)

    @block.sync
    def _(sync):
        ew(sync, bat_pre)
        for m in MEV:
            ch, kc = m["ch"], m["kc"]
            ew(sync, m["wc"])
            S.inc(sync.dma_start(cmbx[kc % 2][:], xq[ch]), f"d_cm{kc % 2}", 16)
            pub16(sync.dma_start(
                cmba[kc % 2][:].rearrange("p (sb d) -> p sb d", d=1024),
                ATTs[ch * 512:(ch + 1) * 512, :]
                .rearrange("(sb p) d -> p sb d", p=128)), m["t_c"])

    ew, pub, pub16 = mkemit(S, "po")

    @block.gpsimd
    def _(g):
        for m in MEV:
            ch, kg = m["ch"], m["kg"]
            ww = list(m["wg"]) + [toks["t_rep1"]]
            if ch == 0:
                ww += toks["t_ya"] + [toks["t_itz"]]
            ew(g, ww)
            pub16(g.dma_gather(
                yg1[kg % 2][:].rearrange("p (b d) -> p b d", d=D),
                YALL, si16[:, ch * 32:(ch + 1) * 32], 512, 512, D), m["t_g1"])
            pub16(g.dma_gather(
                yg2[kg % 2][:].rearrange("p (b d) -> p b d", d=D),
                YALL, si16[:, 256 + ch * 32:256 + (ch + 1) * 32], 512, 512, D),
                m["t_g2"])
            if ch > 0:
                mp = MEV[ch - 1]
                ew(g, [mp["t_os"][3][1]])
                pub16(g.dma_start(OUT[mp["ch"]], ostg[mp["ko"] % 2][:]),
                      mp["t_st"])
        mp = MEV[-1]
        ew(g, [mp["t_os"][3][1]])
        pub16(g.dma_start(OUT[mp["ch"]], ostg[mp["ko"] % 2][:]), mp["t_st"])
    final_waits.append(("d_co0", S.c["d_co0"]))
    final_waits.append(("d_co1", S.c["d_co1"]))

    ew, pub, _ = mkemit(S, "dv")

    @block.vector
    def _(vec):
        for m in MEV:
            ch, kc, kg, ko_ = m["ch"], m["kc"], m["kg"], m["ko"]
            ew(vec, [m["t_c"]] + m["wo"])
            pub(vec.tensor_add(base[:], cmbx[kc % 2][:], cmba[kc % 2][:]),
                m["t_bs"])
            ew(vec, [m["t_g1"], m["t_g2"]])
            for b in range(4):
                t1, t2 = m["t_os"][b]
                w1c = wp1[:, ch * 4 + b:ch * 4 + b + 1]
                w2c = wp2[:, ch * 4 + b:ch * 4 + b + 1]
                pub(vec.scalar_tensor_tensor(
                    ostg[ko_ % 2][:, b * 1024:(b + 1) * 1024],
                    yg1[kg % 2][:, b * 1024:(b + 1) * 1024], w1c,
                    base[:, b * 1024:(b + 1) * 1024], ALU.mult, ALU.add), t1)
                pub(vec.scalar_tensor_tensor(
                    ostg[ko_ % 2][:, b * 1024:(b + 1) * 1024],
                    yg2[kg % 2][:, b * 1024:(b + 1) * 1024], w2c,
                    ostg[ko_ % 2][:, b * 1024:(b + 1) * 1024],
                    ALU.mult, ALU.add), t2)


def kernel(L0_attn_x, L0_attn_wq, L0_attn_wk, L0_attn_wv,
           L0_ffn_Wg, L0_ffn_W1, L0_ffn_W2, L0_ffn_W3):
    per_core, row_idx = host_layouts(
        L0_attn_x, L0_attn_wq, L0_attn_wk, L0_attn_wv,
        L0_ffn_Wg, L0_ffn_W1, L0_ffn_W2, L0_ffn_W3)
    nc = build_program()
    in_maps = [dict(per_core[c]) for c in range(NCORE)]
    res = run_bass_kernel_spmd(nc, in_maps, core_ids=list(range(NCORE)))
    out = np.empty((BATCH, SEQ, D), dtype=np.float32)
    for c in range(NCORE):
        oc = np.asarray(res.results[c]["out"])
        oc = oc.reshape(NCHUNK, 128, 4, D).transpose(0, 2, 1, 3).reshape(QROWS, D)
        out[c // 2][row_idx[c]] = oc
    return out


# version 05 (final)
# speedup vs baseline: 1.6511x; 1.0058x over previous
import numpy as np
import ml_dtypes
import concourse.bass as bass
import concourse.mybir as mybir
from contextlib import ExitStack
from concourse.library_config import mlp as MLP_LIB
from concourse.bass_utils import run_bass_kernel_spmd

F32 = mybir.dt.float32
BF16 = mybir.dt.bfloat16
I32 = mybir.dt.int32
I16 = mybir.dt.int16
AF = mybir.ActivationFunctionType
ALU = mybir.AluOpType
AX = mybir.AxisListType

D = 1024
E = 8
SEQ = 8192
BATCH = 4
NCORE = 8
QROWS = 4096
NJ = 8
NCHUNK = 8
DC = 8
NEG = -1.0e30
EPS = 1e-5
SCL = 1.0 / 32.0
DEBUG_C = False
C2_STAGES = ("G", "F", "M")
ROUTED = True
CCAP = 1536
NB_T = 32
ITI_ROWS = 8 * 1536 + 4096
YTRASH = 8 * 1536


class SemBook:
    def __init__(self, nc, stack):
        self.nc, self.stack = nc, stack
        self.h, self.c = {}, {}

    def sem(self, name):
        if name not in self.h:
            self.h[name] = self.stack.enter_context(self.nc.semaphore(name))
            self.c[name] = 0
        return self.h[name]

    def inc(self, instr, name, n=1):
        instr.then_inc(self.sem(name), n)
        self.c[name] += n
        return self.c[name]

    def wait(self, eng, name, val):
        eng.wait_ge(self.sem(name), val)


class Walk:

    def __init__(self):
        self.cnt = {}

    def pub(self, sem, n=1):
        v = self.cnt.get(sem, 0) + n
        self.cnt[sem] = v
        return (sem, v)


class RWait:

    def __init__(self, buf, k):
        self.buf, self.k = buf, k

    def resolve(self):
        b, k = self.buf, self.k
        if k < b.nbuf:
            return []
        u = b.uses[k - b.nbuf]
        out = []
        if u["prod"] is not None:
            out.append(u["prod"])
        out.extend(u["cons"])
        return out


class RBuf:

    def __init__(self, nbuf=1):
        self.nbuf = nbuf
        self.uses = []

    def acquire(self):
        k = len(self.uses)
        self.uses.append({"prod": None, "cons": []})
        return k, [RWait(self, k)]

    def set_prod(self, k, tok):
        self.uses[k]["prod"] = tok

    def add_cons(self, k, tok):
        self.uses[k]["cons"].append(tok)


def mkemit(S, own):
    def emitw(eng, waits):
        best = {}
        flat = []
        for t in waits:
            if t is None:
                continue
            if isinstance(t, RWait):
                flat.extend(t.resolve())
            else:
                flat.append(t)
        for sem, v in flat:
            if sem == own:
                continue
            if best.get(sem, 0) < v:
                best[sem] = v
        for sem, v in best.items():
            S.wait(eng, sem, v)

    def pub(instr, tok):
        got = S.inc(instr, tok[0], 1)
        assert got == tok[1], (tok, got)

    def pub16(instr, tok):
        got = S.inc(instr, tok[0], 16)
        assert got == tok[1], (tok, got)
    return emitw, pub, pub16


def _rope_tables_np(m, d):
    i = np.arange(d // 2, dtype=np.float32)
    phi = np.power(np.float32(10000.0),
                   np.float32(-2.0) * (i - np.float32(1.0)) / np.float32(d))
    pos = np.arange(m, dtype=np.float32)
    ang = pos[:, None] * phi[None, :]
    return np.cos(ang, dtype=np.float32), np.sin(ang, dtype=np.float32)


def host_layouts(L0_attn_x, L0_attn_wq, L0_attn_wk, L0_attn_wv,
                 L0_ffn_Wg, L0_ffn_W1, L0_ffn_W2, L0_ffn_W3):
    x = np.ascontiguousarray(np.asarray(L0_attn_x, dtype=np.float32))
    wq = np.asarray(L0_attn_wq, dtype=np.float32)
    wk = np.asarray(L0_attn_wk, dtype=np.float32)
    wv = np.asarray(L0_attn_wv, dtype=np.float32)
    Wg = np.asarray(L0_ffn_Wg, dtype=np.float32)
    W1 = np.asarray(L0_ffn_W1, dtype=np.float32)
    W2 = np.asarray(L0_ffn_W2, dtype=np.float32)
    W3 = np.asarray(L0_ffn_W3, dtype=np.float32)

    perm = np.concatenate([np.arange(0, D, 2), np.arange(1, D, 2)])

    def wT_lay(w, permute):
        wt = (w[perm] if permute else w).T
        return np.ascontiguousarray(
            wt.reshape(DC, 128, D).transpose(1, 0, 2).reshape(128, DC * D))

    wqT, wkT, wvT = wT_lay(wq, True), wT_lay(wk, True), wT_lay(wv, False)

    W1h = np.ascontiguousarray(
        W1.reshape(E, DC, 128, 2, 1024).transpose(0, 3, 2, 1, 4).reshape(E, 2, 128, 8192))
    W2h = np.ascontiguousarray(
        W2.reshape(E, DC, 128, 2, 1024).transpose(0, 3, 2, 1, 4).reshape(E, 2, 128, 8192))
    W3h = np.ascontiguousarray(W3.reshape(E, 16, 128, D))
    Wgh = np.ascontiguousarray(
        Wg.reshape(DC, 128, E).transpose(1, 0, 2).reshape(128, DC * E))

    def rows_lay(a):
        n = a.shape[0] // 512
        return np.ascontiguousarray(
            a.reshape(n, 4, 128, D).transpose(0, 2, 1, 3).reshape(n, 128, 4 * D))

    cosT, sinT = _rope_tables_np(SEQ, D)
    cosK = np.ascontiguousarray(cosT.T.reshape(4, 128, SEQ))
    sinK = np.ascontiguousarray(sinT.T.reshape(4, 128, SEQ))
    ident = np.eye(128, dtype=ml_dtypes.bfloat16)
    ident32 = np.eye(128, dtype=np.float32)

    per_core, row_idx = [], {}
    for c in range(NCORE):
        b, h = c // 2, c % 2
        rows = np.concatenate(
            [np.arange(J * 1024 + h * 512, J * 1024 + h * 512 + 512) for J in range(NJ)])
        row_idx[c] = rows
        xqc = np.ascontiguousarray(x[b][rows])
        cosQ = np.ascontiguousarray(cosK[:, :, rows] * np.float32(SCL))
        sinQ = np.ascontiguousarray(sinK[:, :, rows] * np.float32(SCL))
        qpos = (h * 512 + np.arange(512))[:, None]
        kpos = np.arange(1024)[None, :]
        mask = np.where(kpos <= qpos, np.float32(0.0), np.float32(NEG))
        mask = np.ascontiguousarray(
            mask.reshape(4, 128, 1024).transpose(1, 0, 2).reshape(128, 4096))
        per_core.append(dict(
            xb=rows_lay(x[b]), xq=rows_lay(xqc), cosQ=cosQ, sinQ=sinQ, mask=mask,
            wqT=wqT, wkT=wkT, wvT=wvT, cosK=cosK, sinK=sinK,
            W1h=W1h, W2h=W2h, W3h=W3h, Wgh=Wgh, ident=ident,
            ident32=ident32))
    return per_core, row_idx


def build_program(phases=("A", "B", "C"), taps=()):
    nc = bass.Bass("TRN2", target_bir_lowering=False)
    final_waits = []

    def scratch(name, shape, dtype, produced_by):
        if produced_by not in phases:
            kind = "ExternalInput"
        elif name in taps:
            kind = "ExternalOutput"
        else:
            kind = "Internal"
        return nc.dram_tensor(name, shape, dtype, kind=kind).ap()

    xb = nc.dram_tensor("xb", [16, 128, 4096], F32, kind="ExternalInput").ap()
    xq = nc.dram_tensor("xq", [8, 128, 4096], F32, kind="ExternalInput").ap()
    wqT = nc.dram_tensor("wqT", [128, 8192], F32, kind="ExternalInput").ap()
    wkT = nc.dram_tensor("wkT", [128, 8192], F32, kind="ExternalInput").ap()
    wvT = nc.dram_tensor("wvT", [128, 8192], F32, kind="ExternalInput").ap()
    cosK = nc.dram_tensor("cosK", [4, 128, SEQ], F32, kind="ExternalInput").ap()
    sinK = nc.dram_tensor("sinK", [4, 128, SEQ], F32, kind="ExternalInput").ap()
    cosQ = nc.dram_tensor("cosQ", [4, 128, QROWS], F32, kind="ExternalInput").ap()
    sinQ = nc.dram_tensor("sinQ", [4, 128, QROWS], F32, kind="ExternalInput").ap()
    maskd = nc.dram_tensor("mask", [128, 4096], F32, kind="ExternalInput").ap()
    W1f = nc.dram_tensor("W1h", [E, 2, 128, 8192], F32, kind="ExternalInput").ap()
    W2f = nc.dram_tensor("W2h", [E, 2, 128, 8192], F32, kind="ExternalInput").ap()
    W3f = nc.dram_tensor("W3h", [E, 16, 128, D], F32, kind="ExternalInput").ap()
    Wgf = nc.dram_tensor("Wgh", [128, DC * E], F32, kind="ExternalInput").ap()
    ident_d = nc.dram_tensor("ident", [128, 128], BF16, kind="ExternalInput").ap()

    KTs = scratch("KTs", [DC, 128, SEQ], BF16, "A")
    Vs = scratch("Vs", [SEQ, D], BF16, "A")
    QTs = scratch("QTs", [DC, 128, QROWS], BF16, "A")
    XNTs = scratch("XNTs", [DC, 128, QROWS], BF16, "A")
    ATTs = scratch("ATTs", [QROWS, D], F32, "B")
    W1b = nc.dram_tensor("W1b", [E, 2, 128, 8192], BF16, kind="Internal").ap()
    W2b = nc.dram_tensor("W2b", [E, 2, 128, 8192], BF16, kind="Internal").ap()
    W3b = nc.dram_tensor("W3b", [E, 16, 128, D], BF16, kind="Internal").ap()
    if ROUTED:
        XNS = scratch("XNS", [32, 128, D], BF16, "A")
        ITI = nc.dram_tensor("ITI", [ITI_ROWS, 64], F32, kind="Internal").ap()
        YALL = nc.dram_tensor("YALL", [E * CCAP + 128, D], F32, kind="Internal").ap()
    ident32_d = nc.dram_tensor("ident32", [128, 128], F32, kind="ExternalInput").ap()
    OUT = nc.dram_tensor("out", [NCHUNK, 128, 4096], F32, kind="ExternalOutput").ap()
    dbgH = dbgA = dbgW = None
    dbgG = None
    if DEBUG_C:
        dbgH = nc.dram_tensor("dbgH", [NCHUNK * E, 128, 8192], BF16,
                              kind="ExternalOutput").ap()
        dbgA = nc.dram_tensor("dbgA", [NCHUNK * E, 128, 4096], F32,
                              kind="ExternalOutput").ap()
        dbgW = nc.dram_tensor("dbgW", [128, 256], F32,
                              kind="ExternalOutput").ap()
        dbgG = {nm: nc.dram_tensor(nm, [128, wdt], F32, kind="ExternalOutput").ap()
                for nm, wdt in (("dbgm1", 4), ("dbgm2", 4), ("dbgd1", 4),
                                ("dbgd2", 4), ("dbge", 8), ("dbgp", 8),
                                ("dbgq1", 32), ("dbgmk", 32))}

    with ExitStack() as top:
        S = SemBook(nc, top)
        W = Walk()
        for s in ("pe", "dv", "ac", "po"):
            S.sem(s)
        block = top.enter_context(nc.Block())
        pf = [top.enter_context(nc.psum_tensor(f"pf{i}", [128, 512], F32))
              for i in range(6)]
        pbt = [top.enter_context(nc.psum_tensor(f"pbt{i}", [128, 1024], BF16))
               for i in range(2)]
        RB = [RBuf(1) for _ in range(8)]
        ident_s = top.enter_context(nc.sbuf_tensor("ident_s", [128, 128], BF16))

        ident32_s = top.enter_context(nc.sbuf_tensor("ident32_s", [128, 128], F32))

        t_ident = W.pub("d_id0", 16)
        t_id32 = W.pub("d_id1", 16)
        t_wc = None
        if "C" in phases:
            t_wc = [W.pub("d_wc0", 16), W.pub("d_wc1", 16), W.pub("d_wc2", 16)]

        @block.gpsimd
        def _(g):
            S.inc(g.dma_start(ident_s[:], ident_d), "d_id0", 16)
            S.inc(g.dma_start(ident32_s[:], ident32_d), "d_id1", 16)
            if "C" in phases:
                S.inc(g.dma_start(W1b, W1f), "d_wc0", 16)
                S.inc(g.dma_start(W2b, W2f), "d_wc1", 16)
                S.inc(g.dma_start(W3b, W3f), "d_wc2", 16)

        if "A" in phases:
            with ExitStack() as ctx:
                _phase_A(nc, block, S, W, ctx, RB, pf, pbt, ident_s, t_ident,
                         xb, xq, wqT, wkT, wvT, cosK, sinK, cosQ, sinQ,
                         KTs, Vs, QTs, XNTs, final_waits,
                         XNS=(XNS if ROUTED else None))
        if "B" in phases:
            with ExitStack() as ctx:
                _phase_B(nc, block, S, W, ctx, RB, pf, pbt, ident_s, t_ident,
                         KTs, Vs, QTs, maskd, ATTs, final_waits,
                         a_done=("A" in phases))
        if "C" in phases:
            if ROUTED:
                with ExitStack() as ctx:
                    _phase_C2(nc, block, S, W, ctx, RB, pf, pbt, ident_s,
                              ident32_s, t_ident, t_id32, Wgf, W1b, W2b, W3b,
                              t_wc, XNTs, XNS, ITI, YALL, ATTs, xq, OUT,
                              final_waits, prev=("B" in phases))
            else:
                with ExitStack() as ctx:
                    _phase_C(nc, block, S, W, ctx, RB, pf, Wgf, t_wc,
                             W1b, W2b, W3b, XNTs, ATTs, xq, OUT, final_waits,
                             prev=("B" in phases), dbgH=dbgH, dbgA=dbgA,
                             dbgW=dbgW, dbgG=dbgG)

        @block.sync
        def _(sync):
            for nm, val in final_waits:
                S.wait(sync, nm, val)
    mybir.codegen_inst_isa_subclasses(nc)
    return nc


def _phase_A(nc, block, S, W, ctx, RB, pf, pbt, ident_s, t_ident,
             xb, xq, wqT, wkT, wvT, cosK, sinK, cosQ, sinQ,
             KTs, Vs, QTs, XNTs, final_waits, XNS=None):
    sbt = lambda n, shp, dt: ctx.enter_context(nc.sbuf_tensor(n, shp, dt))
    xbuf = [sbt(f"a_x{i}", [128, 4096], F32) for i in range(2)]
    cbuf = [sbt(f"a_c{i}", [128, 2048], F32) for i in range(2)]
    sbin = [sbt(f"a_s{i}", [128, 2048], F32) for i in range(2)]
    xnb = [sbt(f"a_xnb{i}", [128, 1024], BF16) for i in range(4)]
    xnt = [sbt(f"a_xnt{i}", [128, 4096], BF16) for i in range(2)]
    ko = [sbt(f"a_ko{i}", [128, 4096], BF16) for i in range(2)]
    vo = [sbt(f"a_vo{i}", [128, 1024], BF16) for i in range(2)]
    rt = [sbt(f"a_rt{i}", [128, 512], F32) for i in range(4)]
    bnst = [sbt(f"a_bn{i}", [128, 12], F32) for i in range(2)]
    mv = [sbt(f"a_mv{i}", [128, 2], F32) for i in range(2)]
    rstd = [sbt(f"a_rs{i}", [128, 1], F32) for i in range(2)]
    wqs = sbt("a_wq", [128, 8192], BF16)
    wks = sbt("a_wk", [128, 8192], BF16)
    wvs = sbt("a_wv", [128, 8192], BF16)
    eps_t = sbt("a_eps", [128, 1], F32)

    t_eps = W.pub("dv")

    @block.vector
    def _(vec):
        got = S.inc(vec.memset(eps_t[:], EPS), "dv", 1)
        assert got == t_eps[1]

    t_aw = [W.pub(f"d_aw{i}", 16) for i in range(3)]

    @block.gpsimd
    def _(g):
        for i, (wsb, wdr) in enumerate(((wks, wkT), (wvs, wvT), (wqs, wqT))):
            S.inc(g.dma_start(wsb[:], wdr), f"d_aw{i}", 16)

    def one_pass(tag, nst, src_x, cos_d, sin_d, wproj, do_v, do_q,
                 kt_dst, v_dst, xnt_dst):
        RX = RBuf(2)
        RBN = RBuf(2)
        RMV = RBuf(2)
        RRS = RBuf(2)
        RXNB = RBuf(4)
        RXNT = RBuf(2)
        RKO = RBuf(2)
        RVO = RBuf(2)
    

        EV = []
        for st in range(nst):
            par = st % 2
            ev = dict(st=st, par=par)
            k, w = RX.acquire()
            ev["x"] = (k, w)
            dxp = f"d_{tag}x{k % 2}"
            W.pub(dxp, 16)
            W.pub(dxp, 16)
            tkx = W.pub(dxp, 16)
            RX.set_prod(k, tkx)
            ev["t_x"] = tkx
            ev["tiles"] = []
            for t in range(4):
                tp = t % 2
                tl = dict(t=t, tp=tp)
                kb, wb = RBN.acquire()
                tl["bn"] = (kb, wb)
                tl["t_bn0"] = W.pub("dv")
                tl["t_bn"] = W.pub("dv")
                km, wm = RMV.acquire()
                tl["mvw"] = wm
                t_ag = W.pub("dv")
                RBN.add_cons(kb, t_ag)
                RMV.set_prod(km, t_ag)
                tl["t_ag"] = t_ag
                kr, wr = RRS.acquire()
                tl["rsw"] = wr
                t_ln = W.pub("ac")
                RMV.add_cons(km, t_ln)
                tl["t_ln"] = t_ln
                t_rs = W.pub("ac")
                RRS.set_prod(kr, t_rs)
                tl["t_rs"] = t_rs
                kn, wn = RXNB.acquire()
                tl["xnbw"] = wn
                tl["kxnb"] = kn
                t_xn = W.pub("dv")
                RXNB.set_prod(kn, t_xn)
                RMV.add_cons(km, t_xn)
                RRS.add_cons(kr, t_xn)
                tl["t_xn"] = t_xn
                if do_q and XNS is not None:
                    tl["t_xns"] = W.pub(f"d_qxns{kn % 4}", 16)
                    RXNB.add_cons(kn, tl["t_xns"])
                kt, wt = RB[6 + tp].acquire()
                tl["tbw"] = wt
                t_tp = W.pub("pe")
                RB[6 + tp].set_prod(kt, t_tp)
                RXNB.add_cons(kn, t_tp)
                tl["t_tp"] = t_tp
                if t == 0:
                    kxt, wxt = RXNT.acquire()
                    ev["xnt"] = kxt
                    ev["xntw"] = wxt
                t_ev = W.pub("ac")
                RB[6 + tp].add_cons(kt, t_ev)
                tl["t_ev"] = t_ev
                ev["tiles"].append(tl)
            RXNT.set_prod(ev["xnt"], ev["tiles"][3]["t_ev"])
            ev["prs"] = []
            for pr in range(4):
                ba, bb = (0, 1) if pr % 2 == 0 else (2, 3)
                q = dict(pr=pr, ba=ba, bb=bb)
                ka, wa = RB[ba].acquire()
                kb2, wb2 = RB[bb].acquire()
                q["baw"], q["bbw"] = wa, wb2
                t_pa = W.pub("pe")
                t_pb = W.pub("pe")
                RB[ba].set_prod(ka, t_pa)
                RB[bb].set_prod(kb2, t_pb)
                q["t_pa"], q["t_pb"] = t_pa, t_pb
                q["ka"], q["kb"] = ka, kb2
                if pr == 0:
                    kk, wk = RKO.acquire()
                    ev["ko"] = kk
                    ev["kow"] = wk
                t_ra = W.pub("dv")
                t_rb = W.pub("dv")
                t_ro = W.pub("dv")
                RB[ba].add_cons(ka, t_ra)
                RB[bb].add_cons(kb2, t_rb)
                q["t_ra"], q["t_rb"], q["t_ro"] = t_ra, t_rb, t_ro
                if pr == 3:
                    RX.add_cons(ev["x"][0], t_ro)
                ev["prs"].append(q)
            RKO.set_prod(ev["ko"], ev["prs"][3]["t_ro"])
            dkop = f"d_{tag}ko{ev['ko'] % 2}"
            for _ in range(7):
                W.pub(dkop, 16)
            tsk = W.pub(dkop, 16)
            RKO.add_cons(ev["ko"], tsk)
            ev["t_stko"] = tsk
            if do_q:
                tsx = W.pub(f"d_{tag}xnt{ev['xnt'] % 2}", 16)
                RXNT.add_cons(ev["xnt"], tsx)
                ev["t_stxnt"] = tsx
                RXNT.add_cons(ev["xnt"], ev["prs"][3]["t_pb"])
            if do_v:
                ev["vs"] = []
                for sbk in range(4):
                    ba, bb = (4, 5) if sbk % 2 == 0 else (0, 1)
                    v = dict(sbk=sbk, ba=ba, bb=bb)
                    ka, wa = RB[ba].acquire()
                    kb2, wb2 = RB[bb].acquire()
                    v["baw"], v["bbw"] = wa, wb2
                    t_va = W.pub("pe")
                    t_vb = W.pub("pe")
                    RB[ba].set_prod(ka, t_va)
                    RB[bb].set_prod(kb2, t_vb)
                    v["t_va"], v["t_vb"] = t_va, t_vb
                    if sbk == 3:
                        RXNT.add_cons(ev["xnt"], t_vb)
                    kv, wv = RVO.acquire()
                    v["vow"] = wv
                    t_c1 = W.pub("ac")
                    t_c2 = W.pub("ac")
                    RB[ba].add_cons(ka, t_c1)
                    RB[bb].add_cons(kb2, t_c2)
                    RVO.set_prod(kv, t_c2)
                    v["t_c1"], v["t_c2"] = t_c1, t_c2
                    tsv = W.pub(f"d_{tag}vo{kv % 2}", 16)
                    RVO.add_cons(kv, tsv)
                    v["t_stv"] = tsv
                    ev["vs"].append(v)
            EV.append(ev)

        ew, pub, pub16 = mkemit(S, None)

        @block.sync
        def _(sync):
            def loads(ev):
                par, st = ev["par"], ev["st"]
                dxp = ev["t_x"][0]
                ew(sync, ev["x"][1])
                S.inc(sync.dma_start(xbuf[par][:], src_x[st]), dxp, 16)
                S.inc(sync.dma_start(
                    cbuf[par][:].rearrange("p (oc s) -> p oc s", s=512),
                    cos_d[:, :, st * 512:(st + 1) * 512]
                    .rearrange("oc p s -> p oc s")), dxp, 16)
                S.inc(sync.dma_start(
                    sbin[par][:].rearrange("p (oc s) -> p oc s", s=512),
                    sin_d[:, :, st * 512:(st + 1) * 512]
                    .rearrange("oc p s -> p oc s")), dxp, 16)
                assert S.c[dxp] == ev["t_x"][1]

            def stores(ev):
                st, par = ev["st"], ev["par"]
                if do_q and XNS is not None:
                    for tl in ev["tiles"]:
                        ew(sync, [tl["t_xn"]])
                        pub16(sync.dma_start(XNS[st * 4 + tl["t"]],
                                             xnb[tl["kxnb"] % 4][:]), tl["t_xns"])
                dkop = ev["t_stko"][0]
                ew(sync, [ev["prs"][3]["t_ro"]])
                for oc in range(8):
                    ii = sync.dma_start(kt_dst[oc, :, st * 512:(st + 1) * 512],
                                        ko[par][:, oc * 512:(oc + 1) * 512])
                    S.inc(ii, dkop, 16)
                assert S.c[dkop] == ev["t_stko"][1]
                if do_q:
                    ew(sync, [ev["tiles"][3]["t_ev"]])
                    ii = sync.dma_start(
                        xnt_dst[:, :, st * 512:(st + 1) * 512]
                        .rearrange("dc p s -> p dc s"),
                        xnt[par][:].rearrange("p (dc s) -> p dc s", s=512))
                    pub16(ii, ev["t_stxnt"])
                if do_v:
                    for v in ev["vs"]:
                        vp = v["sbk"] % 2
                        ew(sync, [v["t_c2"]])
                        ii = sync.dma_start(
                            v_dst[st * 512 + v["sbk"] * 128:
                                  st * 512 + (v["sbk"] + 1) * 128, :],
                            vo[vp][:])
                        pub16(ii, v["t_stv"])

            for i, ev in enumerate(EV):
                loads(ev)
                if i > 0:
                    stores(EV[i - 1])
            stores(EV[-1])

        ew, pub, _ = mkemit(S, "dv")

        @block.vector
        def _(vec):
            for ev in EV:
                par = ev["par"]
                for tl in ev["tiles"]:
                    t, tp = tl["t"], tl["tp"]
                    xt = xbuf[par][:, t * 1024:(t + 1) * 1024]
                    if t == 0:
                        ew(vec, [ev["t_x"]])
                    ew(vec, tl["bn"][1])
                    pub(vec.bn_stats(bnst[tp][:, 0:6], xt[:, 0:512]), tl["t_bn0"])
                    pub(vec.bn_stats(bnst[tp][:, 6:12], xt[:, 512:1024]), tl["t_bn"])
                    S.wait(vec, "dv", tl["t_bn"][1])
                    ew(vec, tl["mvw"])
                    pub(vec.bn_aggr(mv[tp][:], bnst[tp][:]), tl["t_ag"])
                    ew(vec, [tl["t_rs"]] + tl["xnbw"])
                    pub(vec.tensor_scalar(xnb[tl["kxnb"] % 4][:], xt, mv[tp][:, 0:1],
                                          rstd[tp][:], ALU.subtract, ALU.mult),
                        tl["t_xn"])
                for q in ev["prs"]:
                    pr, ba, bb = q["pr"], q["ba"], q["bb"]
                    ww = [q["t_pa"], q["t_pb"]]
                    if pr == 0:
                        ww = ww + ev["kow"]
                    ew(vec, ww)
                    ca = cbuf[par][:, pr * 512:(pr + 1) * 512]
                    sa = sbin[par][:, pr * 512:(pr + 1) * 512]
                    vec.tensor_mul(rt[0][:], pf[ba][:, 0:512], ca)
                    vec.tensor_mul(rt[1][:], pf[bb][:, 0:512], sa)
                    pub(vec.tensor_mul(rt[2][:], pf[ba][:, 0:512], sa), q["t_ra"])
                    pub(vec.tensor_mul(rt[3][:], pf[bb][:, 0:512], ca), q["t_rb"])
                    vec.tensor_add(ko[par][:, pr * 512:(pr + 1) * 512],
                                   rt[0][:], rt[1][:])
                    pub(vec.tensor_sub(ko[par][:, (pr + 4) * 512:(pr + 5) * 512],
                                       rt[3][:], rt[2][:]), q["t_ro"])

        ew, pub, _ = mkemit(S, "ac")

        @block.scalar
        def _(sc):
            first = True
            for ev in EV:
                par = ev["par"]
                for tl in ev["tiles"]:
                    t, tp = tl["t"], tl["tp"]
                    if first:
                        ew(sc, [t_eps])
                        first = False
                    ew(sc, [tl["t_ag"]] + tl["rsw"])
                    pub(sc.activation(rstd[tp][:], mv[tp][:, 1:2], AF.Ln,
                                      bias=eps_t[:], scale=1.0), tl["t_ln"])
                    S.wait(sc, "ac", tl["t_ln"][1])
                    pub(sc.activation(rstd[tp][:], rstd[tp][:], AF.Exp,
                                      bias=0.0, scale=-0.5), tl["t_rs"])
                    ww = [tl["t_tp"]]
                    if t == 0:
                        ww = ww + ev["xntw"]
                    ew(sc, ww)
                    tb = 6 + tp
                    xv = xnt[par][:].rearrange("p (dc s) -> p dc s", s=512)
                    pub(sc.copy(xv[:, :, t * 128:(t + 1) * 128],
                                pbt[tb - 6][:].rearrange("p (dc s) -> p dc s", s=128)),
                        tl["t_ev"])
                if do_v:
                    for v in ev["vs"]:
                        vp = v["sbk"] % 2
                        ew(sc, [v["t_va"], v["t_vb"]] + v["vow"])
                        pub(sc.copy(vo[vp][:, 0:512], pf[v["ba"]][:, 0:512]),
                            v["t_c1"])
                        pub(sc.copy(vo[vp][:, 512:1024], pf[v["bb"]][:, 0:512]),
                            v["t_c2"])

        ew, pub, _ = mkemit(S, "pe")

        @block.tensor
        def _(te):
            first = True
            for ev in EV:
                par = ev["par"]
                for tl in ev["tiles"]:
                    t, tp = tl["t"], tl["tp"]
                    ww = [tl["t_xn"]] + tl["tbw"]
                    if first:
                        ww = ww + [t_ident] + t_aw
                        first = False
                    ew(te, ww)
                    tb = 6 + tp
                    for dcc in range(8):
                        ii = te.transpose(pbt[tb - 6][:, dcc * 128:(dcc + 1) * 128],
                                          xnb[tl["kxnb"] % 4][:, dcc * 128:(dcc + 1) * 128],
                                          ident_s[:])
                    pub(ii, tl["t_tp"])
                for q in ev["prs"]:
                    pr = q["pr"]
                    ww = q["baw"] + q["bbw"]
                    if pr == 0:
                        ww = ww + [ev["tiles"][3]["t_ev"]]
                    ew(te, ww)
                    for half, bk, tok in ((0, q["ba"], q["t_pa"]),
                                          (1, q["bb"], q["t_pb"])):
                        oc = pr + 4 * half
                        for dcc in range(8):
                            ii = te.matmul(
                                pf[bk][:, 0:512],
                                wproj[:, dcc * 1024 + oc * 128: dcc * 1024 + (oc + 1) * 128],
                                xnt[par][:, dcc * 512:(dcc + 1) * 512],
                                start=(dcc == 0), stop=(dcc == 7))
                        pub(ii, tok)
                if do_v:
                    for v in ev["vs"]:
                        ew(te, v["baw"] + v["bbw"])
                        for dcc in range(8):
                            lhs = xnt[par][:, dcc * 512 + v["sbk"] * 128:
                                           dcc * 512 + (v["sbk"] + 1) * 128]
                            i1 = te.matmul(pf[v["ba"]][:, 0:512], lhs,
                                           wvs[:, dcc * 1024: dcc * 1024 + 512],
                                           start=(dcc == 0), stop=(dcc == 7))
                            i2 = te.matmul(pf[v["bb"]][:, 0:512], lhs,
                                           wvs[:, dcc * 1024 + 512: (dcc + 1) * 1024],
                                           start=(dcc == 0), stop=(dcc == 7))
                        pub(i1, v["t_va"])
                        pub(i2, v["t_vb"])

        for i in range(2):
            for bb2 in (f"d_{tag}ko{i}", f"d_{tag}vo{i}" if do_v else None,
                        f"d_{tag}xnt{i}" if do_q else None,
                        f"d_{tag}xns{i}" if (do_q and XNS is not None) else None):
                if bb2 is not None and bb2 in S.c:
                    final_waits.append((bb2, S.c[bb2]))
        return EV

    one_pass("A", 16, xb, cosK, sinK, wks, True, False, KTs, Vs, None)
    one_pass("q", 8, xq, cosQ, sinQ, wqs, False, True, QTs, None, XNTs)


def _phase_B(nc, block, S, W, ctx, RB, pf, pbt, ident_s, t_ident,
             KTs, Vs, QTs, maskd, ATTs, final_waits, a_done):
    sbt = lambda n, shp, dt: ctx.enter_context(nc.sbuf_tensor(n, shp, dt))
    masks = sbt("b_mask", [128, 4096], F32)
    qb = [sbt(f"b_q{i}", [128, 4096], BF16) for i in range(2)]
    kb = [sbt(f"b_k{i}", [128, 4096], BF16) for i in range(2)]
    vb = [sbt(f"b_v{i}", [128, 4096], BF16) for i in range(2)]
    pbuf = [sbt(f"b_p{i}", [128, 512], BF16) for i in range(2)]
    pts = [sbt(f"b_pt{i}", [128, 512], BF16) for i in range(2)]
    accS = sbt("b_acc", [128, 4096], F32)
    mst = [sbt(f"b_m{i}", [128, 4], F32) for i in range(2)]
    mtmp = sbt("b_mt", [128, 4], F32)
    lbuf = sbt("b_l", [128, 4], F32)
    ltmp = sbt("b_lt", [128, 4], F32)
    negm = sbt("b_nm", [128, 4], F32)
    dmb = sbt("b_dm", [128, 4], F32)
    asc = sbt("b_as", [128, 4], F32)
    lp = sbt("b_lp", [128, 4], F32)
    linv = sbt("b_li", [128, 4], F32)
    astg = [sbt(f"b_at{i}", [128, 1024], F32) for i in range(2)]

    pre = []
    if a_done:
        pre = [(nm, S.c[nm]) for nm in
               ("d_Ako0", "d_Ako1", "d_Avo0", "d_Avo1",
                "d_qko0", "d_qko1", "d_qxnt0", "d_qxnt1") if nm in S.c]

    RQ, RK, RV = RBuf(2), RBuf(2), RBuf(2)
    RPB = [RBuf(1) for _ in range(2)]
    RPT = [RBuf(1) for _ in range(2)]
    RAT = RBuf(2)
    lu_last = [None]
    px_last = [None]

    t_mld = W.pub("d_bm", 16)

    JEV = []
    for J in range(NJ):
        kq, wq_ = RQ.acquire()
        jev = dict(J=J, kq=kq, wq=wq_, ts=[])
        jev["t_q"] = W.pub(f"d_bq{kq % 2}", 16)
        RQ.set_prod(kq, jev["t_q"])
        for t in range(2 * J + 2):
            kk, wk_ = RK.acquire()
            kv, wv_ = RV.acquire()
            tev = dict(t=t, kk=kk, wk=wk_, kv=kv, wv=wv_,
                       diag=(t >= 2 * J), subs=[])
            tev["t_k"] = W.pub(f"d_bk{kk % 2}", 16)
            RK.set_prod(kk, tev["t_k"])
            tev["t_v"] = W.pub(f"d_bv{kv % 2}", 16)
            RV.set_prod(kv, tev["t_v"])
            for sub in range(4):
                par = (t * 4 + sub) % 2
                sbk = sub
                aa, bb2 = 4, 5
                sv = dict(sub=sub, par=par, sbk=sbk, aa=aa, bb2=bb2)
                ks, ws = RB[sbk].acquire()
                sv["ks"], sv["ksw"] = ks, ws
                t_rd = W.pub("dv")
                RB[sbk].add_cons(ks, t_rd)
                sv["t_rd"] = t_rd
                tev["subs"].append(sv)
            tev["w_ng"] = px_last[0]
            tev["t_mx"] = W.pub("dv")
            tev["t_nm"] = W.pub("dv")
            tev["t_md"] = W.pub("dv")
            for sv in tev["subs"]:
                sub, sbk = sv["sub"], sv["sbk"]
                t_mm = W.pub("pe")
                RB[sbk].set_prod(sv["ks"], t_mm)
                sv["t_mm"] = t_mm
                if sub == 3:
                    RK.add_cons(kk, t_mm)
                    if t == 2 * J + 1:
                        RQ.add_cons(kq, t_mm)
            for sv in tev["subs"]:
                sub, par, sbk = sv["sub"], sv["par"], sv["sbk"]
                sv["w_ascp"] = lu_last[0]
                t_al = W.pub("ac")
                sv["t_al"] = t_al
                kpb, wpb = RPB[par].acquire()
                sv["wpb"] = wpb + [lu_last[0]]
                t_px = W.pub("ac")
                RPB[par].set_prod(kpb, t_px)
                RB[sbk].add_cons(sv["ks"], t_px)
                sv["t_px"] = t_px
                sv["kpb"] = kpb
                if sub == 3:
                    px_last[0] = t_px
                ktb, wtb = RB[6 + par].acquire()
                sv["wtb"] = wtb
                t_tr = W.pub("pe")
                RB[6 + par].set_prod(ktb, t_tr)
                RPB[par].add_cons(kpb, t_tr)
                sv["t_tr"] = t_tr
                kpt, wpt = RPT[par].acquire()
                sv["wpt"] = wpt
                t_pc = W.pub("ac")
                RPT[par].set_prod(kpt, t_pc)
                RB[6 + par].add_cons(ktb, t_pc)
                sv["t_pc"] = t_pc
                kaa, waa = RB[sv["aa"]].acquire()
                kab, wab = RB[sv["bb2"]].acquire()
                sv["waa"], sv["wab"] = waa, wab
                sv["kaa"], sv["kab"] = kaa, kab
                t_va = W.pub("pe")
                t_vb = W.pub("pe")
                RB[sv["aa"]].set_prod(kaa, t_va)
                RB[sv["bb2"]].set_prod(kab, t_vb)
                RPT[par].add_cons(kpt, t_vb)
                if sub == 3:
                    RV.add_cons(kv, t_vb)
                sv["t_va"], sv["t_vb"] = t_va, t_vb
            for sv in tev["subs"]:
                sv["t_rs"] = W.pub("dv")
                t_ca = W.pub("dv")
                t_cb = W.pub("dv")
                RB[sv["aa"]].add_cons(sv["kaa"], t_ca)
                RB[sv["bb2"]].add_cons(sv["kab"], t_cb)
                sv["t_ca"], sv["t_cb"] = t_ca, t_cb
            tev["t_lt"] = W.pub("dv")
            tev["t_lu"] = W.pub("dv")
            lu_last[0] = tev["t_lu"]
            jev["ts"].append(tev)
        jev["evac"] = []
        jev["t_li"] = W.pub("dv")
        for sub in range(4):
            ka, wa = RAT.acquire()
            t_at = W.pub("dv")
            RAT.set_prod(ka, t_at)
            t_st = W.pub(f"d_bat{ka % 2}", 16)
            RAT.add_cons(ka, t_st)
            jev["evac"].append(dict(sub=sub, ka=ka, wa=wa,
                                    t_at=t_at, t_st=t_st))
        JEV.append(jev)

    ew, pub, pub16 = mkemit(S, None)

    @block.sync
    def _(sync):
        ew(sync, pre)
        S.inc(sync.dma_start(masks[:], maskd), "d_bm", 16)
        for jev in JEV:
            J = jev["J"]
            ew(sync, jev["wq"])
            pub16(sync.dma_start(
                qb[J % 2][:].rearrange("p (dc s) -> p dc s", s=512),
                QTs[:, :, J * 512:(J + 1) * 512].rearrange("dc p s -> p dc s")),
                jev["t_q"])
            for tev in jev["ts"]:
                t = tev["t"]
                ew(sync, tev["wk"])
                pub16(sync.dma_start(
                    kb[t % 2][:].rearrange("p (dc s) -> p dc s", s=512),
                    KTs[:, :, t * 512:(t + 1) * 512].rearrange("dc p s -> p dc s")),
                    tev["t_k"])
                ew(sync, tev["wv"])
                pub16(sync.dma_start(
                    vb[t % 2][:].rearrange("p (kc d) -> p kc d", d=1024),
                    Vs[t * 512:(t + 1) * 512, :]
                    .rearrange("(kc p) d -> p kc d", p=128)), tev["t_v"])

    ew, pub, pub16 = mkemit(S, "po")

    @block.gpsimd
    def _(g):
        for jev in JEV:
            J = jev["J"]
            for evc in jev["evac"]:
                sp = evc["sub"] % 2
                ew(g, [evc["t_at"]])
                pub16(g.dma_start(
                    ATTs[J * 512 + evc["sub"] * 128:
                         J * 512 + (evc["sub"] + 1) * 128, :],
                    astg[sp][:]), evc["t_st"])
    for i in range(2):
        if f"d_bat{i}" in S.c:
            final_waits.append((f"d_bat{i}", S.c[f"d_bat{i}"]))

    ew, pub, _ = mkemit(S, "pe")

    @block.tensor
    def _(te):
        for jev in JEV:
            J = jev["J"]
            for tev in jev["ts"]:
                t = tev["t"]
                for sv in tev["subs"]:
                    sub, sbk = sv["sub"], sv["sbk"]
                    ww = list(sv["ksw"])
                    if sub == 0:
                        ww += [tev["t_k"], tev["t_v"]]
                        if t == 0:
                            ww += [jev["t_q"]]
                    ew(te, ww)
                    for dcc in range(8):
                        ii = te.matmul(
                            pf[sbk][:, 0:512],
                            qb[J % 2][:, dcc * 512 + sub * 128: dcc * 512 + (sub + 1) * 128],
                            kb[t % 2][:, dcc * 512:(dcc + 1) * 512],
                            start=(dcc == 0), stop=(dcc == 7))
                    pub(ii, sv["t_mm"])
                for sv in tev["subs"]:
                    sub, par, sbk = sv["sub"], sv["par"], sv["sbk"]
                    ew(te, [sv["t_px"]] + sv["wtb"])
                    tb = 6 + par
                    for kc in range(4):
                        ii = te.transpose(pbt[tb - 6][:, kc * 128:(kc + 1) * 128],
                                          pbuf[par][:, kc * 128:(kc + 1) * 128],
                                          ident_s[:])
                    pub(ii, sv["t_tr"])
                    ew(te, [sv["t_pc"]] + sv["waa"] + sv["wab"])
                    for kc in range(4):
                        i1 = te.matmul(pf[sv["aa"]][:, 0:512],
                                       pts[par][:, kc * 128:(kc + 1) * 128],
                                       vb[t % 2][:, kc * 1024: kc * 1024 + 512],
                                       start=(kc == 0), stop=(kc == 3))
                        i2 = te.matmul(pf[sv["bb2"]][:, 0:512],
                                       pts[par][:, kc * 128:(kc + 1) * 128],
                                       vb[t % 2][:, kc * 1024 + 512: (kc + 1) * 1024],
                                       start=(kc == 0), stop=(kc == 3))
                    pub(i1, sv["t_va"])
                    pub(i2, sv["t_vb"])

    ew, pub, _ = mkemit(S, "dv")

    @block.vector
    def _(vec):
        ew(vec, [t_mld])
        for jev in JEV:
            J = jev["J"]
            vec.memset(accS[:], 0.0)
            vec.memset(mst[0][:], NEG)
            vec.memset(lbuf[:], 0.0)
            for tev in jev["ts"]:
                t, diag = tev["t"], tev["diag"]
                mprev, mcur = mst[t % 2], mst[(t + 1) % 2]
                for sv in tev["subs"]:
                    sub, sbk = sv["sub"], sv["sbk"]
                    ew(vec, [sv["t_mm"]])
                    if diag:
                        msl = masks[:].rearrange("p (q k) -> p q k", k=1024)[
                            :, sub, (t - 2 * J) * 512:(t - 2 * J + 1) * 512]
                        vec.tensor_add(pf[sbk][:, 0:512], pf[sbk][:, 0:512], msl)
                    pub(vec.reduce_max(mtmp[:, sub:sub + 1], pf[sbk][:, 0:512],
                                       axis=AX.X), sv["t_rd"])
                S.wait(vec, "dv", tev["subs"][3]["t_rd"][1])
                ew(vec, [tev["w_ng"]])
                pub(vec.tensor_tensor(mcur[:], mtmp[:], mprev[:], ALU.max),
                    tev["t_mx"])
                S.wait(vec, "dv", tev["t_mx"][1])
                pub(vec.tensor_scalar(negm[:], mcur[:], -1.0, None, ALU.mult),
                    tev["t_nm"])
                pub(vec.tensor_sub(dmb[:], mprev[:], mcur[:]), tev["t_md"])
                for sv in tev["subs"]:
                    sub = sv["sub"]
                    accv = accS[:, sub * 1024:(sub + 1) * 1024]
                    ew(vec, [sv["t_px"]])
                    pub(vec.tensor_scalar(accv, accv, asc[:, sub:sub + 1],
                                          None, ALU.mult), sv["t_rs"])
                    ew(vec, [sv["t_va"]])
                    pub(vec.tensor_add(accv[:, 0:512], accv[:, 0:512],
                                       pf[sv["aa"]][:, 0:512]), sv["t_ca"])
                    ew(vec, [sv["t_vb"]])
                    pub(vec.tensor_add(accv[:, 512:1024], accv[:, 512:1024],
                                       pf[sv["bb2"]][:, 0:512]), sv["t_cb"])
                pub(vec.tensor_mul(ltmp[:], lbuf[:], asc[:]), tev["t_lt"])
                S.wait(vec, "dv", tev["t_lt"][1])
                pub(vec.tensor_add(lbuf[:], ltmp[:], lp[:]), tev["t_lu"])
            S.wait(vec, "dv", jev["ts"][-1]["t_lu"][1])
            pub(vec.reciprocal(linv[:], lbuf[:]), jev["t_li"])
            S.wait(vec, "dv", jev["t_li"][1])
            for evc in jev["evac"]:
                sub = evc["sub"]
                sp = sub % 2
                ew(vec, evc["wa"])
                pub(vec.tensor_scalar(astg[sp][:],
                                      accS[:, sub * 1024:(sub + 1) * 1024],
                                      linv[:, sub:sub + 1], None, ALU.mult),
                    evc["t_at"])

    ew, pub, _ = mkemit(S, "ac")

    @block.scalar
    def _(sc):
        for jev in JEV:
            for tev in jev["ts"]:
                for sv in tev["subs"]:
                    sub, par, sbk = sv["sub"], sv["par"], sv["sbk"]
                    ew(sc, [tev["t_md"], sv["w_ascp"]])
                    pub(sc.activation(asc[:, sub:sub + 1], dmb[:, sub:sub + 1],
                                      AF.Exp), sv["t_al"])
                    ew(sc, [x for x in sv["wpb"] if x] + [sv["t_mm"]])
                    pub(sc.activation(pbuf[par][:], pf[sbk][:, 0:512], AF.Exp,
                                      bias=negm[:, sub:sub + 1], scale=1.0,
                                      accum_out=lp[:, sub:sub + 1]), sv["t_px"])
                    ew(sc, [sv["t_tr"]] + sv["wpt"])
                    tb = 6 + par
                    pub(sc.copy(pts[par][:], pbt[tb - 6][:, 0:512]), sv["t_pc"])


def _phase_C(nc, block, S, W, ctx, RB, pf, Wgf, t_wc,
             W1b, W2b, W3b, XNTs, ATTs, xq, OUT, final_waits, prev,
             dbgH=None, dbgA=None, dbgW=None, dbgG=None):
    sbt = lambda n, shp, dt: ctx.enter_context(nc.sbuf_tensor(n, shp, dt))
    NSLOT = 5
    slots = [sbt(f"c_w{i}", [128, 8192], BF16) for i in range(NSLOT)]
    xcb = [sbt(f"c_x{i}", [128, 4096], BF16) for i in range(2)]
    hT = sbt("c_h", [128, 8192], BF16)
    sg = [sbt(f"c_sg{i}", [128, 512], F32) for i in range(2)]
    accF = sbt("c_acc", [128, 4096], F32)
    wgs = sbt("c_wg", [128, 64], BF16)
    wbuf = sbt("c_wb", [128, 256], F32)
    gm1 = sbt("c_g1", [128, 4], F32)
    gm2 = sbt("c_g2", [128, 4], F32)
    gd1 = sbt("c_gd1", [128, 4], F32)
    gd2 = sbt("c_gd2", [128, 4], F32)
    ge = sbt("c_ge", [128, 8], F32)
    gp = sbt("c_gp", [128, 8], F32)
    geq1 = sbt("c_geq1", [128, 32], F32)
    gmask = sbt("c_gmask", [128, 32], F32)
    geq2 = sbt("c_geq2", [128, 32], F32)
    gta = sbt("c_gta", [128, 32], F32)
    cmbx = sbt("c_cx", [128, 4096], F32)
    cmba = sbt("c_ca", [128, 4096], F32)
    ostg = sbt("c_o", [128, 4096], F32)

    pre = []
    if prev:
        pre = [(nm, S.c[nm]) for nm in
               ("d_qxnt0", "d_qxnt1", "d_bat0", "d_bat1") if nm in S.c]

    RW = RBuf(NSLOT)
    RXC = RBuf(2)
    RSG = [RBuf(1) for _ in range(2)]
    RH = RBuf(1)
    RCMB = RBuf(1)
    ROS = RBuf(1)

    t_wg = W.pub("d_cwg", 16)

    GEV = []
    for chunk in range(NCHUNK):
        kx, wx = RXC.acquire()
        g = dict(chunk=chunk, kx=kx, wx=wx)
        g["t_x"] = W.pub(f"d_cx{kx % 2}", 16)
        RXC.set_prod(kx, g["t_x"])
        kb0, wb0 = RB[0].acquire()
        g["wb0"] = wb0
        t_g = W.pub("pe")
        RB[0].set_prod(kb0, t_g)
        RXC.add_cons(kx, t_g)
        g["t_g"] = t_g
        g["t_ir"] = []
        for sub in range(4):
            t1 = W.pub("dv")
            te1 = W.pub("dv")
            tm = W.pub("dv")
            t2 = W.pub("dv")
            g["t_ir"].append((t1, te1, tm, t2))
        t_gd1 = W.pub("dv")
        t_gd = W.pub("dv")
        RB[0].add_cons(kb0, t_gd)
        g["t_gd1"] = t_gd1
        g["t_gd"] = t_gd
        g["t_ges"] = [W.pub("ac") for _ in range(8)]
        g["t_ge"] = g["t_ges"][-1]
        g["t_ga"] = [(W.pub("dv"), W.pub("dv"), W.pub("dv"), W.pub("dv"))
                     for _ in range(4)]
        g["t_gp"] = g["t_ga"][-1][-1]
        g["t_gq"] = [(W.pub("dv"), W.pub("dv")) for _ in range(4)]
        t_gw = W.pub("dv")
        g["t_gw"] = t_gw
        GEV.append(g)
    CEV = []
    for chunk in range(NCHUNK):
        kx, wx = RXC.acquire()
        cev = dict(chunk=chunk, kx=kx, wx=wx, es=[])
        cev["t_x"] = W.pub(f"d_cx{kx % 2}", 16)
        RXC.set_prod(kx, cev["t_x"])
        kc, wc = RCMB.acquire()
        cev["wc"] = wc
        W.pub("d_cc", 16)
        cev["t_c"] = W.pub("d_cc", 16)
        RCMB.set_prod(kc, cev["t_c"])
        for e in range(E):
            eev = dict(e=e, wl=[], uv=[], ys=[])
            kh, wh = RH.acquire()
            eev["wh"] = wh
            items = [("W1", 0), ("W2", 0), ("W1", 1), ("W2", 1),
                     ("W3", 0), ("W3", 1)]
            slot_of = {}
            for wn, half in items:
                kw, wwl = RW.acquire()
                t_l = W.pub(f"d_cw{kw % NSLOT}", 16)
                RW.set_prod(kw, t_l)
                eev["wl"].append(dict(wn=wn, half=half, slot=kw % NSLOT,
                                      kw=kw, w=wwl, t=t_l))
                slot_of[(wn, half)] = (kw % NSLOT, kw, t_l)
            for half in range(2):
                for hc8 in range(8):
                    hc = half * 8 + hc8
                    uv = dict(half=half, hc8=hc8, hc=hc,
                              w1=slot_of[("W1", half)], w2=slot_of[("W2", half)])
                    kU, wU = RB[hc % 2].acquire()
                    kV, wV = RB[2 + hc % 2].acquire()
                    uv["wU"], uv["wV"] = wU, wV
                    t_u = W.pub("pe")
                    t_v = W.pub("pe")
                    RB[hc % 2].set_prod(kU, t_u)
                    RB[2 + hc % 2].set_prod(kV, t_v)
                    uv["t_u"], uv["t_v"] = t_u, t_v
                    if hc8 == 7:
                        RW.add_cons(slot_of[("W1", half)][1], t_u)
                        RW.add_cons(slot_of[("W2", half)][1], t_v)
                        if half == 1 and e == E - 1:
                            RXC.add_cons(kx, t_v)
                    ksg, wsg = RSG[hc % 2].acquire()
                    uv["wsg"] = wsg
                    t_s = W.pub("ac")
                    RSG[hc % 2].set_prod(ksg, t_s)
                    RB[hc % 2].add_cons(kU, t_s)
                    uv["t_s"] = t_s
                    t_h = W.pub("dv")
                    RSG[hc % 2].add_cons(ksg, t_h)
                    RB[2 + hc % 2].add_cons(kV, t_h)
                    uv["t_h"] = t_h
                    if hc == 15:
                        RH.set_prod(kh, t_h)
                    eev["uv"].append(uv)
            ybanks = [(4, 5), (0, 1), (2, 3), (4, 5)]
            for sbk in range(4):
                ba, bb2 = ybanks[sbk]
                y = dict(sbk=sbk, ba=ba, bb2=bb2,
                         w30=slot_of[("W3", 0)], w31=slot_of[("W3", 1)])
                ka, wa = RB[ba].acquire()
                kb2, wb2 = RB[bb2].acquire()
                y["wa"], y["wb"] = wa, wb2
                t_ya = W.pub("pe")
                t_yb = W.pub("pe")
                RB[ba].set_prod(ka, t_ya)
                RB[bb2].set_prod(kb2, t_yb)
                y["t_ya"], y["t_yb"] = t_ya, t_yb
                if sbk == 3:
                    RW.add_cons(slot_of[("W3", 0)][1], t_yb)
                    RW.add_cons(slot_of[("W3", 1)][1], t_yb)
                    RH.add_cons(kh, t_yb)
                t_ea = W.pub("dv")
                t_eb = W.pub("dv")
                RB[ba].add_cons(ka, t_ea)
                RB[bb2].add_cons(kb2, t_eb)
                y["t_ea"], y["t_eb"] = t_ea, t_eb
                eev["ys"].append(y)
            if dbgH is not None:
                eev["t_dbgh"] = W.pub("d_dbgh", 16)
                RH.add_cons(kh, eev["t_dbgh"])
                eev["t_dbga"] = W.pub("d_dbga", 16)
            cev["es"].append(eev)
        ko_, wo = ROS.acquire()
        cev["wo"] = wo
        t_o = W.pub("dv")
        ROS.set_prod(ko_, t_o)
        RCMB.add_cons(kc, t_o)
        cev["t_o"] = t_o
        t_os = W.pub("d_co", 16)
        ROS.add_cons(ko_, t_os)
        cev["t_os"] = t_os
        CEV.append(cev)

    ew, pub, pub16 = mkemit(S, "po")

    @block.gpsimd
    def _(g):
        pub16(g.dma_start(wgs[:], Wgf), t_wg)
        if dbgW is not None:
            ew(g, [GEV[0]["t_gw"]])
            for nm, tns in (("dbgm1", gm1), ("dbgm2", gm2), ("dbgd1", gd1),
                            ("dbgd2", gd2), ("dbge", ge), ("dbgp", gp),
                            ("dbgq1", geq1), ("dbgmk", gmask)):
                S.inc(g.dma_start(dbgG[nm], tns[:]), "d_dbgw", 16)
            ew(g, [GEV[-1]["t_gw"]])
            vv = S.inc(g.dma_start(dbgW, wbuf[:]), "d_dbgw", 16)
            final_waits.append(("d_dbgw", vv))
        if dbgH is not None:
            for cev in CEV:
                for eev in cev["es"]:
                    idx = cev["chunk"] * E + eev["e"]
                    ew(g, [eev["uv"][15]["t_h"]])
                    pub16(g.dma_start(dbgH[idx], hT[:]), eev["t_dbgh"])
                    ew(g, [eev["ys"][3]["t_eb"]])
                    pub16(g.dma_start(dbgA[idx], accF[:]), eev["t_dbga"])
        for cev in CEV:
            ew(g, [cev["t_o"]])
            pub16(g.dma_start(OUT[cev["chunk"]], ostg[:]), cev["t_os"])
    final_waits.append(("d_co0", S.c["d_co0"]))
    final_waits.append(("d_co1", S.c["d_co1"]))
    if dbgH is not None:
        final_waits.append(("d_dbgh", S.c["d_dbgh"]))
        final_waits.append(("d_dbga", S.c["d_dbga"]))

    ew, pub, pub16 = mkemit(S, None)

    @block.sync
    def _(sync):
        if t_wc is not None:
            ew(sync, list(t_wc) + pre)

        def ld_xcb(chunk, kx, tok, waits):
            ew(sync, waits)
            pub16(sync.dma_start(
                xcb[kx % 2][:].rearrange("p (dc s) -> p dc s", s=512),
                XNTs[:, :, chunk * 512:(chunk + 1) * 512]
                .rearrange("dc p s -> p dc s")), tok)

        for g in GEV:
            ld_xcb(g["chunk"], g["kx"], g["t_x"], g["wx"])
        for cev in CEV:
            chunk = cev["chunk"]
            ld_xcb(chunk, cev["kx"], cev["t_x"], cev["wx"])
            ew(sync, cev["wc"])
            S.inc(sync.dma_start(cmbx[:], xq[chunk]), "d_cc", 16)
            ii = sync.dma_start(
                cmba[:].rearrange("p (sb d) -> p sb d", d=1024),
                ATTs[chunk * 512:(chunk + 1) * 512, :]
                .rearrange("(sb p) d -> p sb d", p=128))
            pub16(ii, cev["t_c"])
            for eev in cev["es"]:
                for wl in eev["wl"]:
                    ew(sync, wl["w"])
                    slot = wl["slot"]
                    if wl["wn"] == "W1":
                        ii = sync.dma_start(slots[slot][:], W1b[eev["e"], wl["half"]])
                    elif wl["wn"] == "W2":
                        ii = sync.dma_start(slots[slot][:], W2b[eev["e"], wl["half"]])
                    else:
                        ii = sync.dma_start(
                            slots[slot][:].rearrange("p (hc d) -> p hc d", d=1024),
                            W3b[eev["e"], wl["half"] * 8:(wl["half"] + 1) * 8]
                            .rearrange("hc p d -> p hc d"))
                    pub16(ii, wl["t"])

    ew, pub, _ = mkemit(S, "pe")

    @block.tensor
    def _(te):
        for g in GEV:
            chunk = g["chunk"]
            ww = [g["t_x"], t_wg] + g["wb0"]
            ew(te, ww)
            for sbk in range(4):
                for dcc in range(8):
                    ii = te.matmul(
                        pf[0][:, sbk * 8:(sbk + 1) * 8],
                        xcb[g["kx"] % 2][:, dcc * 512 + sbk * 128:
                                         dcc * 512 + (sbk + 1) * 128],
                        wgs[:, dcc * 8:(dcc + 1) * 8],
                        start=(dcc == 0), stop=(dcc == 7))
            pub(ii, g["t_g"])
        for cev in CEV:
            chunk = cev["chunk"]
            for eev in cev["es"]:
                for uv in eev["uv"]:
                    hc, hc8, half = uv["hc"], uv["hc8"], uv["half"]
                    s1 = uv["w1"][0]
                    s2 = uv["w2"][0]
                    ww = list(uv["wU"])
                    if hc8 == 0:
                        ww += [uv["w1"][2], uv["w2"][2]]
                        if hc == 0 and eev["e"] == 0:
                            ww += [cev["t_x"]]
                    ew(te, ww)
                    for dcc in range(8):
                        ii = te.matmul(
                            pf[hc % 2][:, 0:512],
                            slots[s1][:, dcc * 1024 + hc8 * 128:
                                      dcc * 1024 + (hc8 + 1) * 128],
                            xcb[cev["kx"] % 2][:, dcc * 512:(dcc + 1) * 512],
                            start=(dcc == 0), stop=(dcc == 7))
                    pub(ii, uv["t_u"])
                    ew(te, uv["wV"])
                    for dcc in range(8):
                        ii = te.matmul(
                            pf[2 + hc % 2][:, 0:512],
                            slots[s2][:, dcc * 1024 + hc8 * 128:
                                      dcc * 1024 + (hc8 + 1) * 128],
                            xcb[cev["kx"] % 2][:, dcc * 512:(dcc + 1) * 512],
                            start=(dcc == 0), stop=(dcc == 7))
                    pub(ii, uv["t_v"])
                s30 = eev["ys"][0]["w30"][0]
                s31 = eev["ys"][0]["w31"][0]
                for i, y in enumerate(eev["ys"]):
                    sbk = y["sbk"]
                    ww = list(y["wa"]) + list(y["wb"])
                    if i == 0:
                        ww += [y["w30"][2], y["w31"][2],
                               eev["uv"][15]["t_h"]]
                    ew(te, ww)
                    for hc in range(16):
                        half, hc8 = hc // 8, hc % 8
                        w3s = s30 if half == 0 else s31
                        lhs = hT[:, hc * 512 + sbk * 128: hc * 512 + (sbk + 1) * 128]
                        i1 = te.matmul(pf[y["ba"]][:, 0:512], lhs,
                                       slots[w3s][:, hc8 * 1024: hc8 * 1024 + 512],
                                       start=(hc == 0), stop=(hc == 15))
                        i2 = te.matmul(pf[y["bb2"]][:, 0:512], lhs,
                                       slots[w3s][:, hc8 * 1024 + 512: (hc8 + 1) * 1024],
                                       start=(hc == 0), stop=(hc == 15))
                    pub(i1, y["t_ya"])
                    pub(i2, y["t_yb"])

    ew, pub, _ = mkemit(S, "ac")

    @block.scalar
    def _(sc):
        for g in GEV:
            ew(sc, [g["t_gd"]])
            for sub in range(4):
                pub(sc.activation(ge[:, sub:sub + 1], gd1[:, sub:sub + 1],
                                  AF.Exp), g["t_ges"][2 * sub])
                pub(sc.activation(ge[:, 4 + sub:5 + sub], gd2[:, sub:sub + 1],
                                  AF.Exp), g["t_ges"][2 * sub + 1])
        for cev in CEV:
            for eev in cev["es"]:
                for uv in eev["uv"]:
                    hc = uv["hc"]
                    ew(sc, [uv["t_u"]] + uv["wsg"])
                    pub(sc.activation(sg[hc % 2][:], pf[hc % 2][:, 0:512],
                                      AF.Silu), uv["t_s"])

    ew, pub, _ = mkemit(S, "dv")

    @block.vector
    def _(vec):
        for g in GEV:
            chunk = g["chunk"]
            ew(vec, [g["t_g"]])
            for sub in range(4):
                lg = pf[0][:, sub * 8:(sub + 1) * 8]
                t1, te1, tm, t2 = g["t_ir"][sub]
                pub(vec.reduce_max(gm1[:, sub:sub + 1], lg, axis=AX.X), t1)
                S.wait(vec, "dv", t1[1])
                pub(vec.tensor_scalar(geq1[:, sub * 8:(sub + 1) * 8], lg,
                                      gm1[:, sub:sub + 1], None, ALU.is_equal), te1)
                S.wait(vec, "dv", te1[1])
                pub(vec.scalar_tensor_tensor(gmask[:, sub * 8:(sub + 1) * 8],
                                             geq1[:, sub * 8:(sub + 1) * 8], NEG, lg,
                                             ALU.mult, ALU.add), tm)
                S.wait(vec, "dv", tm[1])
                pub(vec.reduce_max(gm2[:, sub:sub + 1],
                                   gmask[:, sub * 8:(sub + 1) * 8], axis=AX.X), t2)
                S.wait(vec, "dv", t2[1])
            pub(vec.tensor_sub(gd1[:], gm2[:], gm1[:]), g["t_gd1"])
            pub(vec.tensor_sub(gd2[:], gm1[:], gm2[:]), g["t_gd"])
            ew(vec, [g["t_ge"]])
            for sub in range(4):
                ta1, ta2, tr1, tr2 = g["t_ga"][sub]
                pub(vec.tensor_scalar(gp[:, sub:sub + 1], ge[:, sub:sub + 1],
                                      1.0, None, ALU.add), ta1)
                pub(vec.tensor_scalar(gp[:, 4 + sub:5 + sub],
                                      ge[:, 4 + sub:5 + sub], 1.0, None, ALU.add), ta2)
                S.wait(vec, "dv", ta2[1])
                pub(vec.reciprocal(gp[:, sub:sub + 1], gp[:, sub:sub + 1]), tr1)
                pub(vec.reciprocal(gp[:, 4 + sub:5 + sub], gp[:, 4 + sub:5 + sub]),
                    tr2)
            S.wait(vec, "dv", g["t_gp"][1])
            for sub in range(4):
                sl = slice(sub * 8, (sub + 1) * 8)
                tta, tq = g["t_gq"][sub]
                pub(vec.tensor_scalar(gta[:, sl], geq1[:, sl], gp[:, sub:sub + 1],
                                      None, ALU.mult), tta)
                pub(vec.tensor_scalar(geq2[:, sl], gmask[:, sl], gm2[:, sub:sub + 1],
                                      None, ALU.is_equal), tq)
                S.wait(vec, "dv", tq[1])
                ii = vec.scalar_tensor_tensor(
                    wbuf[:, chunk * 32 + sub * 8: chunk * 32 + (sub + 1) * 8],
                    geq2[:, sl], gp[:, 4 + sub:5 + sub], gta[:, sl],
                    ALU.mult, ALU.add)
            pub(ii, g["t_gw"])
        for cev in CEV:
            chunk = cev["chunk"]
            vec.memset(accF[:], 0.0)
            for eev in cev["es"]:
                e = eev["e"]
                for uv in eev["uv"]:
                    hc = uv["hc"]
                    ww = [uv["t_s"], uv["t_v"]]
                    if hc == 0:
                        ww = ww + eev["wh"]
                    ew(vec, ww)
                    pub(vec.tensor_mul(hT[:, hc * 512:(hc + 1) * 512],
                                       sg[hc % 2][:], pf[2 + hc % 2][:, 0:512]),
                        uv["t_h"])
                if dbgA is not None and (e > 0 or chunk > 0):
                    prev_idx = chunk * E + e - 1 if e > 0 else (chunk - 1) * E + E - 1
                    S.wait(vec, "d_dbga", 16 * (prev_idx + 1))
                for y in eev["ys"]:
                    sbk = y["sbk"]
                    wcol = wbuf[:, chunk * 32 + sbk * 8 + e:
                                chunk * 32 + sbk * 8 + e + 1]
                    accv = accF[:, sbk * 1024:(sbk + 1) * 1024]
                    ew(vec, [y["t_ya"]])
                    pub(vec.scalar_tensor_tensor(accv[:, 0:512], pf[y["ba"]][:, 0:512],
                                                 wcol, accv[:, 0:512],
                                                 ALU.mult, ALU.add), y["t_ea"])
                    ew(vec, [y["t_yb"]])
                    pub(vec.scalar_tensor_tensor(accv[:, 512:1024],
                                                 pf[y["bb2"]][:, 0:512],
                                                 wcol, accv[:, 512:1024],
                                                 ALU.mult, ALU.add), y["t_eb"])
            ew(vec, [cev["t_c"]] + cev["wo"])
            vec.tensor_add(ostg[:], cmbx[:], cmba[:])
            pub(vec.tensor_add(ostg[:], ostg[:], accF[:]), cev["t_o"])


def _phase_C2(nc, block, S, W, ctx, RB, pf, pbt, ident_s, ident32_s,
              t_ident, t_id32, Wgf, W1b, W2b, W3b, t_wc,
              XNTs, XNS, ITI, YALL, ATTs, xq, OUT, final_waits, prev):
    sbt = lambda n, shp, dt: ctx.enter_context(nc.sbuf_tensor(n, shp, dt))
    wp1 = sbt("r_w1", [128, NB_T], F32)
    wp2 = sbt("r_w2", [128, NB_T], F32)
    si16 = sbt("r_si", [128, 4 * 256], I16)
    it16 = sbt("r_it", [128, E * (CCAP // 16)], I16)
    toks = {}

    if "G" in C2_STAGES:
        with ExitStack() as c1:
            _c2_gate_route(nc, block, S, W, c1, RB, pf, ident32_s, t_id32,
                           Wgf, XNTs, ITI, YALL, wp1, wp2, si16, it16,
                           toks, prev)
    if "F" in C2_STAGES:
        with ExitStack() as c2:
            _c2_ffn(nc, block, S, W, c2, RB, pf, pbt, ident_s, t_ident,
                    W1b, W2b, W3b, t_wc, XNS, YALL, it16, toks, prev)
    if "M" in C2_STAGES:
        with ExitStack() as c3:
            _c2_combine(nc, block, S, W, c3, ATTs, xq, OUT, si16, wp1, wp2,
                        YALL, toks, final_waits, prev)


def _c2_gate_route(nc, block, S, W, ctx, RB, pf, ident32_s, t_id32,
                   Wgf, XNTs, ITI, YALL, wp1, wp2, si16, it16,
                   toks, prev):
    sbt = lambda n, shp, dt: ctx.enter_context(nc.sbuf_tensor(n, shp, dt))
    ECC = E * CCAP
    xcb = [sbt(f"c_x{i}", [128, 4096], BF16) for i in range(2)]
    wgs = sbt("c_wg", [128, 64], BF16)
    gm1 = sbt("c_g1", [128, 4], F32)
    gm2 = sbt("c_g2", [128, 4], F32)
    gd1 = sbt("c_gd1", [128, 4], F32)
    gd2 = sbt("c_gd2", [128, 4], F32)
    ge = sbt("c_ge", [128, 8], F32)
    gp = sbt("c_gp", [128, 8], F32)
    geq1 = sbt("c_geq1", [128, 32], F32)
    gmask = sbt("c_gmask", [128, 32], F32)
    geq2 = sbt("c_geq2", [128, 32], F32)
    eqT1 = sbt("r_e1", [8, 4096], F32)[:]
    eqT2 = sbt("r_e2", [8, 4096], F32)[:]
    maskT = sbt("r_mk", [8, 4096], F32)[:]
    csum = sbt("r_cs", [8, 4096], F32)[:]
    rov = sbt("r_ov", [8, 4096], F32)[:]
    tixE = sbt("r_tx", [8, 4096], F32)[:]
    sel = sbt("r_sl", [8, 4096], F32)
    rs2 = sbt("r_s2", [8, 4096], F32)
    sel2 = sbt("r_sm", [8, 4096], F32)
    sifs = [sbt(f"r_sf{k}", [128, NB_T], F32) for k in range(4)]
    sifA1, sifA2, sifB1, sifB2 = sifs
    tt = [sbt(f"r_tt{i}", [32, 128], F32) for i in range(2)]
    wt = sbt("r_wt", [16, 4 * 256], F32)
    sfw = [sbt("r_fw0", [16, 96 * 64], F32)]
    ones8 = sbt("r_on", [8, 1], F32)
    ebase = sbt("r_eb", [8, 1], F32)
    ids64 = sbt("r_id", [128, NB_T * 64], F32)
    zt = sbt("r_zt", [128, 1024], F32)

    pre_xnt = []
    if prev:
        pre_xnt = [(nm, S.c[nm]) for nm in ("d_qxnt0", "d_qxnt1") if nm in S.c]

    RXC = RBuf(2)
    RTT = RBuf(2)
    RFW = RBuf(1)

    t_wg = W.pub("d_cwg", 16)
    t_eb = W.pub("po")
    t_tix = W.pub("po")
    t_mz = W.pub("dv")
    t_zt = W.pub("dv")
    t_on = W.pub("dv")
    t_msi = W.pub("dv")
    t_mit = W.pub("dv")
    t_ids = W.pub("po")
    GEV = []
    for chunk in range(NCHUNK):
        kx, wx = RXC.acquire()
        g = dict(chunk=chunk, kx=kx, wx=wx)
        g["t_x"] = W.pub(f"d_cx{kx % 2}", 16)
        RXC.set_prod(kx, g["t_x"])
        kb0, wb0 = RB[0].acquire()
        g["wb0"] = wb0
        t_g = W.pub("pe")
        RB[0].set_prod(kb0, t_g)
        RXC.add_cons(kx, t_g)
        g["t_g"] = t_g
        g["t_ir"] = [tuple(W.pub("dv") for _ in range(4)) for _ in range(4)]
        t_gd1 = W.pub("dv")
        t_gd = W.pub("dv")
        RB[0].add_cons(kb0, t_gd)
        g["t_gd1"], g["t_gd"] = t_gd1, t_gd
        g["t_ges"] = [W.pub("ac") for _ in range(8)]
        g["t_ge"] = g["t_ges"][-1]
        g["t_ga"] = [tuple(W.pub("dv") for _ in range(4)) for _ in range(4)]
        g["t_gp"] = g["t_ga"][-1][-1]
        g["t_wp"] = (W.pub("dv"), W.pub("dv"))
        g["t_gq"] = [W.pub("dv") for _ in range(4)]
        k4, w4 = RB[4].acquire()
        k5, w5 = RB[5].acquire()
        g["w4"], g["w5"] = w4, w5
        g["t_t1"] = W.pub("pe")
        g["t_t2"] = W.pub("pe")
        RB[4].set_prod(k4, g["t_t1"])
        RB[5].set_prod(k5, g["t_t2"])
        g["t_ev1"] = W.pub("dv")
        g["t_ev2"] = W.pub("dv")
        RB[4].add_cons(k4, g["t_ev1"])
        RB[5].add_cons(k5, g["t_ev2"])
        GEV.append(g)
    RT = {}
    kb0, wb0r = RB[0].acquire()
    kb1, wb1r = RB[1].acquire()
    RT["wb0"], RT["wb1"] = wb0r, wb1r
    RT["t_m1"] = W.pub("pe")
    RT["t_m2"] = W.pub("pe")
    RB[0].set_prod(kb0, RT["t_m1"])
    RB[1].set_prod(kb1, RT["t_m2"])
    WRP = []
    for k in range(4):
        kbB, wbB = RB[2 + k % 2].acquire()
        ktt, wtt = RTT.acquire()
        kbS, wbS = RB[k % 2].acquire()
        w = dict(k=k, kbB=kbB, ktt=ktt, kbS=kbS, wbB=wbB, wtt=wtt, wbS=wbS)
        w["t_bt"] = W.pub("pe")
        RB[2 + k % 2].set_prod(kbB, w["t_bt"])
        w["t_sm"] = W.pub("pe")
        RB[k % 2].set_prod(kbS, w["t_sm"])
        RTT.add_cons(ktt, w["t_sm"])
        WRP.append(w)
    for nm in ("mk", "cs", "s1", "ov", "s2", "a", "b", "s3", "sel1", "sel2",
               "eB1", "eB2", "A1", "A2", "B1c", "B2c"):
        RT[nm] = W.pub("dv")
    RB[0].add_cons(kb0, RT["eB1"])
    RB[1].add_cons(kb1, RT["eB2"])
    for k, w in enumerate(WRP):
        w["t_tt"] = W.pub("dv")
        RTT.set_prod(w["ktt"], w["t_tt"])
        RB[2 + k % 2].add_cons(w["kbB"], w["t_tt"])
        w["t_wt"] = [W.pub("dv") for _ in range(8)]
        RB[k % 2].add_cons(w["kbS"], w["t_wt"][7])
    t_c16 = W.pub("dv")
    toks["t_c16"] = t_c16
    t_ite = [W.pub("dv") for _ in range(E)]
    t_itz = [W.pub("d_itz", 16) for _ in range(7)]
    t_sca = [W.pub("d_sca", 16) for _ in range(2)]
    t_rep1 = W.pub("d_rep1", 16)
    t_itw = []
    for e in range(E):
        kf, wf = RFW.acquire()
        t_l = W.pub(f"d_itw{kf % 2}", 16)
        RFW.set_prod(kf, t_l)
        RFW.add_cons(kf, t_ite[e])
        t_itw.append((kf, wf, t_l))
    t_rep2 = W.pub("d_rep2", 16)
    toks["t_itz"] = t_itz[-1]
    toks["t_sca"] = t_sca[-1]
    toks["t_rep1"] = t_rep1
    toks["t_rep2"] = t_rep2

    ew, pub, pub16 = mkemit(S, None)

    @block.sync
    def _(sync):
        ew(sync, pre_xnt)
        for g in GEV:
            chunk = g["chunk"]
            ew(sync, g["wx"])
            pub16(sync.dma_start(
                xcb[g["kx"] % 2][:].rearrange("p (dc s) -> p dc s", s=512),
                XNTs[:, :, chunk * 512:(chunk + 1) * 512]
                .rearrange("dc p s -> p dc s")), g["t_x"])
        ew(sync, [t_c16])
        pub16(sync.dma_start(si16[16:32, :], si16[0:16, :]), t_rep1)
        for e in range(E):
            kf, wf, t_l = t_itw[e]
            ew(sync, [t_sca[1]] + wf)
            pub16(sync.dma_start(
                sfw[0][:].rearrange("q (c x) -> q c x", x=64),
                ITI[e * CCAP:(e + 1) * CCAP, :]
                .rearrange("(c q) x -> q c x", q=16)), t_l)
        ew(sync, [t_ite[E - 1]])
        pub16(sync.dma_start(it16[16:32, :], it16[0:16, :]), t_rep2)

    ew, pub, pub16 = mkemit(S, "po")

    @block.gpsimd
    def _(g):
        pub16(g.dma_start(wgs[:], Wgf), t_wg)
        pub(g.iota(ebase[:], pattern=[[0, 1]], base=0,
                   channel_multiplier=CCAP,
                   allow_small_or_imprecise_dtypes=True), t_eb)
        pub(g.iota(tixE, pattern=[[1, 4096]], base=ECC,
                   channel_multiplier=0,
                   allow_small_or_imprecise_dtypes=True), t_tix)
        ew(g, [t_mz])
        pub(g.iota(ids64[:].rearrange("p (b e) -> p b e", e=64)[:, :, 0:1],
                   pattern=[[128, NB_T], [0, 1]], base=0, channel_multiplier=1,
                   allow_small_or_imprecise_dtypes=True), t_ids)
        g.load_library(MLP_LIB)
        ew(g, [t_zt])
        for c in range(6):
            pub16(g.dma_start(
                ITI[c * 2048:(c + 1) * 2048].rearrange("(c p) e -> p c e",
                                                       p=128),
                zt[:].rearrange("p (c e) -> p c e", e=64)), t_itz[c])
        pub16(g.dma_start(YALL[YTRASH:YTRASH + 128, :], zt[:, 0:1024]),
              t_itz[6])
        ew(g, [t_rep1, t_itz[4]])
        pub16(g.dma_scatter_add(ITI, ids64[:].rearrange("p (b e) -> p b e",
                                                        e=64),
                                si16[:, 512:768], 4096, 4096, 64), t_sca[0])
        pub16(g.dma_scatter_add(ITI, ids64[:].rearrange("p (b e) -> p b e",
                                                        e=64),
                                si16[:, 768:1024], 4096, 4096, 64), t_sca[1])

    ew, pub, _ = mkemit(S, "pe")

    @block.tensor
    def _(te):
        for g in GEV:
            ew(te, [g["t_x"], t_wg] + g["wb0"])
            for sbk in range(4):
                for dcc in range(8):
                    ii = te.matmul(
                        pf[0][:, sbk * 8:(sbk + 1) * 8],
                        xcb[g["kx"] % 2][:, dcc * 512 + sbk * 128:
                                         dcc * 512 + (sbk + 1) * 128],
                        wgs[:, dcc * 8:(dcc + 1) * 8],
                        start=(dcc == 0), stop=(dcc == 7))
            pub(ii, g["t_g"])
            ew(te, [t_id32, g["t_gq"][3]] + g["w4"] + g["w5"])
            for sub in range(4):
                i1 = te.transpose(pf[4][0:8, sub * 128:(sub + 1) * 128],
                                  geq1[:, sub * 8:(sub + 1) * 8], ident32_s[:])
            pub(i1, g["t_t1"])
            for sub in range(4):
                i2 = te.transpose(pf[5][0:8, sub * 128:(sub + 1) * 128],
                                  geq2[:, sub * 8:(sub + 1) * 8], ident32_s[:])
            pub(i2, g["t_t2"])
        ew(te, [RT["sel1"], RT["sel2"], t_on] + RT["wb0"] + RT["wb1"])
        for blk in range(NB_T):
            ii = te.matmul(pf[0][:, blk:blk + 1],
                           sel[0:8, blk * 128:(blk + 1) * 128],
                           ones8[:], start=True, stop=True)
        pub(ii, RT["t_m1"])
        for blk in range(NB_T):
            ii = te.matmul(pf[1][:, blk:blk + 1],
                           sel2[0:8, blk * 128:(blk + 1) * 128],
                           ones8[:], start=True, stop=True)
        pub(ii, RT["t_m2"])
        for w in WRP:
            k = w["k"]
            ew(te, [RT["B2c"]] + w["wbB"])
            pub(te.transpose(pf[2 + k % 2][0:32, 0:128], sifs[k][:],
                             ident32_s[:]), w["t_bt"])
            ew(te, [w["t_tt"]] + w["wbS"] + w["wtt"])
            for gg in range(8):
                ii = te.transpose(pf[k % 2][0:16, gg * 32:(gg + 1) * 32],
                                  tt[k % 2][:, gg * 16:(gg + 1) * 16],
                                  ident32_s[0:32, 0:32])
            pub(ii, w["t_sm"])

    ew, pub, _ = mkemit(S, "ac")

    @block.scalar
    def _(sc):
        for g in GEV:
            ew(sc, [g["t_gd"]])
            for sub in range(4):
                pub(sc.activation(ge[:, sub:sub + 1], gd1[:, sub:sub + 1],
                                  AF.Exp), g["t_ges"][2 * sub])
                pub(sc.activation(ge[:, 4 + sub:5 + sub], gd2[:, sub:sub + 1],
                                  AF.Exp), g["t_ges"][2 * sub + 1])

    ew, pub, _ = mkemit(S, "dv")

    @block.vector
    def _(vec):
        pub(vec.memset(ids64[:], 0.0), t_mz)
        pub(vec.memset(zt[:], 0.0), t_zt)
        pub(vec.memset(ones8[:], 1.0), t_on)
        pub(vec.memset(si16[:], 0), t_msi)
        pub(vec.memset(it16[:], 0), t_mit)
        for g in GEV:
            chunk = g["chunk"]
            ew(vec, [g["t_g"]])
            for sub in range(4):
                lg = pf[0][:, sub * 8:(sub + 1) * 8]
                t1, te1, tm, t2 = g["t_ir"][sub]
                pub(vec.reduce_max(gm1[:, sub:sub + 1], lg, axis=AX.X), t1)
                S.wait(vec, "dv", t1[1])
                pub(vec.tensor_scalar(geq1[:, sub * 8:(sub + 1) * 8], lg,
                                      gm1[:, sub:sub + 1], None, ALU.is_equal),
                    te1)
                S.wait(vec, "dv", te1[1])
                pub(vec.scalar_tensor_tensor(gmask[:, sub * 8:(sub + 1) * 8],
                                             geq1[:, sub * 8:(sub + 1) * 8],
                                             NEG, lg, ALU.mult, ALU.add), tm)
                S.wait(vec, "dv", tm[1])
                pub(vec.reduce_max(gm2[:, sub:sub + 1],
                                   gmask[:, sub * 8:(sub + 1) * 8], axis=AX.X),
                    t2)
                S.wait(vec, "dv", t2[1])
            pub(vec.tensor_sub(gd1[:], gm2[:], gm1[:]), g["t_gd1"])
            pub(vec.tensor_sub(gd2[:], gm1[:], gm2[:]), g["t_gd"])
            ew(vec, [g["t_ge"]])
            for sub in range(4):
                ta1, ta2, tr1, tr2 = g["t_ga"][sub]
                pub(vec.tensor_scalar(gp[:, sub:sub + 1], ge[:, sub:sub + 1],
                                      1.0, None, ALU.add), ta1)
                pub(vec.tensor_scalar(gp[:, 4 + sub:5 + sub],
                                      ge[:, 4 + sub:5 + sub], 1.0, None,
                                      ALU.add), ta2)
                S.wait(vec, "dv", ta2[1])
                pub(vec.reciprocal(gp[:, sub:sub + 1], gp[:, sub:sub + 1]), tr1)
                pub(vec.reciprocal(gp[:, 4 + sub:5 + sub],
                                   gp[:, 4 + sub:5 + sub]), tr2)
            S.wait(vec, "dv", g["t_gp"][1])
            pub(vec.tensor_copy(wp1[:, chunk * 4:(chunk + 1) * 4], gp[:, 0:4]),
                g["t_wp"][0])
            pub(vec.tensor_copy(wp2[:, chunk * 4:(chunk + 1) * 4], gp[:, 4:8]),
                g["t_wp"][1])
            for sub in range(4):
                sl = slice(sub * 8, (sub + 1) * 8)
                pub(vec.tensor_scalar(geq2[:, sl], gmask[:, sl],
                                      gm2[:, sub:sub + 1], None, ALU.is_equal),
                    g["t_gq"][sub])
            ew(vec, [g["t_t1"]])
            pub(vec.tensor_copy(eqT1[:, chunk * 512:(chunk + 1) * 512],
                                pf[4][0:8, 0:512]), g["t_ev1"])
            ew(vec, [g["t_t2"]])
            pub(vec.tensor_copy(eqT2[:, chunk * 512:(chunk + 1) * 512],
                                pf[5][0:8, 0:512]), g["t_ev2"])
        ew(vec, [GEV[-1]["t_ev2"]])
        pub(vec.tensor_add(maskT, eqT1, eqT2), RT["mk"])
        pub(vec.tensor_tensor_scan(csum, maskT, maskT, 0.0,
                                   ALU.add, ALU.bypass), RT["cs"])
        pub(vec.tensor_scalar(rs2[:], csum, -1.0, None, ALU.add), RT["s1"])
        pub(vec.tensor_scalar(rov, csum, float(CCAP), None, ALU.is_gt),
            RT["ov"])
        ew(vec, [t_eb])
        pub(vec.tensor_scalar(rs2[:], rs2[:], ebase[:], None, ALU.add),
            RT["s2"])
        ew(vec, [t_tix])
        pub(vec.tensor_sub(sel[:], tixE, rs2[:]), RT["a"])
        pub(vec.tensor_mul(sel[:], rov, sel[:]), RT["b"])
        pub(vec.tensor_add(rs2[:], rs2[:], sel[:]), RT["s3"])
        pub(vec.tensor_mul(sel[:], eqT1, rs2[:]), RT["sel1"])
        pub(vec.tensor_mul(sel2[:], eqT2, rs2[:]), RT["sel2"])
        ew(vec, [RT["t_m1"]])
        pub(vec.tensor_copy(sifB1[:], pf[0][:, 0:NB_T]), RT["eB1"])
        ew(vec, [RT["t_m2"]])
        pub(vec.tensor_copy(sifB2[:], pf[1][:, 0:NB_T]), RT["eB2"])
        S.wait(vec, "dv", RT["eB2"][1])
        pub(vec.tensor_scalar(sifA1[:], sifB1[:], float(E * CCAP), None,
                              ALU.min), RT["A1"])
        pub(vec.tensor_scalar(sifA2[:], sifB2[:], float(E * CCAP), None,
                              ALU.min), RT["A2"])
        S.wait(vec, "dv", RT["A2"][1])
        pub(vec.tensor_scalar(sifB1[:], sifB1[:], float(ITI_ROWS - 1), None,
                              ALU.min), RT["B1c"])
        pub(vec.tensor_scalar(sifB2[:], sifB2[:], float(ITI_ROWS - 1), None,
                              ALU.min), RT["B2c"])
        for w in WRP:
            k = w["k"]
            ew(vec, [w["t_bt"]])
            pub(vec.tensor_copy(tt[k % 2][:], pf[2 + k % 2][0:32, 0:128]),
                w["t_tt"])
            ew(vec, [w["t_sm"]])
            for gg in range(8):
                pub(vec.tensor_copy(
                    wt[0:16, k * 256:(k + 1) * 256]
                    .rearrange("q (b g) -> q b g", g=8)[:, :, gg:gg + 1],
                    pf[k % 2][0:16, gg * 32:(gg + 1) * 32]
                    .rearrange("q (b x) -> q b x", x=1)), w["t_wt"][gg])
        pub(vec.tensor_copy(si16[0:16, :], wt[0:16, :]), t_c16)
        for e in range(E):
            kf, wf, t_l = t_itw[e]
            ew(vec, [t_l])
            pub(vec.tensor_copy(
                it16[0:16, e * 96:(e + 1) * 96]
                .rearrange("q (c x) -> q c x", x=1),
                sfw[0][:].rearrange("q (c x) -> q c x", x=64)[:, :, 0:1]),
                t_ite[e])


def _c2_ffn(nc, block, S, W, ctx, RB, pf, pbt, ident_s, t_ident,
            W1b, W2b, W3b, t_wc, XNS, YALL, it16, toks, prev):
    sbt = lambda n, shp, dt: ctx.enter_context(nc.sbuf_tensor(n, shp, dt))
    NSLOT = 7
    NSB = CCAP // 128
    slots = [sbt(f"f_w{i}", [128, 4096], BF16) for i in range(NSLOT)]
    xgb = [sbt(f"f_xg{i}", [128, NSB * 1024], BF16) for i in range(2)]
    xgT = sbt("f_xt", [128, DC * CCAP], BF16)
    hTr = sbt("f_h", [128, 16 * CCAP], BF16)
    sg = sbt("f_sg", [128, 512], F32)
    ystg = sbt("f_y", [128, 1024], F32)
    xns_flat = XNS.rearrange("a p d -> (a p) d")

    RW = RBuf(NSLOT)
    RXG = RBuf(2)
    RXT = RBuf(1)
    RH = RBuf(1)
    RSG = RBuf(1)
    RYS = RBuf(1)
    pre_xns = []
    if prev:
        pre_xns = [(nm, S.c[nm]) for nm in
                   ("d_qxns0", "d_qxns1", "d_qxns2", "d_qxns3") if nm in S.c]

    EEV = []
    for e in range(E):
        kxg, wxg = RXG.acquire()
        eev = dict(e=e, kxg=kxg, wxg=wxg)
        eev["t_xga"] = W.pub(f"d_xg{kxg % 2}", 16)
        eev["t_xg"] = W.pub(f"d_xg{kxg % 2}", 16)
        RXG.set_prod(kxg, eev["t_xg"])
        items = [("W1", 0), ("W2", 0), ("W1", 1), ("W2", 1),
                 ("W1", 2), ("W2", 2), ("W1", 3), ("W2", 3),
                 ("W3", 0), ("W3", 1), ("W3", 2), ("W3", 3)]
        slot_of = {}
        eev["wl"] = []
        for wn, q in items:
            kw, wwl = RW.acquire()
            t_l = W.pub(f"d_w{kw % NSLOT}", 16)
            RW.set_prod(kw, t_l)
            eev["wl"].append(dict(wn=wn, q=q, slot=kw % NSLOT, kw=kw, w=wwl,
                                  t=t_l))
            slot_of[(wn, q)] = (kw % NSLOT, kw, t_l)
        kxt, wxt = RXT.acquire()
        eev["wxt"] = wxt
        eev["tps"] = []
        for blk in range(NSB):
            ktb, wtb = RB[6 + blk % 2].acquire()
            t_tp = W.pub("pe")
            RB[6 + blk % 2].set_prod(ktb, t_tp)
            t_ev = W.pub("ac")
            RB[6 + blk % 2].add_cons(ktb, t_ev)
            eev["tps"].append(dict(blk=blk, wtb=wtb, t_tp=t_tp, t_ev=t_ev))
        RXT.set_prod(kxt, eev["tps"][-1]["t_ev"])
        RXG.add_cons(kxg, eev["tps"][-1]["t_tp"])
        eev["uv"] = []
        cnt = 0
        for hc in range(16):
            for stile in range(3):
                q, hq = hc // 4, hc % 4
                uv = dict(hc=hc, stile=stile, q=q, hq=hq,
                          w1=slot_of[("W1", q)], w2=slot_of[("W2", q)])
                kU, wU = RB[cnt % 2].acquire()
                kV, wV = RB[2 + cnt % 2].acquire()
                uv["bu"], uv["bv"] = cnt % 2, 2 + cnt % 2
                uv["wU"], uv["wV"] = wU, wV
                t_u = W.pub("pe")
                t_v = W.pub("pe")
                RB[cnt % 2].set_prod(kU, t_u)
                RB[2 + cnt % 2].set_prod(kV, t_v)
                uv["t_u"], uv["t_v"] = t_u, t_v
                if hc % 4 == 3 and stile == 2:
                    RW.add_cons(slot_of[("W1", q)][1], t_u)
                    RW.add_cons(slot_of[("W2", q)][1], t_v)
                ksg, wsg = RSG.acquire()
                uv["wsg"] = wsg
                t_s = W.pub("ac")
                RSG.set_prod(ksg, t_s)
                RB[cnt % 2].add_cons(kU, t_s)
                uv["t_s"] = t_s
                t_h = W.pub("dv")
                RSG.add_cons(ksg, t_h)
                RB[2 + cnt % 2].add_cons(kV, t_h)
                uv["t_h"] = t_h
                uv["cnt"] = cnt
                cnt += 1
                eev["uv"].append(uv)
        kh, wh = RH.acquire()
        eev["wh"] = wh
        RH.set_prod(kh, eev["uv"][-1]["t_h"])
        eev["ys"] = []
        for sb in range(NSB):
            ka, wa = RB[4].acquire()
            kb2, wb2 = RB[5].acquire()
            y = dict(sb=sb, wa=wa, wb=wb2,
                     w3s=[slot_of[("W3", q)] for q in range(4)])
            t_ya = W.pub("pe")
            t_yb = W.pub("pe")
            RB[4].set_prod(ka, t_ya)
            RB[5].set_prod(kb2, t_yb)
            y["t_ya"], y["t_yb"] = t_ya, t_yb
            if sb == NSB - 1:
                for q in range(4):
                    RW.add_cons(slot_of[("W3", q)][1], t_yb)
                RH.add_cons(kh, t_yb)
                RXT.add_cons(kxt, t_yb)
            kys, wys = RYS.acquire()
            y["wys"] = wys
            t_e1 = W.pub("dv")
            t_e2 = W.pub("dv")
            RB[4].add_cons(ka, t_e1)
            RB[5].add_cons(kb2, t_e2)
            RYS.set_prod(kys, t_e2)
            y["t_e1"], y["t_e2"] = t_e1, t_e2
            t_st = W.pub(f"d_ya{sb % 2}", 16)
            RYS.add_cons(kys, t_st)
            y["t_st"] = t_st
            eev["ys"].append(y)
        EEV.append(eev)
    toks["t_ya"] = [EEV[-1]["ys"][-2]["t_st"], EEV[-1]["ys"][-1]["t_st"]]

    ew, pub, pub16 = mkemit(S, "po")

    @block.gpsimd
    def _(g):
        for eev in EEV:
            e = eev["e"]
            ww = [toks["t_rep2"]] + eev["wxg"]
            if e == 0:
                ww += pre_xns
            ew(g, ww)
            pub16(g.dma_gather(
                xgb[eev["kxg"] % 2][:, 0:6 * D]
                .rearrange("p (b d) -> p b d", d=D),
                xns_flat, it16[:, e * 96:e * 96 + 48], 768, 768, D),
                eev["t_xga"])
            pub16(g.dma_gather(
                xgb[eev["kxg"] % 2][:, 6 * D:12 * D]
                .rearrange("p (b d) -> p b d", d=D),
                xns_flat, it16[:, e * 96 + 48:e * 96 + 96], 768, 768, D),
                eev["t_xg"])
            for wl in eev["wl"]:
                ww = list(wl["w"])
                if e == 0 and wl["kw"] < NSLOT:
                    ww += t_wc
                ew(g, ww)
                slot, q = wl["slot"], wl["q"]
                if wl["wn"] == "W1":
                    src = W1b[e, q // 2].rearrange("p (dc h) -> p dc h", h=1024)[
                        :, :, (q % 2) * 512:(q % 2 + 1) * 512]
                    dst = slots[slot][:].rearrange("p (dc h) -> p dc h", h=512)
                elif wl["wn"] == "W2":
                    src = W2b[e, q // 2].rearrange("p (dc h) -> p dc h", h=1024)[
                        :, :, (q % 2) * 512:(q % 2 + 1) * 512]
                    dst = slots[slot][:].rearrange("p (dc h) -> p dc h", h=512)
                else:
                    src = W3b[e, q * 4:(q + 1) * 4].rearrange("hc p d -> p hc d")
                    dst = slots[slot][:].rearrange("p (hc d) -> p hc d", d=D)
                pub16(g.dma_start(dst, src), wl["t"])

    ew, pub, pub16 = mkemit(S, None)

    @block.sync
    def _(sync):
        for eev in EEV:
            e = eev["e"]
            for y in eev["ys"]:
                ew(sync, [y["t_e2"]])
                pub16(sync.dma_start(
                    YALL[e * CCAP + y["sb"] * 128:
                         e * CCAP + (y["sb"] + 1) * 128, :],
                    ystg[:]), y["t_st"])

    ew, pub, _ = mkemit(S, "pe")

    @block.tensor
    def _(te):
        for eev in EEV:
            for tp in eev["tps"]:
                blk = tp["blk"]
                ww = list(tp["wtb"])
                if blk == 0:
                    ww += [eev["t_xg"], t_ident] + eev["wxt"]
                ew(te, ww)
                for dcc in range(8):
                    ii = te.transpose(
                        pbt[blk % 2][:, dcc * 128:(dcc + 1) * 128],
                        xgb[eev["kxg"] % 2][:, blk * 1024 + dcc * 128:
                                            blk * 1024 + (dcc + 1) * 128],
                        ident_s[:])
                pub(ii, tp["t_tp"])
            for uv in eev["uv"]:
                stile, hq = uv["stile"], uv["hq"]
                ww = list(uv["wU"])
                if uv["cnt"] == 0:
                    ww += [eev["tps"][-1]["t_ev"]]
                if stile == 0:
                    ww += [uv["w1"][2], uv["w2"][2]]
                ew(te, ww)
                s1 = uv["w1"][0]
                for dcc in range(8):
                    ii = te.matmul(
                        pf[uv["bu"]][:, 0:512],
                        slots[s1][:, dcc * 512 + hq * 128:
                                  dcc * 512 + (hq + 1) * 128],
                        xgT[:, dcc * CCAP + stile * 512:
                            dcc * CCAP + (stile + 1) * 512],
                        start=(dcc == 0), stop=(dcc == 7))
                pub(ii, uv["t_u"])
                ew(te, uv["wV"])
                s2 = uv["w2"][0]
                for dcc in range(8):
                    ii = te.matmul(
                        pf[uv["bv"]][:, 0:512],
                        slots[s2][:, dcc * 512 + hq * 128:
                                  dcc * 512 + (hq + 1) * 128],
                        xgT[:, dcc * CCAP + stile * 512:
                            dcc * CCAP + (stile + 1) * 512],
                        start=(dcc == 0), stop=(dcc == 7))
                pub(ii, uv["t_v"])
            for y in eev["ys"]:
                sb = y["sb"]
                ww = list(y["wa"]) + list(y["wb"])
                if sb == 0:
                    ww += [eev["uv"][-1]["t_h"]] + [w3[2] for w3 in y["w3s"]]
                ew(te, ww)
                for hc in range(16):
                    q, hq = hc // 4, hc % 4
                    w3s = y["w3s"][q][0]
                    lhs = hTr[:, hc * CCAP + sb * 128:
                              hc * CCAP + (sb + 1) * 128]
                    i1 = te.matmul(pf[4][:, 0:512], lhs,
                                   slots[w3s][:, hq * 1024:hq * 1024 + 512],
                                   start=(hc == 0), stop=(hc == 15))
                    i2 = te.matmul(pf[5][:, 0:512], lhs,
                                   slots[w3s][:, hq * 1024 + 512:
                                              (hq + 1) * 1024],
                                   start=(hc == 0), stop=(hc == 15))
                pub(i1, y["t_ya"])
                pub(i2, y["t_yb"])

    ew, pub, _ = mkemit(S, "ac")

    @block.scalar
    def _(sc):
        for eev in EEV:
            for tp in eev["tps"]:
                blk = tp["blk"]
                ew(sc, [tp["t_tp"]])
                xv = xgT[:].rearrange("p (dc s) -> p dc s", s=CCAP)
                pub(sc.copy(xv[:, :, blk * 128:(blk + 1) * 128],
                            pbt[blk % 2][:].rearrange("p (dc s) -> p dc s",
                                                      s=128)), tp["t_ev"])
            for uv in eev["uv"]:
                ew(sc, [uv["t_u"]] + uv["wsg"])
                pub(sc.activation(sg[:], pf[uv["bu"]][:, 0:512], AF.Silu),
                    uv["t_s"])

    ew, pub, _ = mkemit(S, "dv")

    @block.vector
    def _(vec):
        for eev in EEV:
            for uv in eev["uv"]:
                hc, stile = uv["hc"], uv["stile"]
                ww = [uv["t_s"], uv["t_v"]]
                if uv["cnt"] == 0:
                    ww += eev["wh"]
                ew(vec, ww)
                pub(vec.tensor_mul(
                    hTr[:, hc * CCAP + stile * 512:
                        hc * CCAP + stile * 512 + 512],
                    sg[:], pf[uv["bv"]][:, 0:512]), uv["t_h"])
            for y in eev["ys"]:
                ew(vec, [y["t_ya"]] + y["wys"])
                pub(vec.tensor_copy(ystg[:, 0:512], pf[4][:, 0:512]),
                    y["t_e1"])
                ew(vec, [y["t_yb"]])
                pub(vec.tensor_copy(ystg[:, 512:1024], pf[5][:, 0:512]),
                    y["t_e2"])


def _c2_combine(nc, block, S, W, ctx, ATTs, xq, OUT, si16, wp1, wp2,
                YALL, toks, final_waits, prev):
    sbt = lambda n, shp, dt: ctx.enter_context(nc.sbuf_tensor(n, shp, dt))
    cmbx = [sbt(f"m_x{i}", [128, 4096], F32) for i in range(2)]
    cmba = [sbt(f"m_a{i}", [128, 4096], F32) for i in range(2)]
    yg1 = [sbt(f"m_y1{i}", [128, 4096], F32) for i in range(2)]
    yg2 = [sbt(f"m_y2{i}", [128, 4096], F32) for i in range(2)]
    base = sbt("m_b", [128, 4096], F32)
    ostg = [sbt(f"m_o{i}", [128, 4096], F32) for i in range(2)]

    RCM = RBuf(2)
    RYG = RBuf(2)
    ROS = RBuf(2)
    bat_pre = [(nm, S.c[nm]) for nm in ("d_bat0", "d_bat1") if nm in S.c]

    MEV = []
    for ch in range(NCHUNK):
        kc, wc = RCM.acquire()
        m = dict(ch=ch, kc=kc, wc=wc)
        W.pub(f"d_cm{kc % 2}", 16)
        m["t_c"] = W.pub(f"d_cm{kc % 2}", 16)
        RCM.set_prod(kc, m["t_c"])
        kg, wg_ = RYG.acquire()
        m["kg"], m["wg"] = kg, wg_
        m["t_g1"] = W.pub(f"d_g1{kg % 2}", 16)
        m["t_g2"] = W.pub(f"d_g2{kg % 2}", 16)
        RYG.set_prod(kg, m["t_g2"])
        ko_, wo = ROS.acquire()
        m["ko"], m["wo"] = ko_, wo
        m["t_bs"] = W.pub("dv")
        m["t_os"] = [tuple(W.pub("dv") for _ in range(2)) for _ in range(4)]
        RCM.add_cons(kc, m["t_bs"])
        t_o = m["t_os"][3][1]
        RYG.add_cons(kg, t_o)
        ROS.set_prod(ko_, t_o)
        m["t_st"] = W.pub(f"d_co{ko_ % 2}", 16)
        ROS.add_cons(ko_, m["t_st"])
        MEV.append(m)

    ew, pub, pub16 = mkemit(S, None)

    @block.sync
    def _(sync):
        ew(sync, bat_pre)
        for m in MEV:
            ch, kc = m["ch"], m["kc"]
            ew(sync, m["wc"])
            S.inc(sync.dma_start(cmbx[kc % 2][:], xq[ch]), f"d_cm{kc % 2}", 16)
            pub16(sync.dma_start(
                cmba[kc % 2][:].rearrange("p (sb d) -> p sb d", d=1024),
                ATTs[ch * 512:(ch + 1) * 512, :]
                .rearrange("(sb p) d -> p sb d", p=128)), m["t_c"])

    ew, pub, pub16 = mkemit(S, "po")

    @block.gpsimd
    def _(g):
        for m in MEV:
            ch, kg = m["ch"], m["kg"]
            ww = list(m["wg"]) + [toks["t_rep1"]]
            if ch == 0:
                ww += toks["t_ya"] + [toks["t_itz"]]
            ew(g, ww)
            pub16(g.dma_gather(
                yg1[kg % 2][:].rearrange("p (b d) -> p b d", d=D),
                YALL, si16[:, ch * 32:(ch + 1) * 32], 512, 512, D), m["t_g1"])
            pub16(g.dma_gather(
                yg2[kg % 2][:].rearrange("p (b d) -> p b d", d=D),
                YALL, si16[:, 256 + ch * 32:256 + (ch + 1) * 32], 512, 512, D),
                m["t_g2"])
            if ch > 0:
                mp = MEV[ch - 1]
                ew(g, [mp["t_os"][3][1]])
                pub16(g.dma_start(OUT[mp["ch"]], ostg[mp["ko"] % 2][:]),
                      mp["t_st"])
        mp = MEV[-1]
        ew(g, [mp["t_os"][3][1]])
        pub16(g.dma_start(OUT[mp["ch"]], ostg[mp["ko"] % 2][:]), mp["t_st"])
    final_waits.append(("d_co0", S.c["d_co0"]))
    final_waits.append(("d_co1", S.c["d_co1"]))

    ew, pub, _ = mkemit(S, "dv")

    @block.vector
    def _(vec):
        for m in MEV:
            ch, kc, kg, ko_ = m["ch"], m["kc"], m["kg"], m["ko"]
            ew(vec, [m["t_c"]] + m["wo"])
            pub(vec.tensor_add(base[:], cmbx[kc % 2][:], cmba[kc % 2][:]),
                m["t_bs"])
            ew(vec, [m["t_g1"], m["t_g2"]])
            for b in range(4):
                t1, t2 = m["t_os"][b]
                w1c = wp1[:, ch * 4 + b:ch * 4 + b + 1]
                w2c = wp2[:, ch * 4 + b:ch * 4 + b + 1]
                pub(vec.scalar_tensor_tensor(
                    ostg[ko_ % 2][:, b * 1024:(b + 1) * 1024],
                    yg1[kg % 2][:, b * 1024:(b + 1) * 1024], w1c,
                    base[:, b * 1024:(b + 1) * 1024], ALU.mult, ALU.add), t1)
                pub(vec.scalar_tensor_tensor(
                    ostg[ko_ % 2][:, b * 1024:(b + 1) * 1024],
                    yg2[kg % 2][:, b * 1024:(b + 1) * 1024], w2c,
                    ostg[ko_ % 2][:, b * 1024:(b + 1) * 1024],
                    ALU.mult, ALU.add), t2)


def kernel(L0_attn_x, L0_attn_wq, L0_attn_wk, L0_attn_wv,
           L0_ffn_Wg, L0_ffn_W1, L0_ffn_W2, L0_ffn_W3):
    per_core, row_idx = host_layouts(
        L0_attn_x, L0_attn_wq, L0_attn_wk, L0_attn_wv,
        L0_ffn_Wg, L0_ffn_W1, L0_ffn_W2, L0_ffn_W3)
    nc = build_program()
    in_maps = [dict(per_core[c]) for c in range(NCORE)]
    res = run_bass_kernel_spmd(nc, in_maps, core_ids=list(range(NCORE)))
    out = np.empty((BATCH, SEQ, D), dtype=np.float32)
    for c in range(NCORE):
        oc = np.asarray(res.results[c]["out"])
        oc = oc.reshape(NCHUNK, 128, 4, D).transpose(0, 2, 1, 3).reshape(QROWS, D)
        out[c // 2][row_idx[c]] = oc
    return out
